# version 10
# baseline (speedup 1.0000x reference)
import numpy as np
import ml_dtypes
from contextlib import ExitStack

import concourse.bass as bass
import concourse.mybir as mybir
from concourse.bass_utils import run_bass_kernel_spmd

F32 = mybir.dt.float32
BF16 = mybir.dt.bfloat16
I32 = mybir.dt.int32
ALU = mybir.AluOpType
ACT = mybir.ActivationFunctionType
AXX = mybir.AxisListType.X

S, D, NH, NKV, HD = 8192, 2048, 16, 4, 128
HQKV, HID, NC = 3072, 5632, 8
T = S // NC
NT = T // 128
NDC = D // 128
NHC = HID // 128
EPS = 1e-5
NEG_BIG = -30000.0
SCORE_SCALE = float(1.0 / np.float32(np.sqrt(np.float64(128.0))))
THETA = 500000.0

ENGS = ("sync", "vector", "scalar", "tensor", "gpsimd")


def _cw2pi():
    two_pi = 2.0 * np.pi
    c1 = float(np.float32(np.round(two_pi * 512) / 512.0))
    r = two_pi - c1
    c2 = float(np.float32(np.round(r * 2.0**32) / 2.0**32))
    c3 = float(np.float32(two_pi - c1 - c2))
    return c1, c2, c3


def _halfpi2():
    hp = 0.5 * np.pi
    a = float(np.float32(hp))
    b = float(np.float32(hp - a))
    return a, b


class Sem:
    def __init__(self, es, nc, name):
        self.h = es.enter_context(nc.semaphore(name))
        self.v = 0


class Phase:

    def __init__(self, nc):
        self.nc = nc
        self.ops = {k: [] for k in ENGS}

    def em(self, eng, fn):
        self.ops[eng].append(fn)

    def run(self):
        with self.nc.Block() as block:
            for eng in ENGS:
                fns = self.ops[eng]
                if not fns:
                    continue
                getattr(block, eng)(lambda e, fns=fns: [f(e) for f in fns])


class Build:

    def __init__(self, cfg):
        self.cfg = cfg
        self.nc = bass.Bass("TRN2", target_bir_lowering=False, detect_race_conditions=False)
        self.es = ExitStack()
        self._declare_io()
        self._declare_sems()
        self._declare_persistent()

    def _declare_io(self):
        nc = self.nc
        self.x_in = nc.dram_tensor("x", [T, D], F32, kind="ExternalInput")
        self.wqkvT = nc.dram_tensor("wqkvT", [D, HQKV], F32, kind="ExternalInput")
        self.woT = nc.dram_tensor("woT", [D, D], F32, kind="ExternalInput")
        self.nwrep = nc.dram_tensor("nwrep", [128, D], F32, kind="ExternalInput")
        self.wgT = nc.dram_tensor("wgT", [D, HID], I32, kind="ExternalInput")
        self.wuT = nc.dram_tensor("wuT", [D, HID], I32, kind="ExternalInput")
        self.wdT = nc.dram_tensor("wdT", [HID, D], I32, kind="ExternalInput")
        self.sg_in = nc.dram_tensor("sg", [128, NHC], F32, kind="ExternalInput")
        self.su_in = nc.dram_tensor("su", [128, NHC], F32, kind="ExternalInput")
        self.sd_in = nc.dram_tensor("sd", [128, NDC], F32, kind="ExternalInput")
        self.pos_in = nc.dram_tensor("pos", [1, T], F32, kind="ExternalInput")
        self.mask_in = nc.dram_tensor("mask", [128, 1024], F32, kind="ExternalInput")
        self.xfr_in = nc.dram_tensor("xfr", [128, 1], F32, kind="ExternalInput")
        self.sgn_in = nc.dram_tensor("sgn", [128, 1], F32, kind="ExternalInput")
        self.idf_in = nc.dram_tensor("idf", [128, 128], F32, kind="ExternalInput")
        self.onesf_in = nc.dram_tensor("onesf", [128, 128], F32, kind="ExternalInput")
        self.onec_in = nc.dram_tensor("onec", [128, 1], F32, kind="ExternalInput")
        self.y_out = nc.dram_tensor("y", [T, D], F32, kind="ExternalOutput")
        self.ag_in = nc.dram_tensor("ag_in", [2048, T], BF16)
        self.ag_out = nc.dram_tensor("ag_out", [NC * 2048, T], BF16)
        self.xnhi_d = nc.dram_tensor("xnhi_d", [D, T], BF16)
        self.xnlo_d = nc.dram_tensor("xnlo_d", [D, T], BF16)
        self.qhi_d = nc.dram_tensor("qhi_d", [D, T], BF16)
        self.qlo_d = nc.dram_tensor("qlo_d", [D, T], BF16)
        self.row_d = nc.dram_tensor("row_d", [1, 2048], F32)
        self.dbg = {}
        for name, shape in self.cfg.get("debug_outputs", {}).items():
            self.dbg[name] = self.nc.dram_tensor(name, list(shape), F32, kind="ExternalOutput")

    def _declare_sems(self):
        es, nc = self.es, self.nc
        self.s_ld = Sem(es, nc, "s_ld")
        self.s_ld2 = Sem(es, nc, "s_ld2")
        self.s_ld3 = Sem(es, nc, "s_ld3")
        self.s_v = Sem(es, nc, "s_v")
        self.s_a = Sem(es, nc, "s_a")
        self.s_t = Sem(es, nc, "s_t")
        self.s_g = Sem(es, nc, "s_g")
        self.s_st = Sem(es, nc, "s_st")
        self.all_sems = [self.s_ld, self.s_ld2, self.s_ld3, self.s_v, self.s_a,
                         self.s_t, self.s_g, self.s_st]

    def sbuf(self, name, shape, dtype, stack=None, side=None):
        kw = {} if side is None else {"side": side}
        return (stack or self.es).enter_context(self.nc.sbuf_tensor(name, shape, dtype, **kw))

    def psum(self, name, shape, dtype, stack):
        return stack.enter_context(self.nc.psum_tensor(name, shape, dtype))

    def _declare_persistent(self):
        sb = self.sbuf
        self.dq_cols = sb("dq_cols", [128, NT], F32)
        self.ds2_cols = sb("ds2_cols", [128, NT], F32)
        self.masksb = sb("masksb", [128, 1024], F32)
        self.nwsb = sb("nwsb", [128, D], F32)
        self.sgsb = sb("sgsb", [128, NHC], F32)
        self.susb = sb("susb", [128, NHC], F32)
        self.sdsb = sb("sdsb", [128, NDC], F32)
        self.xfrsb = sb("xfrsb", [128, 1], F32)
        self.sgnsb = sb("sgnsb", [128, 1], F32)
        self.idf = sb("idf_s", [128, 128], F32)
        self.onescol = sb("onescol", [128, 1], F32)
        self.onesrow1 = sb("onesrow1", [1, 128], F32)
        self.ta = sb("ta", [128, D], F32)
        self.tb = sb("tb", [128, D], F32)
        self.tc_ = sb("tc_", [128, D], F32)
        self.ti = sb("ti", [128, D], I32)
        self.td = sb("td", [128, D], F32)
        self.te = sb("te", [128, D], F32)
        self.st24 = sb("st24", [128, 24], F32)
        self.st8 = sb("st8", [128, 8], F32)
        self.c1 = sb("c1", [128, 1], F32)
        self.c2 = sb("c2", [128, 1], F32)
        self.c3 = sb("c3", [128, 1], F32)
        self.c4 = sb("c4", [128, 1], F32)
        self.stage = sb("stage", [128, D], BF16)
        self.stage2 = sb("stage2", [128, D], BF16)
        self.evrow = sb("evrow", [16, 128], F32)
        self.rowa = sb("rowa", [1, 128], F32)
        self.rowb = sb("rowb", [1, 128], F32)
        self.rowc = sb("rowc", [1, 128], F32)
        self.epsc = sb("epsc", [128, 1], F32)
        self.j0 = sb("j0", [128, 1], F32)
        self.j1 = sb("j1", [128, 1], F32)
        Build._sp_j0 = self.j0[:]
        Build._sp_j1 = self.j1[:]

    def declare_pA(self, stack):
        sb = self.sbuf
        self.rT = sb("rT", [128, NDC * T], BF16, stack=stack, side="right")
        self.dqb = sb("dqb", [128, T], F32, stack=stack, side="right")
        self.possb = sb("possb", [128, T], F32, stack=stack, side="right")
        self.ctabQ = sb("ctabQ", [128, T], F32, stack=stack, side="right")
        self.stabQ = sb("stabQ", [128, T], F32, stack=stack, side="right")
        self.ctabK = sb("ctabK", [128, T], F32, stack=stack, side="right")
        self.stabK = sb("stabK", [128, T], F32, stack=stack, side="right")

    @staticmethod
    def sp(e):
        e.tensor_copy(Build._sp_j1, Build._sp_j0)

    def dump_now(self, ph, names):
        for nm in names:
            t = getattr(self, nm)
            shp = list(t.ap().shape)
            d = self.nc.dram_tensor("dbg_" + nm, shp, t.ap().dtype, kind="ExternalOutput")
            self.s_st.v += 16
            def f(e, t=t, d=d):
                e.dma_start(out=d.ap(), in_=t.ap()).then_inc(self.s_st.h, 16)
            ph.em("sync", f)

    def guard_all(self, ph):
        vals = [(s.h, s.v) for s in self.all_sems if s.v > 0]
        for eng in ENGS:
            def f(e, vals=vals):
                for h, v in vals:
                    e.wait_ge(h, v)
            ph.em(eng, f)

    def p0(self, ph):
        b = self
        s_ld = b.s_ld
        loads = [
            (b.masksb[:], b.mask_in[:, :]),
            (b.nwsb[:], b.nwrep[:, :]),
            (b.sgsb[:], b.sg_in[:, :]),
            (b.susb[:], b.su_in[:, :]),
            (b.sdsb[:], b.sd_in[:, :]),
            (b.xfrsb[:], b.xfr_in[:, :]),
            (b.sgnsb[:], b.sgn_in[:, :]),
            (b.idf[:], b.idf_in[:, :]),
            (b.onescol[:], b.onec_in[:, :]),
            (b.onesrow1[:], b.onesf_in[0:1, 0:128]),
        ]
        def f(e):
            for out, in_ in loads:
                e.dma_start(out=out, in_=in_).then_inc(s_ld.h, 16)
        ph.em("sync", f)
        s_ld.v += 16 * len(loads)
        b.s_v.v += 1
        ph.em("vector", lambda e: (e.memset(b.epsc[:], EPS), e.memset(b.j0[:], 0.0),
                                   e.memset(b.j1[:], 0.0).then_inc(b.s_v.h, 1)))

    def p1(self, ph, pst):
        b = self
        nc = self.nc
        rT = b.rT
        b.xs = b.sbuf("xs", [128, D], F32, stack=pst)
        trp = [b.psum(f"p1trp{i}", [128, 512], F32, pst) for i in range(2)]
        s_ld, s_ld2, s_v, s_a, s_t, s_st = b.s_ld, b.s_ld2, b.s_v, b.s_a, b.s_t, b.s_st
        skip = self.cfg.get("p1_skip", set())
        n_tiles = self.cfg.get("p1_tiles", NT)

        for it in range(n_tiles):
            prev_v, prev_a, prev_st, prev_t = s_v.v, s_a.v, s_st.v, s_t.v
            s_ld.v += 16
            ldv = s_ld.v
            ph.em("sync", lambda e, it=it, prev_v=prev_v, prev_a=prev_a, prev_st=prev_st, ldv=ldv: (
                e.wait_ge(s_v.h, prev_v),
                e.wait_ge(s_a.h, prev_a),
                e.wait_ge(s_st.h, prev_st),
                e.dma_start(out=b.xs[:], in_=b.x_in[128 * it:128 * (it + 1), :]).then_inc(s_ld.h, 16),
            ))
            s_st.v += 16
            ph.em("scalar", lambda e, it=it, ldv=ldv: (
                e.wait_ge(s_ld.h, ldv),
                e.dma_start(out=b.y_out[128 * it:128 * (it + 1), :], in_=b.xs[:]).then_inc(s_st.h, 16),
            ))
            s_v.v += 1
            v_stats = s_v.v
            def f_stats(e, ldv=ldv):
                e.wait_ge(s_ld.h, ldv)
                e.tensor_reduce(b.c1[:], b.xs[:], axis=AXX, op=ALU.add)
                e.tensor_mul(b.te[:], b.xs[:], b.xs[:])
                e.tensor_reduce(b.c3[:], b.te[:], axis=AXX, op=ALU.add)
                Build.sp(e)
                e.tensor_scalar_mul(b.c1[:], b.c1[:], 1.0 / D)
                Build.sp(e)
                e.tensor_scalar_mul(b.c3[:], b.c3[:], 1.0 / D)
                e.tensor_mul(b.c2[:], b.c1[:], b.c1[:])
                Build.sp(e)
                e.tensor_sub(b.c3[:], b.c3[:], b.c2[:]).then_inc(s_v.h, 1)
            ph.em("vector", f_stats)
            s_a.v += 1
            a_rsq = s_a.v
            def f_rsq(e, v_stats=v_stats):
                e.wait_ge(s_v.h, v_stats)
                e.activation(out=b.c4[:], in_=b.c3[:], func=ACT.Sqrt, bias=b.epsc[:, 0:1], scale=1.0).then_inc(s_a.h, 1)
            ph.em("scalar", f_rsq)
            s_v.v += 1
            v_nmr = s_v.v
            def f_nmr(e, a_rsq=a_rsq):
                e.wait_ge(s_a.h, a_rsq)
                e.reciprocal(b.c4[:], b.c4[:]).then_inc(s_v.h, 1)
            ph.em("vector", f_nmr)
            s_v.v += 1
            v_xn = s_v.v
            def f_xn(e, prev_t=prev_t):
                e.wait_ge(s_t.h, prev_t)
                e.tensor_scalar(b.te[:], b.xs[:], b.c1[:, 0:1], None, op0=ALU.subtract)
                e.tensor_scalar(b.ta[:], b.te[:], b.c4[:, 0:1], None, op0=ALU.mult).then_inc(s_v.h, 1)
            ph.em("vector", f_xn)
            a_xn = v_xn
            s_v.v += 1
            v_qs = s_v.v
            def f_qs(e, it=it):
                e.reduce_max(b.c1[:], b.ta[:], axis=AXX, apply_absolute_value=True)
                Build.sp(e)
                e.tensor_scalar_max(b.c1[:], b.c1[:], 1e-5)
                Build.sp(e)
                e.reciprocal(b.c2[:], b.c1[:])
                Build.sp(e)
                e.tensor_scalar_mul(b.c2[:], b.c2[:], 127.0)
                Build.sp(e)
                e.reciprocal(b.c3[:], b.c2[:])
                Build.sp(e)
                e.tensor_copy(b.dq_cols[:, it:it + 1], b.c3[:]).then_inc(s_v.h, 1)
            ph.em("vector", f_qs)
            s_v.v += 1
            def f_yq(e):
                e.tensor_scalar(b.tb[:], b.ta[:], b.c2[:, 0:1], None, op0=ALU.mult).then_inc(s_v.h, 1)
            ph.em("vector", f_yq)
            s_v.v += 1
            v_rr = s_v.v
            def f_rr(e, prev_t=prev_t):
                e.wait_ge(s_t.h, prev_t)
                e.tensor_copy(b.ti[:], b.tb[:])
                e.tensor_copy(b.tc_[:], b.ti[:])
                e.tensor_sub(b.td[:], b.tb[:], b.tc_[:])
                e.tensor_single_scalar(b.te[:], b.td[:], 0.5, op=ALU.is_ge)
                e.tensor_add(b.tc_[:], b.tc_[:], b.te[:])
                e.tensor_single_scalar(b.te[:], b.td[:], -0.5, op=ALU.is_le)
                e.tensor_sub(b.td[:], b.tc_[:], b.te[:]).then_inc(s_v.h, 1)
            ph.em("vector", f_rr)
            for src_i in ([] if "tr" in skip else [i_ for i_ in range(2) if f"tr{i_}" not in skip]):
                for k in range(4):
                    bank = trp[(it * 8 + src_i * 4 + k) % 2]
                    src_sem, src_val = (s_v, v_rr) if src_i == 0 else (s_v, a_xn)
                    s_t.v += 1
                    t_tr = s_t.v
                    need_a = s_a.v
                    need_v = s_v.v
                    def f_tr(e, k=k, bank=bank, src_i=src_i, src_sem=src_sem,
                             src_val=src_val, need_a=need_a, need_v=need_v):
                        e.wait_ge(src_sem.h, src_val)
                        e.wait_ge(s_a.h, need_a)
                        e.wait_ge(s_v.h, need_v)
                        src = b.td if src_i == 0 else b.ta
                        for j in range(4):
                            dc = 4 * k + j
                            li = e.transpose(bank[:, 128 * j:128 * (j + 1)],
                                             src[:, 128 * dc:128 * (dc + 1)], b.idf[:])
                        li.then_inc(s_t.h, 1)
                    ph.em("tensor", f_tr)
                    if src_i == 0:
                        s_a.v += 1
                        def f_ev(e, k=k, bank=bank, it=it, t_tr=t_tr):
                            e.wait_ge(s_t.h, t_tr)
                            out = bass.AP(rT, (4 * k) * T + 128 * it,
                                          [[NDC * T, 128], [T, 4], [1, 128]])
                            e.activation(out=out, in_=bank[:], func=ACT.Copy,
                                         bias=0.0, scale=1.0).then_inc(s_a.h, 1)
                        ph.em("scalar", f_ev)
                    else:
                        s_a.v += 1
                        a_hi = s_a.v
                        def f_hi(e, k=k, bank=bank, t_tr=t_tr, prev_st=prev_st):
                            e.wait_ge(s_t.h, t_tr)
                            e.wait_ge(s_st.h, prev_st)
                            e.activation(out=b.stage[:, 512 * k:512 * (k + 1)], in_=bank[:],
                                         func=ACT.Copy, bias=0.0, scale=1.0).then_inc(s_a.h, 1)
                        ph.em("scalar", f_hi)
                        s_v.v += 1
                        def f_lo(e, k=k, bank=bank, a_hi=a_hi):
                            e.wait_ge(s_a.h, a_hi)
                            e.tensor_sub(b.stage2[:, 512 * k:512 * (k + 1)], bank[:],
                                         b.stage[:, 512 * k:512 * (k + 1)]).then_inc(s_v.h, 1)
                        ph.em("vector", f_lo)
            if "spill" in skip or "tr" in skip or "tr1" in skip:
                continue
            s_st.v += 32
            need_v = s_v.v
            need_a = s_a.v
            def f_spill(e, it=it, need_v=need_v, need_a=need_a):
                e.wait_ge(s_v.h, need_v)
                e.wait_ge(s_a.h, need_a)
                for src, dst in ((b.stage, b.xnhi_d), (b.stage2, b.xnlo_d)):
                    out = bass.AP(dst, 128 * it, [[T, 128], [128 * T, 16], [1, 128]])
                    in_ = bass.AP(src, 0, [[D, 128], [128, 16], [1, 128]])
                    e.dma_start(out=out, in_=in_).then_inc(s_st.h, 16)
            ph.em("sync", f_spill)

    def p15(self, ph, pst):
        b = self
        s_v, s_a, s_t, s_ld2, s_ld = b.s_v, b.s_a, b.s_t, b.s_ld2, b.s_ld
        ptr = b.psum(f"p15tr", [128, 128], F32, pst)
        s_ld.v += 16
        ph.em("sync", lambda e: e.dma_start(
            out=b.possb[:], in_=bass.AP(b.pos_in, 0, [[0, 128], [1, T]])).then_inc(s_ld.h, 16))

        s_t.v += 1
        t_tr = s_t.v
        need_v = s_v.v
        ph.em("tensor", lambda e, t=t_tr, nv=need_v: (
            e.wait_ge(s_v.h, nv),
            e.transpose(ptr[0:NT, :], b.dq_cols[:], b.idf[:]).then_inc(s_t.h, 1),
        ))
        s_a.v += 1
        a_ev = s_a.v
        ph.em("scalar", lambda e, t=t_tr: (
            e.wait_ge(s_t.h, t),
            e.activation(out=b.evrow[0:NT, :], in_=ptr[0:NT, :], func=ACT.Copy,
                         bias=0.0, scale=1.0).then_inc(s_a.h, 1),
        ))
        s_ld2.v += 32
        ld2 = s_ld2.v
        ph.em("scalar", lambda e, a=a_ev, ld2=ld2: (
            e.wait_ge(s_a.h, a),
            e.dma_start(out=bass.AP(b.row_d, 0, [[128, NT], [1, 128]]),
                        in_=bass.AP(b.evrow, 0, [[128, NT], [1, 128]])).then_inc(s_ld2.h, 16),
            e.wait_ge(s_ld2.h, ld2 - 16),
            e.dma_start(out=b.dqb[:], in_=bass.AP(b.row_d, 0, [[0, 128], [1, T]])).then_inc(s_ld2.h, 16),
        ))

        c1, c2, c3 = _cw2pi()
        hp1, hp2 = _halfpi2()
        inv2pi = float(np.float32(1.0 / (2.0 * np.pi)))
        pi_clamp = 3.1415925
        prev_a0 = s_a.v
        s_v.v += 1
        v_ms = s_v.v
        def f_ms(e, prev_a0=prev_a0):
            e.wait_ge(s_a.h, prev_a0)
            e.memset(b.c1[:], THETA).then_inc(s_v.h, 1)
        ph.em("vector", f_ms)
        s_a.v += 1
        a_ln = s_a.v
        def f_ln(e, v_ms=v_ms):
            e.wait_ge(s_v.h, v_ms)
            e.activation(out=b.c2[:], in_=b.c1[:], func=ACT.Ln, bias=0.0, scale=1.0).then_inc(s_a.h, 1)
        ph.em("scalar", f_ln)
        s_v.v += 1
        v_fr = s_v.v
        def f_fr(e, a_ln=a_ln):
            e.wait_ge(s_a.h, a_ln)
            e.tensor_scalar(b.c3[:], b.xfrsb[:], b.c2[:, 0:1], None, op0=ALU.mult).then_inc(s_v.h, 1)
        ph.em("vector", f_fr)
        s_a.v += 1
        a_fr = s_a.v
        ph.em("scalar", lambda e, v=v_fr: (
            e.wait_ge(s_v.h, v),
            e.activation(out=b.c4[:], in_=b.c3[:], func=ACT.Exp, bias=0.0, scale=1.0).then_inc(s_a.h, 1),
        ))
        ld_now = b.s_ld.v
        for ipass in range(2):
            prev_a2 = s_a.v
            s_v.v += 1
            v_ang = s_v.v
            def f_ang(e, ipass=ipass, prev_a2=prev_a2, ld_now=ld_now):
                e.wait_ge(s_a.h, prev_a2)
                e.wait_ge(b.s_ld.h, ld_now)
                ap_t = b.possb[:, 0:T]
                e.tensor_scalar(b.ta[:, 0:T], ap_t, b.c4[:, 0:1], None, op0=ALU.mult)
                if ipass == 1:
                    e.tensor_scalar(b.ta[:, 0:T], b.ta[:, 0:T], hp1, hp2, op0=ALU.add, op1=ALU.add)
                e.tensor_scalar(b.tb[:, 0:T], b.ta[:, 0:T], inv2pi, None, op0=ALU.mult)
                e.tensor_copy(b.ti[:, 0:T], b.tb[:, 0:T])
                e.tensor_copy(b.tb[:, 0:T], b.ti[:, 0:T])
                e.scalar_tensor_tensor(b.tc_[:, 0:T], b.tb[:, 0:T], -c1, b.ta[:, 0:T],
                                       op0=ALU.mult, op1=ALU.add)
                e.scalar_tensor_tensor(b.tc_[:, 0:T], b.tb[:, 0:T], -c2, b.tc_[:, 0:T],
                                       op0=ALU.mult, op1=ALU.add)
                e.scalar_tensor_tensor(b.tc_[:, 0:T], b.tb[:, 0:T], -c3, b.tc_[:, 0:T],
                                       op0=ALU.mult, op1=ALU.add)
                e.tensor_single_scalar(b.td[:, 0:T], b.tc_[:, 0:T], float(np.float32(np.pi)), op=ALU.is_gt)
                e.tensor_single_scalar(b.te[:, 0:T], b.tc_[:, 0:T], float(np.float32(-np.pi)), op=ALU.is_lt)
                e.tensor_sub(b.td[:, 0:T], b.te[:, 0:T], b.td[:, 0:T])
                e.scalar_tensor_tensor(b.tc_[:, 0:T], b.td[:, 0:T], float(np.float32(2 * np.pi)),
                                       b.tc_[:, 0:T], op0=ALU.mult, op1=ALU.add)
                e.tensor_scalar_min(b.tc_[:, 0:T], b.tc_[:, 0:T], pi_clamp)
                e.tensor_scalar_max(b.tc_[:, 0:T], b.tc_[:, 0:T], -pi_clamp).then_inc(s_v.h, 1)
            ph.em("vector", f_ang)
            s_a.v += 1
            a_sin = s_a.v
            tgt = [b.stabK, b.ctabK][ipass]
            ph.em("scalar", lambda e, v=v_ang, tgt=tgt: (
                e.wait_ge(s_v.h, v),
                e.activation(out=tgt[:], in_=b.tc_[:, 0:T], func=ACT.Sin, bias=0.0, scale=1.0).then_inc(s_a.h, 1),
            ))
        s_v.v += 1
        def f_tab(e, a=s_a.v, ld2=s_ld2.v):
            e.wait_ge(s_a.h, a)
            e.wait_ge(s_ld2.h, ld2)
            e.tensor_scalar(b.stabK[:], b.stabK[:], b.sgnsb[:, 0:1], None, op0=ALU.mult)
            e.tensor_mul(b.stabK[:], b.stabK[:], b.dqb[:])
            e.tensor_mul(b.ctabK[:], b.ctabK[:], b.dqb[:])
            e.tensor_scalar_mul(b.stabQ[:], b.stabK[:], SCORE_SCALE)
            e.tensor_scalar_mul(b.ctabQ[:], b.ctabK[:], SCORE_SCALE).then_inc(s_v.h, 1)
        ph.em("vector", f_tab)

    def p2(self, ph, pst):
        b = self
        s_ld, s_ld2, s_v, s_a, s_t = b.s_ld, b.s_ld2, b.s_v, b.s_a, b.s_t
        ps_qkv = [b.psum(f"p2qkv{i}", [128, 1024], F32, pst) for i in range(2)]
        trpv = [b.psum(f"p2trv{i}", [128, 512], F32, pst) for i in range(2)]
        wtile = [b.sbuf(f"p2wt{i}", [128, D], F32, stack=pst, side="right") for i in range(2)]
        whi = [b.sbuf(f"p2whi{i}", [128, D], BF16, stack=pst, side="right") for i in range(2)]
        wlo = [b.sbuf(f"p2wlo{i}", [128, D], BF16, stack=pst, side="right") for i in range(2)]
        qf = b.sbuf("p2qf", [128, T], F32, stack=pst, side="right")
        qsw = b.sbuf("p2qsw", [128, T], F32, stack=pst, side="right")
        qr = b.sbuf("p2qr", [128, T], F32, stack=pst, side="right")
        kst_hi = b.sbuf("p2ksth", [128, T], BF16, stack=pst, side="right")
        kst_lo = b.sbuf("p2kstl", [128, T], BF16, stack=pst, side="right")
        vst_hi = b.sbuf("p2vsth", [128, T], BF16, stack=pst, side="right")
        vst_lo = b.sbuf("p2vstl", [128, T], BF16, stack=pst, side="right")
        swapmask = [i ^ 1 for i in range(32)]

        conv_v = {}
        mm_t = {}
        epi_a = {}
        epi_v = {}
        ld_v = {}

        for hc in range(24):
            buf = hc % 2
            s_ld.v += 16
            ld_v[hc] = s_ld.v
            wait_conv = conv_v.get(hc - 2, 0)
            def f_ld(e, hc=hc, buf=buf, wait_conv=wait_conv):
                e.wait_ge(s_v.h, wait_conv)
                in_ = bass.AP(b.wqkvT, 128 * hc, [[HQKV, 128], [128 * HQKV, NDC], [1, 128]])
                e.dma_start(out=wtile[buf][:], in_=in_).then_inc(s_ld.h, 16)
            ph.em("sync", f_ld)
            wait_mm = mm_t.get(hc - 2, 0)
            s_v.v += 2
            conv_v[hc] = s_v.v
            def f_cv(e, hc=hc, buf=buf, wait_mm=wait_mm, ldv=ld_v[hc]):
                e.wait_ge(s_ld.h, ldv)
                e.wait_ge(s_t.h, wait_mm)
                e.tensor_copy(whi[buf][:], wtile[buf][:]).then_inc(s_v.h, 1)
                e.tensor_sub(wlo[buf][:], wtile[buf][:], whi[buf][:]).then_inc(s_v.h, 1)
            ph.em("vector", f_cv)
            wait_ea = epi_a.get(hc - 2, 0)
            wait_ev = epi_v.get(hc - 2, 0)
            s_t.v += 1
            mm_t[hc] = s_t.v
            def f_mm(e, hc=hc, buf=buf, cv=conv_v[hc], wait_ea=wait_ea, wait_ev=wait_ev):
                e.wait_ge(s_v.h, cv)
                e.wait_ge(s_a.h, wait_ea)
                e.wait_ge(s_v.h, wait_ev)
                for dc in range(NDC):
                    for th in range(2):
                        for w in (whi[buf], wlo[buf]):
                            li = e.matmul(ps_qkv[buf][:, 512 * th:512 * (th + 1)],
                                          w[:, 128 * dc:128 * (dc + 1)],
                                          b.rT[:, dc * T + 512 * th: dc * T + 512 * (th + 1)],
                                          start=(dc == 0 and w is whi[buf]),
                                          stop=(dc == NDC - 1 and w is wlo[buf]))
                li.then_inc(s_t.h, 1)
            ph.em("tensor", f_mm)
            if hc < 16:
                s_a.v += 1
                a_ev = s_a.v
                def f_ev(e, buf=buf, mmv=mm_t[hc], pv=epi_v.get(hc - 1, 0)):
                    e.wait_ge(s_t.h, mmv)
                    e.wait_ge(s_v.h, pv)
                    e.activation(out=qf[:], in_=ps_qkv[buf][:], func=ACT.Copy,
                                 bias=0.0, scale=1.0).then_inc(s_a.h, 1)
                ph.em("scalar", f_ev)
                s_v.v += 4
                v_rope = s_v.v
                def f_rope(e, a_ev=a_ev):
                    e.wait_ge(s_a.h, a_ev)
                    e.stream_shuffle(qsw[:], qf[:], swapmask)
                    e.tensor_mul(qr[:], qf[:], b.ctabQ[:])
                    e.tensor_mul(qsw[:], qsw[:], b.stabQ[:])
                    e.tensor_add(qr[:], qr[:], qsw[:]).then_inc(s_v.h, 4)
                ph.em("vector", f_rope)
                s_a.v += 1
                a_hi = s_a.v
                def f_hi(e, v_rope=v_rope, pstore=b.s_ld2.v):
                    e.wait_ge(s_v.h, v_rope)
                    e.wait_ge(s_ld2.h, pstore)
                    e.activation(out=b.stage[:, 0:T], in_=qr[:],
                                 func=ACT.Copy, bias=0.0, scale=1.0).then_inc(s_a.h, 1)
                ph.em("scalar", f_hi)
                s_v.v += 1
                v_lo = s_v.v
                def f_lo(e, a_hi=a_hi):
                    e.wait_ge(s_a.h, a_hi)
                    e.tensor_sub(b.stage2[:, 0:T], qr[:], b.stage[:, 0:T]).then_inc(s_v.h, 1)
                ph.em("vector", f_lo)
                s_ld2.v += 32
                def f_st(e, hc=hc, v_lo=v_lo, a_hi=a_hi):
                    e.wait_ge(s_v.h, v_lo)
                    e.wait_ge(s_a.h, a_hi)
                    e.dma_start(out=b.qhi_d[128 * hc:128 * (hc + 1), :],
                                in_=b.stage[:, 0:T]).then_inc(s_ld2.h, 16)
                    e.dma_start(out=b.qlo_d[128 * hc:128 * (hc + 1), :],
                                in_=b.stage2[:, 0:T]).then_inc(s_ld2.h, 16)
                ph.em("scalar", f_st)
                epi_a[hc] = a_hi
                epi_v[hc] = v_lo
            elif hc < 20:
                g = hc - 16
                s_a.v += 1
                a_ev = s_a.v
                def f_ev(e, buf=buf, mmv=mm_t[hc], pv=epi_v.get(hc - 1, 0), pst_=0):
                    e.wait_ge(s_t.h, mmv)
                    e.wait_ge(s_v.h, pv)
                    e.activation(out=qf[:], in_=ps_qkv[buf][:], func=ACT.Copy,
                                 bias=0.0, scale=1.0).then_inc(s_a.h, 1)
                ph.em("scalar", f_ev)
                s_v.v += 4
                v_rope = s_v.v
                def f_rope(e, a_ev=a_ev, pstore=b.s_ld2.v):
                    e.wait_ge(s_a.h, a_ev)
                    e.wait_ge(s_ld2.h, pstore)
                    e.stream_shuffle(qsw[:], qf[:], swapmask)
                    e.tensor_mul(qr[:], qf[:], b.ctabK[:])
                    e.tensor_mul(qsw[:], qsw[:], b.stabK[:])
                    e.tensor_add(qr[:], qr[:], qsw[:]).then_inc(s_v.h, 4)
                ph.em("vector", f_rope)
                s_a.v += 1
                a_hi = s_a.v
                def f_hi(e, v_rope=v_rope):
                    e.wait_ge(s_v.h, v_rope)
                    e.activation(out=kst_hi[:], in_=qr[:], func=ACT.Copy,
                                 bias=0.0, scale=1.0).then_inc(s_a.h, 1)
                ph.em("scalar", f_hi)
                s_v.v += 1
                v_lo = s_v.v
                def f_lo(e, a_hi=a_hi):
                    e.wait_ge(s_a.h, a_hi)
                    e.tensor_sub(kst_lo[:], qr[:], kst_hi[:]).then_inc(s_v.h, 1)
                ph.em("vector", f_lo)
                s_ld2.v += 32
                def f_st(e, g=g, v_lo=v_lo, a_hi=a_hi):
                    e.wait_ge(s_v.h, v_lo)
                    e.wait_ge(s_a.h, a_hi)
                    e.dma_start(out=b.ag_in[128 * g:128 * (g + 1), :], in_=kst_hi[:]).then_inc(s_ld2.h, 16)
                    e.dma_start(out=b.ag_in[512 + 128 * g:512 + 128 * (g + 1), :], in_=kst_lo[:]).then_inc(s_ld2.h, 16)
                ph.em("scalar", f_st)
                epi_a[hc] = a_hi
                epi_v[hc] = v_lo
            else:
                g = hc - 20
                s_v.v += 1
                v_dq = s_v.v
                def f_dq(e, buf=buf, mmv=mm_t[hc], pv=epi_v.get(hc - 1, 0)):
                    e.wait_ge(s_t.h, mmv)
                    e.wait_ge(s_v.h, pv)
                    e.tensor_mul(qf[:], ps_qkv[buf][:], b.dqb[:]).then_inc(s_v.h, 1)
                ph.em("vector", f_dq)
                tr_t = []
                for kb in range(2):
                    s_t.v += 1
                    tr_t.append(s_t.v)
                    def f_tr(e, kb=kb, v_dq=v_dq, pa=epi_a.get(hc - 1, 0), pvv=epi_v.get(hc - 1, 0)):
                        e.wait_ge(s_v.h, v_dq)
                        e.wait_ge(s_a.h, pa)
                        e.wait_ge(s_v.h, pvv)
                        for j in range(4):
                            ts = 4 * kb + j
                            li = e.transpose(trpv[kb][:, 128 * j:128 * (j + 1)],
                                             qf[:, 128 * ts:128 * (ts + 1)], b.idf[:])
                        li.then_inc(s_t.h, 1)
                    ph.em("tensor", f_tr)
                s_a.v += 2
                a_hi = s_a.v
                def f_hi(e, tr_t=tuple(tr_t), pst2=b.s_ld2.v):
                    e.wait_ge(s_ld2.h, pst2)
                    for kb in range(2):
                        e.wait_ge(s_t.h, tr_t[kb])
                        e.activation(out=vst_hi[:, 512 * kb:512 * (kb + 1)], in_=trpv[kb][:],
                                     func=ACT.Copy, bias=0.0, scale=1.0).then_inc(s_a.h, 1)
                ph.em("scalar", f_hi)
                s_v.v += 2
                v_lo = s_v.v
                def f_lo(e, a_hi=a_hi):
                    e.wait_ge(s_a.h, a_hi)
                    for kb in range(2):
                        e.tensor_sub(vst_lo[:, 512 * kb:512 * (kb + 1)], trpv[kb][:],
                                     vst_hi[:, 512 * kb:512 * (kb + 1)]).then_inc(s_v.h, 1)
                ph.em("vector", f_lo)
                s_ld2.v += 32
                def f_st(e, g=g, v_lo=v_lo, a_hi=a_hi):
                    e.wait_ge(s_v.h, v_lo)
                    e.wait_ge(s_a.h, a_hi)
                    for src, base in ((vst_hi, 1024 * T), (vst_lo, 1536 * T)):
                        out = bass.AP(b.ag_in, base + 128 * g,
                                      [[512, 128], [128 * 512, 8], [1, 128]])
                        in_ = bass.AP(src, 0, [[T, 128], [128, 8], [1, 128]])
                        e.dma_start(out=out, in_=in_).then_inc(s_ld2.h, 16)
                ph.em("scalar", f_st)
                epi_a[hc] = a_hi
                epi_v[hc] = v_lo

    def p3(self, ph):
        b = self
        need = [(b.s_ld2.h, b.s_ld2.v)]
        b.s_g.v += 1
        gv = b.s_g.v
        def f_ag(e, need=tuple(need)):
            for h, v in need:
                e.wait_ge(h, v)
            e.collective_compute(
                "AllGather", ALU.bypass,
                replica_groups=[list(range(NC))],
                ins=[b.ag_in[:, :]],
                outs=[b.ag_out[:, :]],
            ).then_inc(b.s_g.h, 1)
        ph.em("gpsimd", f_ag)
        for eng in ("sync", "vector", "scalar", "tensor"):
            ph.em(eng, lambda e, gv=gv: e.wait_ge(b.s_g.h, gv))

    def declare_pB(self, stack):
        sb = self.sbuf
        self.r2T = sb("r2T", [128, NDC * T], BF16, stack=stack, side="right")

    def declare_p4locals(self, stack):
        sb = self.sbuf
        self.qtlhi = [sb("qtlhi0", [128, D], BF16, stack=stack, side="right")] * 2
        self.qtllo = [sb("qtllo0", [128, D], BF16, stack=stack, side="right")] * 2
        self.khi = [sb(f"khi{i}", [128, NKV * 1024], BF16, stack=stack, side="right") for i in range(2)]
        self.klo = [sb(f"klo{i}", [128, NKV * 1024], BF16, stack=stack, side="right") for i in range(2)]
        self.vhi = [sb(f"vhi{i}", [128, NC * 512], BF16, stack=stack, side="right") for i in range(2)]
        self.vlo = [sb(f"vlo{i}", [128, NC * 512], BF16, stack=stack, side="right") for i in range(2)]
        self.phiT = sb("phiT", [128, 8 * 512], BF16, stack=stack, side="right")
        self.ploT = sb("ploT", [128, 8 * 512], BF16, stack=stack, side="right")
        self.fTrow = sb("fTrow", [1, 512], F32, stack=stack, side="right")
        self.fb_sb = sb("fb_sb", [128, 512], F32, stack=stack, side="right")
        self.q2b = sb("q2b", [128, 128], F32, stack=stack, side="right")
        self.colsum = sb("colsum", [128, 128], F32, stack=stack, side="right")
        self.O_sb = sb("O_sb", [128, NH * 128], F32, stack=stack, side="right")
        self.m_st = sb("m_st", [128, NH], F32, stack=stack, side="right")
        self.l_st = sb("l_st", [128, NH], F32, stack=stack, side="right")
        self.f_st = sb("f_st", [128, NH], F32, stack=stack, side="right")
        self.cs_st = sb("cs_st", [128, NH], F32, stack=stack, side="right")
        self.nmn_st = sb("nmn_st", [128, NH], F32, stack=stack, side="right")
        self.tm1 = sb("tm1", [128, 1], F32, stack=stack, side="right")

    def p4(self, ph, pst):
        b = self
        self.declare_p4locals(pst)
        s_ld, s_ld2, s_v, s_a, s_t = b.s_ld, b.s_ld2, b.s_v, b.s_a, b.s_t
        ps_sc = [b.psum(f"p4sc{i}", [128, 1024], F32, pst) for i in range(2)]
        trp = [b.psum(f"p4tr{i}", [128, 512], F32, pst) for i in range(2)]
        ps_pv = [b.psum(f"p4pv{i}", [128, 512], F32, pst) for i in range(2)]
        p32 = b.ta

        n_qt = self.cfg.get("n_qt", NT)
        hseq = 0
        exp_a = {}
        msk_v = {}
        pv_done_v = {}
        tr_a = {}
        tr_v = {}

        s_v.v += 3
        def f_init(e):
            e.memset(b.O_sb[:], 0.0)
            e.memset(b.l_st[:], 0.0)
            e.memset(b.m_st[:], NEG_BIG).then_inc(s_v.h, 3)
        ph.em("vector", f_init)
        s_ld2.v += 32
        def f_q0(e):
            for src, dst in ((b.qhi_d, b.qtlhi[0]), (b.qlo_d, b.qtllo[0])):
                in_ = bass.AP(src, 0, [[T, 128], [128 * T, NDC], [1, 128]])
                e.dma_start(out=dst[:], in_=in_).then_inc(s_ld2.h, 16)
        ph.em("scalar", f_q0)

        vis = 0
        sc_t = {}
        trp_t = {}
        pv_t = {}
        kv_free = {}
        kv_ld = {}
        pend = []

        gq_pv_t = {}
        gq_oadd = {}
        def emit_head_pe_tail(hp):
            info = pend.pop(0)
            assert info[0] == hp
            g_, vb_, visidx, hh = info[1], info[2], info[3], info[4]
            p_ = hh % 4
            gseq = hp // 4
            t_trs = []
            for kb in range(2):
                s_t.v += 1
                t_trs.append(s_t.v)
                def f_tr(e, kb=kb, hp=hp, ea=exp_a[hp], pa=tr_a.get(hp - 1, 0),
                         pv_=tr_v.get(hp - 1, 0)):
                    e.wait_ge(s_a.h, ea)
                    e.wait_ge(s_a.h, pa)
                    e.wait_ge(s_v.h, pv_)
                    p32v = b.ta[:, 1024 * (hp % 2):1024 * (hp % 2 + 1)]
                    for j in range(4):
                        kc = 4 * kb + j
                        li = e.transpose(trp[kb][:, 128 * j:128 * (j + 1)],
                                         p32v[:, 128 * kc:128 * (kc + 1)], b.idf[:])
                    li.then_inc(s_t.h, 1)
                ph.em("tensor", f_tr)
            trp_t[hp] = tuple(t_trs)
            s_a.v += 2
            a_phi = s_a.v
            def f_phi(e, t_trs=tuple(t_trs), p_=p_, ppv=gq_pv_t.get(gseq - 1, 0)):
                e.wait_ge(s_t.h, ppv)
                for kb in range(2):
                    e.wait_ge(s_t.h, t_trs[kb])
                    out = bass.AP(b.phiT, (4 * kb) * 512 + 128 * p_,
                                  [[8 * 512, 128], [512, 4], [1, 128]])
                    e.activation(out=out, in_=trp[kb][:], func=ACT.Copy,
                                 bias=0.0, scale=1.0).then_inc(s_a.h, 1)
            ph.em("scalar", f_phi)
            tr_a[hp] = s_a.v
            s_v.v += 2
            def f_plo(e, a_phi=a_phi, p_=p_):
                e.wait_ge(s_a.h, a_phi)
                for kb in range(2):
                    out = bass.AP(b.ploT, (4 * kb) * 512 + 128 * p_,
                                  [[8 * 512, 128], [512, 4], [1, 128]])
                    in1 = bass.AP(b.phiT, (4 * kb) * 512 + 128 * p_,
                                  [[8 * 512, 128], [512, 4], [1, 128]])
                    e.tensor_sub(out, trp[kb][:], in1).then_inc(s_v.h, 1)
            ph.em("vector", f_plo)
            tr_v[hp] = s_v.v
            if p_ != 3:
                return
            pvb = ps_pv[gseq % 2]
            s_t.v += 1
            t_ftr = s_t.v
            def f_ftr(e, g=g_, ea=exp_a[hp], pa=tr_a[hp], pv_=tr_v[hp]):
                e.wait_ge(s_a.h, pa)
                e.wait_ge(s_v.h, pv_)
                e.transpose(trp[0][0:4, 0:128], b.f_st[:, 4 * g:4 * (g + 1)],
                            b.idf[:]).then_inc(s_t.h, 1)
            ph.em("tensor", f_ftr)
            s_a.v += 1
            a_er = s_a.v
            def f_er(e, t_ftr=t_ftr):
                e.wait_ge(s_t.h, t_ftr)
                e.activation(out=b.evrow[0:4, :], in_=trp[0][0:4, 0:128], func=ACT.Copy,
                             bias=0.0, scale=1.0).then_inc(s_a.h, 1)
            ph.em("scalar", f_er)
            s_ld2.v += 16
            ld_fr = s_ld2.v
            def f_frd(e, a_er=a_er, pfb=gq_pv_t.get(gseq - 1, 0)):
                e.wait_ge(s_a.h, a_er)
                e.wait_ge(s_t.h, pfb)
                e.dma_start(out=b.fTrow[0:1, :],
                            in_=bass.AP(b.evrow, 0, [[128, 4], [1, 128]])).then_inc(s_ld2.h, 16)
            ph.em("scalar", f_frd)
            s_t.v += 1
            t_fb = s_t.v
            def f_fb(e, g=g_, ld_fr=ld_fr, po=gq_oadd.get(gseq - 2, 0)):
                e.wait_ge(s_ld2.h, ld_fr)
                e.wait_ge(s_v.h, po)
                e.matmul(pvb[:], b.onesrow1[0:1, 0:128], b.fTrow[0:1, :],
                         start=True, stop=True).then_inc(s_t.h, 1)
            ph.em("tensor", f_fb)
            s_a.v += 1
            a_fbe = s_a.v
            def f_fbe(e, t_fb=t_fb, pv_=gq_oadd.get(gseq - 1, 0)):
                e.wait_ge(s_t.h, t_fb)
                e.wait_ge(s_v.h, pv_)
                e.activation(out=b.fb_sb[:], in_=pvb[:], func=ACT.Copy,
                             bias=0.0, scale=1.0).then_inc(s_a.h, 1)
            ph.em("scalar", f_fbe)
            s_v.v += 1
            v_om = s_v.v
            def f_om(e, g=g_, a_fbe=a_fbe):
                e.wait_ge(s_a.h, a_fbe)
                e.tensor_mul(b.O_sb[:, 512 * g:512 * (g + 1)],
                             b.O_sb[:, 512 * g:512 * (g + 1)], b.fb_sb[:]).then_inc(s_v.h, 1)
            ph.em("vector", f_om)
            s_t.v += 1
            gq_pv_t[gseq] = s_t.v
            def f_pv(e, g=g_, pvb=pvb, vb=vb_, kvld=kv_ld[visidx], vom=v_om,
                     ta_=tr_a[hp], tv_=tr_v[hp]):
                e.wait_ge(s_a.h, ta_)
                e.wait_ge(s_v.h, tv_)
                e.wait_ge(s_v.h, vom)
                e.wait_ge(s_ld.h, kvld)
                for kc in range(NC):
                    vh = b.vhi[vb][:, 512 * kc + 128 * g:512 * kc + 128 * (g + 1)]
                    vl = b.vlo[vb][:, 512 * kc + 128 * g:512 * kc + 128 * (g + 1)]
                    pT = b.phiT[:, 512 * kc:512 * (kc + 1)]
                    pTl = b.ploT[:, 512 * kc:512 * (kc + 1)]
                    e.matmul(pvb[:], vh, pT, start=(kc == 0), stop=False)
                    e.matmul(pvb[:], vl, pT, start=False, stop=False)
                    li = e.matmul(pvb[:], vh, pTl, start=False, stop=(kc == NC - 1))
                li.then_inc(s_t.h, 1)
            ph.em("tensor", f_pv)
            s_v.v += 1
            def f_oa(e, g=g_, pvb=pvb, pvt=gq_pv_t[gseq]):
                e.wait_ge(s_t.h, pvt)
                e.tensor_add(b.O_sb[:, 512 * g:512 * (g + 1)],
                             b.O_sb[:, 512 * g:512 * (g + 1)], pvb[:]).then_inc(s_v.h, 1)
            ph.em("vector", f_oa)
            gq_oadd[gseq] = s_v.v
            if hh == NH - 1:
                kv_free[visidx] = s_t.v

        for I in range(n_qt):
            qb = I % 2
            q_ld2 = s_ld2.v
            for J in range(I + 1):
                vb = vis % 2
                s_ld.v += 16 * 10
                kv_ld[vis] = s_ld.v
                def f_kv(e, J=J, vb=vb, free_t=kv_free.get(vis - 2, 0)):
                    e.wait_ge(s_t.h, free_t)
                    for g in range(NKV):
                        e.dma_start(out=b.khi[vb][:, 1024 * g:1024 * (g + 1)],
                                    in_=bass.AP(b.ag_out, 128 * g * T + 128 * J,
                                                [[T, 128], [2048 * T, NC], [1, 128]])
                                    ).then_inc(s_ld.h, 16)
                    for g in range(NKV):
                        e.dma_start(out=b.klo[vb][:, 1024 * g:1024 * (g + 1)],
                                    in_=bass.AP(b.ag_out, (512 + 128 * g) * T + 128 * J,
                                                [[T, 128], [2048 * T, NC], [1, 128]])
                                    ).then_inc(s_ld.h, 16)
                    e.dma_start(out=b.vhi[vb][:],
                                in_=bass.AP(b.ag_out, 1024 * T + 128 * J * 512, [[512, 128], [2048 * T, NC], [1, 512]])
                                ).then_inc(s_ld.h, 16)
                    e.dma_start(out=b.vlo[vb][:],
                                in_=bass.AP(b.ag_out, 1536 * T + 128 * J * 512, [[512, 128], [2048 * T, NC], [1, 512]])
                                ).then_inc(s_ld.h, 16)
                ph.em("sync", f_kv)
                for h in range(NH):
                    g = h // 4
                    bank = ps_sc[hseq % 2]
                    s_t.v += 1
                    sc_t[hseq] = s_t.v
                    def f_sc(e, h=h, g=g, bank=bank, kvld=kv_ld[vis], qb=qb, q_ld2=q_ld2,
                             pe=exp_a.get(hseq - 2, 0), pm=msk_v.get(hseq - 2, 0), vb=vb):
                        e.wait_ge(s_ld.h, kvld - 32)
                        e.wait_ge(s_ld2.h, q_ld2)
                        e.wait_ge(s_a.h, pe)
                        e.wait_ge(s_v.h, pm)
                        for kh in range(2):
                            out = bank[:, 512 * kh:512 * (kh + 1)]
                            rhs = b.khi[vb][:, 1024 * g + 512 * kh:1024 * g + 512 * (kh + 1)]
                            rhs_lo = b.klo[vb][:, 1024 * g + 512 * kh:1024 * g + 512 * (kh + 1)]
                            lh = b.qtlhi[qb][:, 128 * h:128 * (h + 1)]
                            ll = b.qtllo[qb][:, 128 * h:128 * (h + 1)]
                            e.matmul(out, lh, rhs, start=True, stop=False)
                            e.matmul(out, lh, rhs_lo, start=False, stop=False)
                            li = e.matmul(out, ll, rhs, start=False, stop=True)
                        li.then_inc(s_t.h, 1)
                    ph.em("tensor", f_sc)
                    if pend:
                        emit_head_pe_tail(hseq - 1)
                    nv = 5 if J == I else 4
                    s_v.v += nv
                    def f_mx(e, h=h, bank=bank, sct=sc_t[hseq], diag=(J == I), nv=nv):
                        e.wait_ge(s_t.h, sct)
                        if diag:
                            e.tensor_add(bank[:], bank[:], b.masksb[:])
                        e.reduce_max(b.tm1[:], bank[:], axis=AXX)
                        Build.sp(e)
                        e.tensor_scalar(b.f_st[:, h:h + 1], b.m_st[:, h:h + 1], b.tm1[:, 0:1],
                                        0.0, op0=ALU.subtract, op1=ALU.min)
                        e.tensor_scalar(b.nmn_st[:, h:h + 1], b.m_st[:, h:h + 1], b.tm1[:, 0:1],
                                        -1.0, op0=ALU.max, op1=ALU.mult)
                        e.tensor_scalar(b.m_st[:, h:h + 1], b.m_st[:, h:h + 1], b.tm1[:, 0:1],
                                        None, op0=ALU.max).then_inc(s_v.h, nv)
                    ph.em("vector", f_mx)
                    msk_v[hseq] = s_v.v
                    s_a.v += 2
                    def f_exp(e, h=h, hp=hseq, bank=bank, mv=msk_v[hseq],
                              ptr_v=trp_t.get(hseq - 2, (0, 0))):
                        e.wait_ge(s_v.h, mv)
                        e.wait_ge(s_t.h, ptr_v[1])
                        p32v = b.ta[:, 1024 * (hp % 2):1024 * (hp % 2 + 1)]
                        e.activation(out=b.f_st[:, h:h + 1], in_=b.f_st[:, h:h + 1],
                                     func=ACT.Exp, bias=0.0, scale=1.0).then_inc(s_a.h, 1)
                        e.activation(out=p32v, in_=bank[:], func=ACT.Exp,
                                     bias=b.nmn_st[:, h:h + 1], scale=1.0,
                                     accum_out=b.cs_st[:, h:h + 1]).then_inc(s_a.h, 1)
                    ph.em("scalar", f_exp)
                    exp_a[hseq] = s_a.v
                    pend.append((hseq, g, vb, vis, h))
                    hseq += 1
                s_v.v += 2
                def f_l(e, ea=s_a.v):
                    e.wait_ge(s_a.h, ea)
                    e.tensor_mul(b.l_st[:], b.l_st[:], b.f_st[:])
                    Build.sp(e)
                    e.tensor_add(b.l_st[:], b.l_st[:], b.cs_st[:]).then_inc(s_v.h, 2)
                ph.em("vector", f_l)
                vis += 1
            if pend:
                emit_head_pe_tail(hseq - 1)
            if I + 1 < n_qt:
                s_ld2.v += 32
                def f_qn(e, I=I, pe=s_t.v):
                    e.wait_ge(s_t.h, pe)
                    for src, dst in ((b.qhi_d, b.qtlhi[0]), (b.qlo_d, b.qtllo[0])):
                        in_ = bass.AP(src, 128 * (I + 1), [[T, 128], [128 * T, NDC], [1, 128]])
                        e.dma_start(out=dst[:], in_=in_).then_inc(s_ld2.h, 16)
                ph.em("scalar", f_qn)
            s_v.v += 1
            v_li = s_v.v
            def f_li(e):
                Build.sp(e)
                Build.sp(e)
                e.reciprocal(b.cs_st[:], b.l_st[:]).then_inc(s_v.h, 1)
            ph.em("vector", f_li)
            s_t.v += 1
            t_ltr = s_t.v
            def f_ltr(e, v_li=v_li, pa=s_a.v, pv=s_v.v):
                e.wait_ge(s_v.h, v_li)
                e.wait_ge(s_a.h, pa)
                e.transpose(trp[0][0:NH, 0:128], b.cs_st[:], b.idf[:]).then_inc(s_t.h, 1)
            ph.em("tensor", f_ltr)
            s_a.v += 1
            a_ler = s_a.v
            def f_ler(e, t_ltr=t_ltr):
                e.wait_ge(s_t.h, t_ltr)
                e.activation(out=b.evrow[0:NH, :], in_=trp[0][0:NH, 0:128], func=ACT.Copy,
                             bias=0.0, scale=1.0).then_inc(s_a.h, 1)
            ph.em("scalar", f_ler)
            for g in range(NKV):
                s_ld2.v += 16
                ld_lr = s_ld2.v
                ph.em("scalar", lambda e, g=g, a=a_ler, pfb=s_t.v: (
                    e.wait_ge(s_a.h, a),
                    e.wait_ge(s_t.h, pfb),
                    e.dma_start(out=b.fTrow[0:1, :],
                                in_=bass.AP(b.evrow, 128 * 4 * g, [[128, 4], [1, 128]])).then_inc(s_ld2.h, 16),
                ))
                s_t.v += 1
                t_lb = s_t.v
                def f_lb(e, g=g, ld_lr=ld_lr, po=s_v.v):
                    e.wait_ge(s_ld2.h, ld_lr)
                    e.wait_ge(s_v.h, po)
                    e.matmul(ps_pv[0][:], b.onesrow1[0:1, 0:128], b.fTrow[0:1, :],
                             start=True, stop=True).then_inc(s_t.h, 1)
                ph.em("tensor", f_lb)
                s_a.v += 1
                a_lbe = s_a.v
                def f_lbe(e, t_lb=t_lb, pv=s_v.v):
                    e.wait_ge(s_t.h, t_lb)
                    e.wait_ge(s_v.h, pv)
                    e.activation(out=b.fb_sb[:], in_=ps_pv[0][:], func=ACT.Copy,
                                 bias=0.0, scale=1.0).then_inc(s_a.h, 1)
                ph.em("scalar", f_lbe)
                s_v.v += 1
                def f_attg(e, g=g, a_lbe=a_lbe):
                    e.wait_ge(s_a.h, a_lbe)
                    e.tensor_mul(b.tb[:, 512 * g:512 * (g + 1)],
                                 b.O_sb[:, 512 * g:512 * (g + 1)], b.fb_sb[:]).then_inc(s_v.h, 1)
                ph.em("vector", f_attg)
            v_att = s_v.v
            s_v.v += 2
            v_cs = s_v.v
            def f_cs(e):
                e.tensor_mul(b.tc_[:], b.tb[:], b.tb[:])
                hview = bass.AP(b.tc_, 0, [[D, 128], [1, 128], [128, NH]])
                e.tensor_reduce(b.colsum[:], hview, axis=AXX, op=ALU.add).then_inc(s_v.h, 2)
            ph.em("vector", f_cs)
            s_t.v += 1
            t_ss = s_t.v
            def f_ssm(e, v_cs=v_cs, pa=s_a.v):
                e.wait_ge(s_v.h, v_cs)
                e.wait_ge(s_a.h, pa)
                e.matmul(trp[1][0:128, 0:1], b.colsum[:], b.onescol[:],
                         start=True, stop=True).then_inc(s_t.h, 1)
            ph.em("tensor", f_ssm)
            s_a.v += 1
            a_srt = s_a.v
            def f_srt(e, t_ss=t_ss):
                e.wait_ge(s_t.h, t_ss)
                e.activation(out=b.c2[:], in_=trp[1][0:128, 0:1], func=ACT.Sqrt,
                             bias=b.epsc[:, 0:1], scale=1.0 / D).then_inc(s_a.h, 1)
            ph.em("scalar", f_srt)
            s_v.v += 1
            v_ri = s_v.v
            def f_ri(e, a_srt=a_srt):
                e.wait_ge(s_a.h, a_srt)
                e.reciprocal(b.c2[:], b.c2[:]).then_inc(s_v.h, 1)
            ph.em("vector", f_ri)
            s_v.v += 1
            v_q2 = s_v.v
            def f_q2(e):
                e.tensor_mul(b.tc_[:], b.tb[:], b.nwsb[:])
                hview = bass.AP(b.tc_, 0, [[D, 128], [1, 128], [128, NH]])
                e.tensor_reduce(b.colsum[:], hview, axis=AXX, op=ALU.max,
                                apply_absolute_value=True).then_inc(s_v.h, 1)
            ph.em("vector", f_q2)
            s_t.v += 1
            t_cm = s_t.v
            def f_cm(e, v_q2=v_q2, pa=s_a.v):
                e.wait_ge(s_v.h, v_q2)
                e.wait_ge(s_a.h, pa)
                e.transpose(trp[0][:, 0:128], b.colsum[:], b.idf[:]).then_inc(s_t.h, 1)
            ph.em("tensor", f_cm)
            s_a.v += 1
            a_cme = s_a.v
            def f_cme(e, t_cm=t_cm):
                e.wait_ge(s_t.h, t_cm)
                e.activation(out=b.q2b[:], in_=trp[0][0:128, 0:128], func=ACT.Copy,
                             bias=0.0, scale=1.0).then_inc(s_a.h, 1)
            ph.em("scalar", f_cme)
            s_v.v += 1
            v_cm2 = s_v.v
            def f_cm2(e, a_cme=a_cme, I=I, v_ri=v_ri):
                e.wait_ge(s_a.h, a_cme)
                e.reduce_max(b.c1[:], b.q2b[:], axis=AXX)
                Build.sp(e)
                e.tensor_mul(b.c1[:], b.c1[:], b.c2[:])
                Build.sp(e)
                e.tensor_scalar_max(b.c1[:], b.c1[:], 1e-5)
                Build.sp(e)
                e.reciprocal(b.c3[:], b.c1[:])
                Build.sp(e)
                e.tensor_scalar_mul(b.c3[:], b.c3[:], 127.0)
                e.tensor_scalar_mul(b.c4[:], b.c1[:], 1.0 / 127.0)
                Build.sp(e)
                e.tensor_copy(b.ds2_cols[:, I:I + 1], b.c4[:])
                e.tensor_mul(b.c3[:], b.c3[:], b.c2[:]).then_inc(s_v.h, 1)
            ph.em("vector", f_cm2)
            s_t.v += 2
            t_ctr = s_t.v - 1
            t_qb = s_t.v
            def f_bc(e, v_cm2=v_cm2, pa=s_a.v):
                e.wait_ge(s_v.h, v_cm2)
                e.wait_ge(s_a.h, pa)
                e.transpose(trp[1][0:1, 0:128], b.c3[:], b.idf[:]).then_inc(s_t.h, 1)
            ph.em("tensor", f_bc)
            s_a.v += 1
            a_ctr = s_a.v
            def f_ctre(e, t_ctr=t_ctr):
                e.wait_ge(s_t.h, t_ctr)
                e.activation(out=b.rowa[:], in_=trp[1][0:1, 0:128], func=ACT.Copy,
                             bias=0.0, scale=1.0).then_inc(s_a.h, 1)
            ph.em("scalar", f_ctre)
            def f_qbm(e, a_ctr=a_ctr):
                e.wait_ge(s_a.h, a_ctr)
                e.matmul(ps_pv[1][:, 0:128], b.onesrow1[0:1, 0:128], b.rowa[0:1, :],
                         start=True, stop=True).then_inc(s_t.h, 1)
            ph.em("tensor", f_qbm)
            s_a.v += 1
            a_bc = s_a.v
            def f_bce(e, t_qb=t_qb):
                e.wait_ge(s_t.h, t_qb)
                e.activation(out=b.q2b[:], in_=ps_pv[1][:, 0:128], func=ACT.Copy,
                             bias=0.0, scale=1.0).then_inc(s_a.h, 1)
            ph.em("scalar", f_bce)
            s_v.v += 1
            v_quant = s_v.v
            def f_q(e, a_bc=a_bc, I=I):
                e.wait_ge(s_a.h, a_bc)
                for h in range(NH):
                    e.tensor_mul(b.te[:, 128 * h:128 * (h + 1)],
                                 b.tc_[:, 128 * h:128 * (h + 1)], b.q2b[:])
                e.tensor_copy(b.ti[:], b.te[:])
                e.tensor_copy(b.tc_[:], b.ti[:])
                e.tensor_sub(b.td[:], b.te[:], b.tc_[:])
                e.tensor_single_scalar(b.te[:], b.td[:], 0.5, op=ALU.is_ge)
                e.tensor_add(b.tc_[:], b.tc_[:], b.te[:])
                e.tensor_single_scalar(b.te[:], b.td[:], -0.5, op=ALU.is_le)
                e.tensor_sub(b.td[:], b.tc_[:], b.te[:])
                e.memset(b.O_sb[:], 0.0)
                e.memset(b.l_st[:], 0.0)
                e.memset(b.m_st[:], NEG_BIG).then_inc(s_v.h, 1)
            ph.em("vector", f_q)
            s_a.v += 1
            def f_r2w(e, vq=v_quant, I=I):
                e.wait_ge(s_v.h, vq)
                out = bass.AP(b.r2T, 128 * I, [[NDC * T, 128], [T, NDC], [1, 128]])
                e.activation(out=out, in_=b.td[:], func=ACT.Copy, bias=0.0,
                             scale=1.0).then_inc(s_a.h, 1)
            ph.em("scalar", f_r2w)
            tr_a[hseq - 1] = s_a.v
            tr_v[hseq - 1] = s_v.v
            gq_oadd[(hseq - 1) // 4] = s_v.v
            gq_pv_t[(hseq - 1) // 4] = s_t.v

    def p5(self, ph, pst):
        b = self
        s_ld, s_ld2, s_v, s_a, s_t, s_st = b.s_ld, b.s_ld2, b.s_v, b.s_a, b.s_t, b.s_st
        ps_wo = [b.psum(f"p5wo{i}", [128, 512], F32, pst) for i in range(NT)]
        wot = [b.sbuf(f"p5wot{i}", [128, 512], F32, stack=pst, side="right") for i in range(2)]
        woev = b.sbuf("p5woev", [128, 512], F32, stack=pst, side="right")
        conv_v = {}
        mm_t = {}
        ld_v = {}
        for dm in range(4):
            evprev_a = s_a.v
            evprev_st = s_st.v
            for dc in range(NDC):
                i = dm * NDC + dc
                buf = i % 2
                s_ld.v += 16
                ld_v[i] = s_ld.v
                def f_ld(e, dm=dm, dc=dc, buf=buf, wc=conv_v.get(i - 2, 0)):
                    e.wait_ge(s_v.h, wc)
                    e.dma_start(out=wot[buf][:],
                                in_=b.woT[128 * dc:128 * (dc + 1), 512 * dm:512 * (dm + 1)]
                                ).then_inc(s_ld.h, 16)
                ph.em("sync", f_ld)
                s_v.v += 2
                conv_v[i] = s_v.v
                def f_cv(e, buf=buf, ldv=ld_v[i], wm=mm_t.get(i - 2, 0)):
                    e.wait_ge(s_ld.h, ldv)
                    e.wait_ge(s_t.h, wm)
                    e.tensor_copy(b.stage[:, 512 * buf:512 * (buf + 1)], wot[buf][:]).then_inc(s_v.h, 1)
                    e.tensor_sub(b.stage2[:, 512 * buf:512 * (buf + 1)], wot[buf][:],
                                 b.stage[:, 512 * buf:512 * (buf + 1)]).then_inc(s_v.h, 1)
                ph.em("vector", f_cv)
                s_t.v += 1
                mm_t[i] = s_t.v
                def f_mm(e, dm=dm, dc=dc, buf=buf, cv=conv_v[i], ea=evprev_a):
                    e.wait_ge(s_v.h, cv)
                    e.wait_ge(s_a.h, ea)
                    for w_i, w in enumerate((b.stage[:, 512 * buf:512 * (buf + 1)],
                                             b.stage2[:, 512 * buf:512 * (buf + 1)])):
                        for tt in range(NT):
                            lhsT = b.r2T[:, dc * T + 128 * tt:dc * T + 128 * (tt + 1)]
                            li = e.matmul(ps_wo[tt][:], lhsT, w,
                                          start=(dc == 0 and w_i == 0), stop=(dc == NDC - 1 and w_i == 1))
                    li.then_inc(s_t.h, 1)
                ph.em("tensor", f_mm)
            for tt in range(NT):
                s_a.v += 1
                a_ev = s_a.v
                def f_ev(e, tt=tt, dm=dm, mmv=s_t.v, pst_=b.s_ld3.v):
                    e.wait_ge(s_t.h, mmv)
                    e.wait_ge(b.s_ld3.h, pst_)
                    e.activation(out=woev[:], in_=ps_wo[tt][:], func=ACT.Copy, bias=0.0,
                                 scale=b.ds2_cols[:, tt:tt + 1]).then_inc(s_a.h, 1)
                ph.em("scalar", f_ev)
                b.s_ld3.v += 16
                def f_acc(e, tt=tt, dm=dm, a_ev=a_ev):
                    e.wait_ge(s_a.h, a_ev)
                    e.dma_start(out=b.y_out[128 * tt:128 * (tt + 1), 512 * dm:512 * (dm + 1)],
                                in_=woev[:], accum_op=ALU.add).then_inc(b.s_ld3.h, 16)
                ph.em("gpsimd", f_acc)

    def declare_pC(self, stack):
        sb = self.sbuf
        self.xnthi = sb("xnthi", [128, NDC * 512], BF16, stack=stack, side="right")
        self.xntlo = sb("xntlo", [128, NDC * 512], BF16, stack=stack, side="right")
        self.gubuf_hi = sb("gubuf_hi", [128, 11 * 512], BF16, stack=stack, side="right")
        self.gubuf_lo = sb("gubuf_lo", [128, 11 * 512], BF16, stack=stack, side="right")
        self.wgi = [sb(f"wgi{i}", [128, D], I32, stack=stack, side="right") for i in range(2)]
        self.wui = [sb(f"wui{i}", [128, D], I32, stack=stack, side="right") for i in range(2)]
        self.wgbf = [sb(f"wgbf{i}", [128, D], BF16, stack=stack, side="right") for i in range(2)]
        self.wubf = [sb(f"wubf{i}", [128, D], BF16, stack=stack, side="right") for i in range(2)]
        self.widm = [sb(f"widm{i}", [128, 11 * 128], I32, stack=stack, side="right") for i in range(2)]
        self.wdbf = [sb(f"wdbf{i}", [128, 11 * 128], BF16, stack=stack, side="right") for i in range(2)]
        self.gsil = sb("gsil", [128, 512], F32, stack=stack, side="right")
        self.gu32 = sb("gu32", [128, 512], F32, stack=stack, side="right")
        self.ffev = sb("ffev", [128, 512], F32, stack=stack, side="right")
        self.fftr = sb("fftr", [128, 512], F32, stack=stack, side="right")

    def p6(self, ph, pst):
        b = self
        s_ld, s_ld2, s_v, s_a, s_t, s_st = b.s_ld, b.s_ld2, b.s_v, b.s_a, b.s_t, b.s_st
        ps_g_ = [b.psum(f"p6g{i}", [128, 512], F32, pst) for i in range(2)]
        ps_u_ = [b.psum(f"p6u{i}", [128, 512], F32, pst) for i in range(2)]
        ps_d = [b.psum(f"p6d{i}", [128, 512], F32, pst) for i in range(2)]
        trp6 = [b.psum(f"p6tr{i}", [128, 512], F32, pst) for i in range(2)]

        NQ = 4
        HCQ = NHC // NQ

        for th in range(2):
            s_ld2.v += 32
            xn_ld = s_ld2.v
            prev_all = (s_t.v, s_v.v, s_a.v)
            def f_xn(e, th=th, prev_all=prev_all):
                e.wait_ge(s_t.h, prev_all[0])
                e.wait_ge(s_v.h, prev_all[1])
                e.wait_ge(s_a.h, prev_all[2])
                for src, dst in ((b.xnhi_d, b.xnthi), (b.xnlo_d, b.xntlo)):
                    in_ = bass.AP(src, 512 * th, [[T, 128], [128 * T, NDC], [1, 512]])
                    e.dma_start(out=dst[:], in_=in_).then_inc(s_ld2.h, 16)
            ph.em("scalar", f_xn)
            guv = {}
            gua = {}
            ldg = {}
            cvv = {}
            mmgu_t = {}
            for q in range(NQ):
                for hc in range(HCQ):
                    hqc = q * HCQ + hc
                    buf = hc % 2
                    s_ld.v += 32
                    ldg[hqc] = s_ld.v
                    def f_ld(e, hqc=hqc, buf=buf, wc=cvv.get(hqc - 2, 0)):
                        e.wait_ge(s_v.h, wc)
                        for w_in, dst in ((b.wgT, b.wgi[buf]), (b.wuT, b.wui[buf])):
                            in_ = bass.AP(w_in, 128 * hqc, [[HID, 128], [128 * HID, NDC], [1, 128]])
                            e.dma_start(out=dst[:], in_=in_).then_inc(s_ld.h, 16)
                    ph.em("sync", f_ld)
                    s_v.v += 2
                    cvv[hqc] = s_v.v
                    def f_cv(e, buf=buf, ldv=ldg[hqc], wm=mmgu_t.get(hqc - 2, 0)):
                        e.wait_ge(s_ld.h, ldv)
                        e.wait_ge(s_t.h, wm)
                        e.tensor_scalar(b.wgbf[buf][:], b.wgi[buf][:], 128.0, None,
                                        op0=ALU.subtract).then_inc(s_v.h, 1)
                        e.tensor_scalar(b.wubf[buf][:], b.wui[buf][:], 128.0, None,
                                        op0=ALU.subtract).then_inc(s_v.h, 1)
                    ph.em("vector", f_cv)
                    s_t.v += 1
                    mmgu_t[hqc] = s_t.v
                    def f_mm(e, th=th, buf=buf, cv=cvv[hqc], xn_ld=xn_ld,
                             pg=gua.get(hqc - 2, 0), pv=guv.get(hqc - 2, 0)):
                        e.wait_ge(s_v.h, cv)
                        e.wait_ge(s_ld2.h, xn_ld)
                        e.wait_ge(s_a.h, pg)
                        e.wait_ge(s_v.h, pv)
                        for W, psx in ((b.wgbf[buf], ps_g_[buf]), (b.wubf[buf], ps_u_[buf])):
                            for dc in range(NDC):
                                for xn in (b.xnthi, b.xntlo):
                                    li = e.matmul(psx[:], W[:, 128 * dc:128 * (dc + 1)],
                                                  xn[:, 512 * dc:512 * (dc + 1)],
                                                  start=(dc == 0 and xn is b.xnthi),
                                                  stop=(dc == NDC - 1 and xn is b.xntlo))
                        li.then_inc(s_t.h, 1)
                    ph.em("tensor", f_mm)
                    s_a.v += 1
                    a_sil = s_a.v
                    def f_sil(e, hqc=hqc, buf=buf, mm=mmgu_t[hqc], pv=guv.get(hqc - 1, 0)):
                        e.wait_ge(s_t.h, mm)
                        e.wait_ge(s_v.h, pv)
                        e.activation(out=b.gsil[:], in_=ps_g_[buf][:], func=ACT.Silu, bias=0.0,
                                     scale=b.sgsb[:, hqc:hqc + 1]).then_inc(s_a.h, 1)
                    ph.em("scalar", f_sil)
                    s_v.v += 1
                    v_gu = s_v.v
                    def f_gu(e, hqc=hqc, buf=buf, a_sil=a_sil):
                        e.wait_ge(s_a.h, a_sil)
                        e.scalar_tensor_tensor(b.gu32[:], ps_u_[buf][:], b.susb[:, hqc:hqc + 1],
                                               b.gsil[:], op0=ALU.mult, op1=ALU.mult).then_inc(s_v.h, 1)
                    ph.em("vector", f_gu)
                    s_a.v += 1
                    a_hi = s_a.v
                    def f_hi(e, hc=hc, v_gu=v_gu):
                        e.wait_ge(s_v.h, v_gu)
                        e.activation(out=b.gubuf_hi[:, 512 * hc:512 * (hc + 1)], in_=b.gu32[:],
                                     func=ACT.Copy, bias=0.0, scale=1.0).then_inc(s_a.h, 1)
                    ph.em("scalar", f_hi)
                    gua[hqc] = s_a.v
                    s_v.v += 1
                    def f_lo(e, hc=hc, a_hi=a_hi):
                        e.wait_ge(s_a.h, a_hi)
                        e.tensor_sub(b.gubuf_lo[:, 512 * hc:512 * (hc + 1)], b.gu32[:],
                                     b.gubuf_hi[:, 512 * hc:512 * (hc + 1)]).then_inc(s_v.h, 1)
                    ph.em("vector", f_lo)
                    guv[hqc] = s_v.v
                gu_all_v = s_v.v
                gu_all_a = s_a.v
                dconv = {}
                dmm = {}
                for dm in range(NDC):
                    buf = dm % 2
                    s_ld.v += 16
                    dl = s_ld.v
                    def f_ld2(e, q=q, dm=dm, buf=buf, wc=dconv.get(dm - 2, 0)):
                        e.wait_ge(s_v.h, wc)
                        in_ = bass.AP(b.wdT, (q * HCQ * 128) * D + 128 * dm,
                                      [[D, 128], [128 * D, HCQ], [1, 128]])
                        e.dma_start(out=b.widm[buf][:], in_=in_).then_inc(s_ld.h, 16)
                    ph.em("sync", f_ld2)
                    s_v.v += 1
                    dconv[dm] = s_v.v
                    def f_cv2(e, buf=buf, dl=dl, wm=dmm.get(dm - 2, 0)):
                        e.wait_ge(s_ld.h, dl)
                        e.wait_ge(s_t.h, wm)
                        e.tensor_scalar(b.wdbf[buf][:], b.widm[buf][:], 128.0, None,
                                        op0=ALU.subtract).then_inc(s_v.h, 1)
                    ph.em("vector", f_cv2)
                    s_t.v += 1
                    dmm[dm] = s_t.v
                    def f_dmm(e, dm=dm, buf=buf, cv=dconv[dm], guv_=gu_all_v, gua_=gu_all_a,
                              pe=s_a.v):
                        e.wait_ge(s_v.h, cv)
                        e.wait_ge(s_v.h, guv_)
                        e.wait_ge(s_a.h, gua_)
                        e.wait_ge(s_a.h, pe)
                        for hc in range(HCQ):
                            for gu in (b.gubuf_hi, b.gubuf_lo):
                                li = e.matmul(ps_d[buf][:], b.wdbf[buf][:, 128 * hc:128 * (hc + 1)],
                                              gu[:, 512 * hc:512 * (hc + 1)],
                                              start=(hc == 0 and gu is b.gubuf_hi),
                                              stop=(hc == HCQ - 1 and gu is b.gubuf_lo))
                        li.then_inc(s_t.h, 1)
                    ph.em("tensor", f_dmm)
                    s_a.v += 1
                    a_ev = s_a.v
                    def f_ev(e, dm=dm, buf=buf, mm=dmm[dm], ptr=s_t.v):
                        e.wait_ge(s_t.h, mm)
                        e.wait_ge(s_t.h, ptr)
                        e.activation(out=b.ffev[:], in_=ps_d[buf][:], func=ACT.Copy, bias=0.0,
                                     scale=b.sdsb[:, dm:dm + 1]).then_inc(s_a.h, 1)
                    ph.em("scalar", f_ev)
                    s_t.v += 1
                    t_tr0 = s_t.v
                    def f_tr(e, a_ev=a_ev, pa=s_a.v):
                        e.wait_ge(s_a.h, a_ev)
                        e.wait_ge(s_a.h, pa)
                        for j in range(4):
                            li = e.transpose(trp6[0][:, 128 * j:128 * (j + 1)],
                                             b.ffev[:, 128 * j:128 * (j + 1)], b.idf[:])
                        li.then_inc(s_t.h, 1)
                    ph.em("tensor", f_tr)
                    s_a.v += 1
                    a_ev2 = s_a.v
                    def f_ev2(e, t_tr=t_tr0, pst_=b.s_ld3.v):
                        e.wait_ge(s_t.h, t_tr)
                        e.wait_ge(b.s_ld3.h, pst_)
                        e.activation(out=b.fftr[:], in_=trp6[0][:], func=ACT.Copy, bias=0.0,
                                     scale=1.0).then_inc(s_a.h, 1)
                    ph.em("scalar", f_ev2)
                    b.s_ld3.v += 16
                    def f_acc(e, th=th, dm=dm, a_ev2=a_ev2):
                        e.wait_ge(s_a.h, a_ev2)
                        out = bass.AP(b.y_out, (512 * th) * D + 128 * dm,
                                      [[D, 128], [128 * D, 4], [1, 128]])
                        e.dma_start(out=out, in_=b.fftr[:], accum_op=ALU.add).then_inc(b.s_ld3.h, 16)
                    ph.em("gpsimd", f_acc)



def make_core_inputs(inputs, core):
    x = np.asarray(inputs["L0_attn_x"])[0]
    wqkv = np.asarray(inputs["L0_attn_wqkv"])
    wo = np.asarray(inputs["L0_attn_wo"])
    nw = np.asarray(inputs["L0_attn_norm_w"])
    wg = np.asarray(inputs["L0_ffn_wq_gate"])
    sgv = np.asarray(inputs["L0_ffn_s_gate"])
    wu = np.asarray(inputs["L0_ffn_wq_up"])
    suv = np.asarray(inputs["L0_ffn_s_up"])
    wd = np.asarray(inputs["L0_ffn_wq_down"])
    sdv = np.asarray(inputs["L0_ffn_s_down"])

    shared = dict(
        wqkvT=np.ascontiguousarray(wqkv.T),
        woT=np.ascontiguousarray(wo.T),
        nwrep=np.ascontiguousarray(
            np.broadcast_to(nw.reshape(NH, 128).T[:, :, None], (128, NH, 128)).reshape(128, D)
        ).astype(np.float32),
        wgT=np.ascontiguousarray(wg.T),
        wuT=np.ascontiguousarray(wu.T),
        wdT=np.ascontiguousarray(wd.T),
        sg=np.ascontiguousarray(sgv.reshape(NHC, 128).T).astype(np.float32),
        su=np.ascontiguousarray(suv.reshape(NHC, 128).T).astype(np.float32),
        sd=np.ascontiguousarray(sdv.reshape(NDC, 128).T).astype(np.float32),
        xfr=np.ascontiguousarray((-(np.arange(128) // 2) / 64.0).reshape(128, 1)).astype(np.float32),
        sgn=np.ascontiguousarray(np.where(np.arange(128) % 2 == 0, -1.0, 1.0).reshape(128, 1)).astype(np.float32),
        idf=np.eye(128, dtype=np.float32),
        onesf=np.ones((128, 128), dtype=np.float32),
        onec=np.ones((128, 1), dtype=np.float32),
    )
    xc = np.ascontiguousarray(x[core::NC])
    pos = (core + NC * np.arange(T)).astype(np.float32).reshape(1, T)
    i_idx = np.arange(128)[:, None]
    r_idx = (np.arange(1024)[None, :] // 128)
    m_idx = (np.arange(1024)[None, :] % 128)
    allowed = (m_idx < i_idx) | ((m_idx == i_idx) & (r_idx <= core))
    mask = np.where(allowed, 0.0, NEG_BIG).astype(np.float32)
    return dict(shared, x=xc, pos=pos, mask=mask)


def build_full(cfg=None):
    b = Build(cfg or dict(debug_outputs={}))
    ph = Phase(b.nc)
    b.p0(ph)
    ph.run()
    with ExitStack() as sA:
        b.declare_pA(sA)
        with ExitStack() as sP1:
            ph = Phase(b.nc); b.p1(ph, sP1); ph.run()
            ph = Phase(b.nc); b.p15(ph, sP1); ph.run()
        with ExitStack() as sP2:
            ph = Phase(b.nc); b.guard_all(ph); b.p2(ph, sP2); ph.run()
            ph = Phase(b.nc); b.p3(ph); ph.run()
    with ExitStack() as sB:
        b.declare_pB(sB)
        with ExitStack() as sP4:
            ph = Phase(b.nc); b.guard_all(ph); b.p4(ph, sP4); ph.run()
        with ExitStack() as sP5:
            ph = Phase(b.nc); b.guard_all(ph); b.p5(ph, sP5); ph.run()
    with ExitStack() as sC:
        b.declare_pC(sC)
        with ExitStack() as sP6:
            ph = Phase(b.nc); b.guard_all(ph); b.p6(ph, sP6); ph.run()
    ph = Phase(b.nc)
    b.guard_all(ph)
    ph.run()
    b.es.close()
    return b


def kernel(**inputs):
    b = build_full()
    in_maps = [make_core_inputs(inputs, c) for c in range(NC)]
    res = run_bass_kernel_spmd(b.nc, in_maps, core_ids=list(range(NC)))
    y = np.empty((S, D), dtype=np.float32)
    for c in range(NC):
        y[c::NC] = res.results[c]["y"]
    return y.reshape(1, S, D)


# version 11
# speedup vs baseline: 1.0623x; 1.0623x over previous
import numpy as np
import ml_dtypes
from contextlib import ExitStack

import concourse.bass as bass
import concourse.mybir as mybir
from concourse.bass_utils import run_bass_kernel_spmd

F32 = mybir.dt.float32
BF16 = mybir.dt.bfloat16
I32 = mybir.dt.int32
ALU = mybir.AluOpType
ACT = mybir.ActivationFunctionType
AXX = mybir.AxisListType.X

S, D, NH, NKV, HD = 8192, 2048, 16, 4, 128
HQKV, HID, NC = 3072, 5632, 8
T = S // NC
NT = T // 128
NDC = D // 128
NHC = HID // 128
EPS = 1e-5
NEG_BIG = -30000.0
SCORE_SCALE = float(1.0 / np.float32(np.sqrt(np.float64(128.0))))
THETA = 500000.0

ENGS = ("sync", "vector", "scalar", "tensor", "gpsimd")


def _cw2pi():
    two_pi = 2.0 * np.pi
    c1 = float(np.float32(np.round(two_pi * 512) / 512.0))
    r = two_pi - c1
    c2 = float(np.float32(np.round(r * 2.0**32) / 2.0**32))
    c3 = float(np.float32(two_pi - c1 - c2))
    return c1, c2, c3


def _halfpi2():
    hp = 0.5 * np.pi
    a = float(np.float32(hp))
    b = float(np.float32(hp - a))
    return a, b


class Sem:
    def __init__(self, es, nc, name):
        self.h = es.enter_context(nc.semaphore(name))
        self.v = 0


class Phase:

    def __init__(self, nc):
        self.nc = nc
        self.ops = {k: [] for k in ENGS}

    def em(self, eng, fn):
        self.ops[eng].append(fn)

    def run(self):
        with self.nc.Block() as block:
            for eng in ENGS:
                fns = self.ops[eng]
                if not fns:
                    continue
                getattr(block, eng)(lambda e, fns=fns: [f(e) for f in fns])


class Build:

    def __init__(self, cfg):
        self.cfg = cfg
        self.nc = bass.Bass("TRN2", target_bir_lowering=False, detect_race_conditions=False)
        self.es = ExitStack()
        self._declare_io()
        self._declare_sems()
        self._declare_persistent()

    def _declare_io(self):
        nc = self.nc
        self.x_in = nc.dram_tensor("x", [T, D], F32, kind="ExternalInput")
        self.wqkvT = nc.dram_tensor("wqkvT", [D, HQKV], F32, kind="ExternalInput")
        self.woT = nc.dram_tensor("woT", [D, D], F32, kind="ExternalInput")
        self.nwrep = nc.dram_tensor("nwrep", [128, D], F32, kind="ExternalInput")
        self.wgT = nc.dram_tensor("wgT", [D, HID], I32, kind="ExternalInput")
        self.wuT = nc.dram_tensor("wuT", [D, HID], I32, kind="ExternalInput")
        self.wdT = nc.dram_tensor("wdT", [HID, D], I32, kind="ExternalInput")
        self.sg_in = nc.dram_tensor("sg", [128, NHC], F32, kind="ExternalInput")
        self.su_in = nc.dram_tensor("su", [128, NHC], F32, kind="ExternalInput")
        self.sd_in = nc.dram_tensor("sd", [128, NDC], F32, kind="ExternalInput")
        self.pos_in = nc.dram_tensor("pos", [1, T], F32, kind="ExternalInput")
        self.mask_in = nc.dram_tensor("mask", [128, 1024], F32, kind="ExternalInput")
        self.xfr_in = nc.dram_tensor("xfr", [128, 1], F32, kind="ExternalInput")
        self.sgn_in = nc.dram_tensor("sgn", [128, 1], F32, kind="ExternalInput")
        self.idf_in = nc.dram_tensor("idf", [128, 128], F32, kind="ExternalInput")
        self.onesf_in = nc.dram_tensor("onesf", [128, 128], F32, kind="ExternalInput")
        self.onec_in = nc.dram_tensor("onec", [128, 1], F32, kind="ExternalInput")
        self.y_out = nc.dram_tensor("y", [T, D], F32, kind="ExternalOutput")
        self.ag_in = nc.dram_tensor("ag_in", [2048, T], BF16)
        self.ag_out = nc.dram_tensor("ag_out", [NC * 2048, T], BF16)
        self.xnhi_d = nc.dram_tensor("xnhi_d", [D, T], BF16)
        self.xnlo_d = nc.dram_tensor("xnlo_d", [D, T], BF16)
        self.qhi_d = nc.dram_tensor("qhi_d", [D, T], BF16)
        self.qlo_d = nc.dram_tensor("qlo_d", [D, T], BF16)
        self.row_d = nc.dram_tensor("row_d", [1, 2048], F32)
        self.dbg = {}
        for name, shape in self.cfg.get("debug_outputs", {}).items():
            self.dbg[name] = self.nc.dram_tensor(name, list(shape), F32, kind="ExternalOutput")

    def _declare_sems(self):
        es, nc = self.es, self.nc
        self.s_ld = Sem(es, nc, "s_ld")
        self.s_ld2 = Sem(es, nc, "s_ld2")
        self.s_ld3 = Sem(es, nc, "s_ld3")
        self.s_v = Sem(es, nc, "s_v")
        self.s_a = Sem(es, nc, "s_a")
        self.s_t = Sem(es, nc, "s_t")
        self.s_g = Sem(es, nc, "s_g")
        self.s_st = Sem(es, nc, "s_st")
        self.all_sems = [self.s_ld, self.s_ld2, self.s_ld3, self.s_v, self.s_a,
                         self.s_t, self.s_g, self.s_st]

    def sbuf(self, name, shape, dtype, stack=None, side=None):
        kw = {} if side is None else {"side": side}
        return (stack or self.es).enter_context(self.nc.sbuf_tensor(name, shape, dtype, **kw))

    def psum(self, name, shape, dtype, stack):
        return stack.enter_context(self.nc.psum_tensor(name, shape, dtype))

    def _declare_persistent(self):
        sb = self.sbuf
        self.dq_cols = sb("dq_cols", [128, NT], F32)
        self.ds2_cols = sb("ds2_cols", [128, NT], F32)
        self.masksb = sb("masksb", [128, 1024], F32)
        self.nwsb = sb("nwsb", [128, D], F32)
        self.sgsb = sb("sgsb", [128, NHC], F32)
        self.susb = sb("susb", [128, NHC], F32)
        self.sdsb = sb("sdsb", [128, NDC], F32)
        self.xfrsb = sb("xfrsb", [128, 1], F32)
        self.sgnsb = sb("sgnsb", [128, 1], F32)
        self.idf = sb("idf_s", [128, 128], F32)
        self.onescol = sb("onescol", [128, 1], F32)
        self.onesrow1 = sb("onesrow1", [1, 128], F32)
        self.ta = sb("ta", [128, D], F32)
        self.tb = sb("tb", [128, D], F32)
        self.tc_ = sb("tc_", [128, D], F32)
        self.ti = sb("ti", [128, D], I32)
        self.td = sb("td", [128, D], F32)
        self.te = sb("te", [128, D], F32)
        self.st24 = sb("st24", [128, 24], F32)
        self.st8 = sb("st8", [128, 8], F32)
        self.c1 = sb("c1", [128, 1], F32)
        self.c2 = sb("c2", [128, 1], F32)
        self.c3 = sb("c3", [128, 1], F32)
        self.c4 = sb("c4", [128, 1], F32)
        self.stage = sb("stage", [128, D], BF16)
        self.stage2 = sb("stage2", [128, D], BF16)
        self.evrow = sb("evrow", [16, 128], F32)
        self.rowa = sb("rowa", [1, 128], F32)
        self.rowb = sb("rowb", [1, 128], F32)
        self.rowc = sb("rowc", [1, 128], F32)
        self.epsc = sb("epsc", [128, 1], F32)
        self.j0 = sb("j0", [128, 1], F32)
        self.j1 = sb("j1", [128, 1], F32)
        Build._sp_j0 = self.j0[:]
        Build._sp_j1 = self.j1[:]

    def declare_pA(self, stack):
        sb = self.sbuf
        self.rT = sb("rT", [128, NDC * T], BF16, stack=stack, side="right")
        self.dqb = sb("dqb", [128, T], F32, stack=stack, side="right")
        self.possb = sb("possb", [128, T], F32, stack=stack, side="right")
        self.ctabQ = sb("ctabQ", [128, T], F32, stack=stack, side="right")
        self.stabQ = sb("stabQ", [128, T], F32, stack=stack, side="right")
        self.ctabK = sb("ctabK", [128, T], F32, stack=stack, side="right")
        self.stabK = sb("stabK", [128, T], F32, stack=stack, side="right")

    @staticmethod
    def sp(e):
        e.tensor_copy(Build._sp_j1, Build._sp_j0)

    def dump_now(self, ph, names):
        for nm in names:
            t = getattr(self, nm)
            shp = list(t.ap().shape)
            d = self.nc.dram_tensor("dbg_" + nm, shp, t.ap().dtype, kind="ExternalOutput")
            self.s_st.v += 16
            def f(e, t=t, d=d):
                e.dma_start(out=d.ap(), in_=t.ap()).then_inc(self.s_st.h, 16)
            ph.em("sync", f)

    def guard_all(self, ph):
        vals = [(s.h, s.v) for s in self.all_sems if s.v > 0]
        for eng in ENGS:
            def f(e, vals=vals):
                for h, v in vals:
                    e.wait_ge(h, v)
            ph.em(eng, f)

    def p0(self, ph):
        b = self
        s_ld = b.s_ld
        loads = [
            (b.masksb[:], b.mask_in[:, :]),
            (b.nwsb[:], b.nwrep[:, :]),
            (b.sgsb[:], b.sg_in[:, :]),
            (b.susb[:], b.su_in[:, :]),
            (b.sdsb[:], b.sd_in[:, :]),
            (b.xfrsb[:], b.xfr_in[:, :]),
            (b.sgnsb[:], b.sgn_in[:, :]),
            (b.idf[:], b.idf_in[:, :]),
            (b.onescol[:], b.onec_in[:, :]),
            (b.onesrow1[:], b.onesf_in[0:1, 0:128]),
        ]
        def f(e):
            for out, in_ in loads:
                e.dma_start(out=out, in_=in_).then_inc(s_ld.h, 16)
        ph.em("sync", f)
        s_ld.v += 16 * len(loads)
        b.s_v.v += 1
        ph.em("vector", lambda e: (e.memset(b.epsc[:], EPS), e.memset(b.j0[:], 0.0),
                                   e.memset(b.j1[:], 0.0).then_inc(b.s_v.h, 1)))

    def p1(self, ph, pst):
        b = self
        nc = self.nc
        rT = b.rT
        b.xs = b.sbuf("xs", [128, D], F32, stack=pst)
        trp = [b.psum(f"p1trp{i}", [128, 512], F32, pst) for i in range(2)]
        s_ld, s_ld2, s_v, s_a, s_t, s_st = b.s_ld, b.s_ld2, b.s_v, b.s_a, b.s_t, b.s_st
        skip = self.cfg.get("p1_skip", set())
        n_tiles = self.cfg.get("p1_tiles", NT)

        for it in range(n_tiles):
            prev_v, prev_a, prev_st, prev_t = s_v.v, s_a.v, s_st.v, s_t.v
            s_ld.v += 16
            ldv = s_ld.v
            ph.em("sync", lambda e, it=it, prev_v=prev_v, prev_a=prev_a, prev_st=prev_st, ldv=ldv: (
                e.wait_ge(s_v.h, prev_v),
                e.wait_ge(s_a.h, prev_a),
                e.wait_ge(s_st.h, prev_st),
                e.dma_start(out=b.xs[:], in_=b.x_in[128 * it:128 * (it + 1), :]).then_inc(s_ld.h, 16),
            ))
            s_st.v += 16
            ph.em("scalar", lambda e, it=it, ldv=ldv: (
                e.wait_ge(s_ld.h, ldv),
                e.dma_start(out=b.y_out[128 * it:128 * (it + 1), :], in_=b.xs[:]).then_inc(s_st.h, 16),
            ))
            s_v.v += 1
            v_stats = s_v.v
            def f_stats(e, ldv=ldv):
                e.wait_ge(s_ld.h, ldv)
                e.tensor_reduce(b.c1[:], b.xs[:], axis=AXX, op=ALU.add)
                e.tensor_mul(b.te[:], b.xs[:], b.xs[:])
                e.tensor_reduce(b.c3[:], b.te[:], axis=AXX, op=ALU.add)
                Build.sp(e)
                e.tensor_scalar_mul(b.c1[:], b.c1[:], 1.0 / D)
                Build.sp(e)
                e.tensor_scalar_mul(b.c3[:], b.c3[:], 1.0 / D)
                e.tensor_mul(b.c2[:], b.c1[:], b.c1[:])
                Build.sp(e)
                e.tensor_sub(b.c3[:], b.c3[:], b.c2[:]).then_inc(s_v.h, 1)
            ph.em("vector", f_stats)
            s_a.v += 1
            a_rsq = s_a.v
            def f_rsq(e, v_stats=v_stats):
                e.wait_ge(s_v.h, v_stats)
                e.activation(out=b.c4[:], in_=b.c3[:], func=ACT.Sqrt, bias=b.epsc[:, 0:1], scale=1.0).then_inc(s_a.h, 1)
            ph.em("scalar", f_rsq)
            s_v.v += 1
            v_nmr = s_v.v
            def f_nmr(e, a_rsq=a_rsq):
                e.wait_ge(s_a.h, a_rsq)
                e.reciprocal(b.c4[:], b.c4[:]).then_inc(s_v.h, 1)
            ph.em("vector", f_nmr)
            s_v.v += 1
            v_xn = s_v.v
            def f_xn(e, prev_t=prev_t):
                e.wait_ge(s_t.h, prev_t)
                e.tensor_scalar(b.te[:], b.xs[:], b.c1[:, 0:1], None, op0=ALU.subtract)
                e.tensor_scalar(b.ta[:], b.te[:], b.c4[:, 0:1], None, op0=ALU.mult).then_inc(s_v.h, 1)
            ph.em("vector", f_xn)
            a_xn = v_xn
            s_v.v += 1
            v_qs = s_v.v
            def f_qs(e, it=it):
                e.reduce_max(b.c1[:], b.ta[:], axis=AXX, apply_absolute_value=True)
                Build.sp(e)
                e.tensor_scalar_max(b.c1[:], b.c1[:], 1e-5)
                Build.sp(e)
                e.reciprocal(b.c2[:], b.c1[:])
                Build.sp(e)
                e.tensor_scalar_mul(b.c2[:], b.c2[:], 127.0)
                Build.sp(e)
                e.reciprocal(b.c3[:], b.c2[:])
                Build.sp(e)
                e.tensor_copy(b.dq_cols[:, it:it + 1], b.c3[:]).then_inc(s_v.h, 1)
            ph.em("vector", f_qs)
            s_v.v += 1
            def f_yq(e):
                e.tensor_scalar(b.tb[:], b.ta[:], b.c2[:, 0:1], None, op0=ALU.mult).then_inc(s_v.h, 1)
            ph.em("vector", f_yq)
            s_v.v += 1
            v_rr = s_v.v
            def f_rr(e, prev_t=prev_t):
                e.wait_ge(s_t.h, prev_t)
                e.tensor_copy(b.ti[:], b.tb[:])
                e.tensor_copy(b.tc_[:], b.ti[:])
                e.tensor_sub(b.td[:], b.tb[:], b.tc_[:])
                e.tensor_single_scalar(b.te[:], b.td[:], 0.5, op=ALU.is_ge)
                e.tensor_add(b.tc_[:], b.tc_[:], b.te[:])
                e.tensor_single_scalar(b.te[:], b.td[:], -0.5, op=ALU.is_le)
                e.tensor_sub(b.td[:], b.tc_[:], b.te[:]).then_inc(s_v.h, 1)
            ph.em("vector", f_rr)
            for src_i in ([] if "tr" in skip else [i_ for i_ in range(2) if f"tr{i_}" not in skip]):
                for k in range(4):
                    bank = trp[(it * 8 + src_i * 4 + k) % 2]
                    src_sem, src_val = (s_v, v_rr) if src_i == 0 else (s_v, a_xn)
                    s_t.v += 1
                    t_tr = s_t.v
                    need_a = s_a.v
                    need_v = s_v.v
                    def f_tr(e, k=k, bank=bank, src_i=src_i, src_sem=src_sem,
                             src_val=src_val, need_a=need_a, need_v=need_v):
                        e.wait_ge(src_sem.h, src_val)
                        e.wait_ge(s_a.h, need_a)
                        e.wait_ge(s_v.h, need_v)
                        src = b.td if src_i == 0 else b.ta
                        for j in range(4):
                            dc = 4 * k + j
                            li = e.transpose(bank[:, 128 * j:128 * (j + 1)],
                                             src[:, 128 * dc:128 * (dc + 1)], b.idf[:])
                        li.then_inc(s_t.h, 1)
                    ph.em("tensor", f_tr)
                    if src_i == 0:
                        s_a.v += 1
                        def f_ev(e, k=k, bank=bank, it=it, t_tr=t_tr):
                            e.wait_ge(s_t.h, t_tr)
                            out = bass.AP(rT, (4 * k) * T + 128 * it,
                                          [[NDC * T, 128], [T, 4], [1, 128]])
                            e.activation(out=out, in_=bank[:], func=ACT.Copy,
                                         bias=0.0, scale=1.0).then_inc(s_a.h, 1)
                        ph.em("scalar", f_ev)
                    else:
                        s_a.v += 1
                        a_hi = s_a.v
                        def f_hi(e, k=k, bank=bank, t_tr=t_tr, prev_st=prev_st):
                            e.wait_ge(s_t.h, t_tr)
                            e.wait_ge(s_st.h, prev_st)
                            e.activation(out=b.stage[:, 512 * k:512 * (k + 1)], in_=bank[:],
                                         func=ACT.Copy, bias=0.0, scale=1.0).then_inc(s_a.h, 1)
                        ph.em("scalar", f_hi)
                        s_v.v += 1
                        def f_lo(e, k=k, bank=bank, a_hi=a_hi):
                            e.wait_ge(s_a.h, a_hi)
                            e.tensor_sub(b.stage2[:, 512 * k:512 * (k + 1)], bank[:],
                                         b.stage[:, 512 * k:512 * (k + 1)]).then_inc(s_v.h, 1)
                        ph.em("vector", f_lo)
            if "spill" in skip or "tr" in skip or "tr1" in skip:
                continue
            s_st.v += 32
            need_v = s_v.v
            need_a = s_a.v
            def f_spill(e, it=it, need_v=need_v, need_a=need_a):
                e.wait_ge(s_v.h, need_v)
                e.wait_ge(s_a.h, need_a)
                for src, dst in ((b.stage, b.xnhi_d), (b.stage2, b.xnlo_d)):
                    out = bass.AP(dst, 128 * it, [[T, 128], [128 * T, 16], [1, 128]])
                    in_ = bass.AP(src, 0, [[D, 128], [128, 16], [1, 128]])
                    e.dma_start(out=out, in_=in_).then_inc(s_st.h, 16)
            ph.em("sync", f_spill)

    def p15(self, ph, pst):
        b = self
        s_v, s_a, s_t, s_ld2, s_ld = b.s_v, b.s_a, b.s_t, b.s_ld2, b.s_ld
        ptr = b.psum(f"p15tr", [128, 128], F32, pst)
        s_ld.v += 16
        ph.em("sync", lambda e: e.dma_start(
            out=b.possb[:], in_=bass.AP(b.pos_in, 0, [[0, 128], [1, T]])).then_inc(s_ld.h, 16))

        s_t.v += 1
        t_tr = s_t.v
        need_v = s_v.v
        ph.em("tensor", lambda e, t=t_tr, nv=need_v: (
            e.wait_ge(s_v.h, nv),
            e.transpose(ptr[0:NT, :], b.dq_cols[:], b.idf[:]).then_inc(s_t.h, 1),
        ))
        s_a.v += 1
        a_ev = s_a.v
        ph.em("scalar", lambda e, t=t_tr: (
            e.wait_ge(s_t.h, t),
            e.activation(out=b.evrow[0:NT, :], in_=ptr[0:NT, :], func=ACT.Copy,
                         bias=0.0, scale=1.0).then_inc(s_a.h, 1),
        ))
        s_ld2.v += 32
        ld2 = s_ld2.v
        ph.em("scalar", lambda e, a=a_ev, ld2=ld2: (
            e.wait_ge(s_a.h, a),
            e.dma_start(out=bass.AP(b.row_d, 0, [[128, NT], [1, 128]]),
                        in_=bass.AP(b.evrow, 0, [[128, NT], [1, 128]])).then_inc(s_ld2.h, 16),
            e.wait_ge(s_ld2.h, ld2 - 16),
            e.dma_start(out=b.dqb[:], in_=bass.AP(b.row_d, 0, [[0, 128], [1, T]])).then_inc(s_ld2.h, 16),
        ))

        c1, c2, c3 = _cw2pi()
        hp1, hp2 = _halfpi2()
        inv2pi = float(np.float32(1.0 / (2.0 * np.pi)))
        pi_clamp = 3.1415925
        prev_a0 = s_a.v
        s_v.v += 1
        v_ms = s_v.v
        def f_ms(e, prev_a0=prev_a0):
            e.wait_ge(s_a.h, prev_a0)
            e.memset(b.c1[:], THETA).then_inc(s_v.h, 1)
        ph.em("vector", f_ms)
        s_a.v += 1
        a_ln = s_a.v
        def f_ln(e, v_ms=v_ms):
            e.wait_ge(s_v.h, v_ms)
            e.activation(out=b.c2[:], in_=b.c1[:], func=ACT.Ln, bias=0.0, scale=1.0).then_inc(s_a.h, 1)
        ph.em("scalar", f_ln)
        s_v.v += 1
        v_fr = s_v.v
        def f_fr(e, a_ln=a_ln):
            e.wait_ge(s_a.h, a_ln)
            e.tensor_scalar(b.c3[:], b.xfrsb[:], b.c2[:, 0:1], None, op0=ALU.mult).then_inc(s_v.h, 1)
        ph.em("vector", f_fr)
        s_a.v += 1
        a_fr = s_a.v
        ph.em("scalar", lambda e, v=v_fr: (
            e.wait_ge(s_v.h, v),
            e.activation(out=b.c4[:], in_=b.c3[:], func=ACT.Exp, bias=0.0, scale=1.0).then_inc(s_a.h, 1),
        ))
        ld_now = b.s_ld.v
        for ipass in range(2):
            prev_a2 = s_a.v
            s_v.v += 1
            v_ang = s_v.v
            def f_ang(e, ipass=ipass, prev_a2=prev_a2, ld_now=ld_now):
                e.wait_ge(s_a.h, prev_a2)
                e.wait_ge(b.s_ld.h, ld_now)
                ap_t = b.possb[:, 0:T]
                e.tensor_scalar(b.ta[:, 0:T], ap_t, b.c4[:, 0:1], None, op0=ALU.mult)
                if ipass == 1:
                    e.tensor_scalar(b.ta[:, 0:T], b.ta[:, 0:T], hp1, hp2, op0=ALU.add, op1=ALU.add)
                e.tensor_scalar(b.tb[:, 0:T], b.ta[:, 0:T], inv2pi, None, op0=ALU.mult)
                e.tensor_copy(b.ti[:, 0:T], b.tb[:, 0:T])
                e.tensor_copy(b.tb[:, 0:T], b.ti[:, 0:T])
                e.scalar_tensor_tensor(b.tc_[:, 0:T], b.tb[:, 0:T], -c1, b.ta[:, 0:T],
                                       op0=ALU.mult, op1=ALU.add)
                e.scalar_tensor_tensor(b.tc_[:, 0:T], b.tb[:, 0:T], -c2, b.tc_[:, 0:T],
                                       op0=ALU.mult, op1=ALU.add)
                e.scalar_tensor_tensor(b.tc_[:, 0:T], b.tb[:, 0:T], -c3, b.tc_[:, 0:T],
                                       op0=ALU.mult, op1=ALU.add)
                e.tensor_single_scalar(b.td[:, 0:T], b.tc_[:, 0:T], float(np.float32(np.pi)), op=ALU.is_gt)
                e.tensor_single_scalar(b.te[:, 0:T], b.tc_[:, 0:T], float(np.float32(-np.pi)), op=ALU.is_lt)
                e.tensor_sub(b.td[:, 0:T], b.te[:, 0:T], b.td[:, 0:T])
                e.scalar_tensor_tensor(b.tc_[:, 0:T], b.td[:, 0:T], float(np.float32(2 * np.pi)),
                                       b.tc_[:, 0:T], op0=ALU.mult, op1=ALU.add)
                e.tensor_scalar_min(b.tc_[:, 0:T], b.tc_[:, 0:T], pi_clamp)
                e.tensor_scalar_max(b.tc_[:, 0:T], b.tc_[:, 0:T], -pi_clamp).then_inc(s_v.h, 1)
            ph.em("vector", f_ang)
            s_a.v += 1
            a_sin = s_a.v
            tgt = [b.stabK, b.ctabK][ipass]
            ph.em("scalar", lambda e, v=v_ang, tgt=tgt: (
                e.wait_ge(s_v.h, v),
                e.activation(out=tgt[:], in_=b.tc_[:, 0:T], func=ACT.Sin, bias=0.0, scale=1.0).then_inc(s_a.h, 1),
            ))
        s_v.v += 1
        def f_tab(e, a=s_a.v, ld2=s_ld2.v):
            e.wait_ge(s_a.h, a)
            e.wait_ge(s_ld2.h, ld2)
            e.tensor_scalar(b.stabK[:], b.stabK[:], b.sgnsb[:, 0:1], None, op0=ALU.mult)
            e.tensor_mul(b.stabK[:], b.stabK[:], b.dqb[:])
            e.tensor_mul(b.ctabK[:], b.ctabK[:], b.dqb[:])
            e.tensor_scalar_mul(b.stabQ[:], b.stabK[:], SCORE_SCALE)
            e.tensor_scalar_mul(b.ctabQ[:], b.ctabK[:], SCORE_SCALE).then_inc(s_v.h, 1)
        ph.em("vector", f_tab)

    def p2(self, ph, pst):
        b = self
        s_ld, s_ld2, s_v, s_a, s_t = b.s_ld, b.s_ld2, b.s_v, b.s_a, b.s_t
        ps_qkv = [b.psum(f"p2qkv{i}", [128, 1024], F32, pst) for i in range(2)]
        trpv = [b.psum(f"p2trv{i}", [128, 512], F32, pst) for i in range(2)]
        wtile = [b.sbuf(f"p2wt{i}", [128, D], F32, stack=pst, side="right") for i in range(2)]
        whi = [b.sbuf(f"p2whi{i}", [128, D], BF16, stack=pst, side="right") for i in range(2)]
        wlo = [b.sbuf(f"p2wlo{i}", [128, D], BF16, stack=pst, side="right") for i in range(2)]
        qf = b.sbuf("p2qf", [128, T], F32, stack=pst, side="right")
        qsw = b.sbuf("p2qsw", [128, T], F32, stack=pst, side="right")
        qr = b.sbuf("p2qr", [128, T], F32, stack=pst, side="right")
        kst_hi = b.sbuf("p2ksth", [128, T], BF16, stack=pst, side="right")
        kst_lo = b.sbuf("p2kstl", [128, T], BF16, stack=pst, side="right")
        vst_hi = b.sbuf("p2vsth", [128, T], BF16, stack=pst, side="right")
        vst_lo = b.sbuf("p2vstl", [128, T], BF16, stack=pst, side="right")
        swapmask = [i ^ 1 for i in range(32)]

        conv_v = {}
        mm_t = {}
        epi_a = {}
        epi_v = {}
        ld_v = {}

        for hc in range(24):
            buf = hc % 2
            s_ld.v += 16
            ld_v[hc] = s_ld.v
            wait_conv = conv_v.get(hc - 2, 0)
            def f_ld(e, hc=hc, buf=buf, wait_conv=wait_conv):
                e.wait_ge(s_v.h, wait_conv)
                in_ = bass.AP(b.wqkvT, 128 * hc, [[HQKV, 128], [128 * HQKV, NDC], [1, 128]])
                e.dma_start(out=wtile[buf][:], in_=in_).then_inc(s_ld.h, 16)
            ph.em("sync", f_ld)
            wait_mm = mm_t.get(hc - 2, 0)
            s_v.v += 2
            conv_v[hc] = s_v.v
            def f_cv(e, hc=hc, buf=buf, wait_mm=wait_mm, ldv=ld_v[hc]):
                e.wait_ge(s_ld.h, ldv)
                e.wait_ge(s_t.h, wait_mm)
                e.tensor_copy(whi[buf][:], wtile[buf][:]).then_inc(s_v.h, 1)
                e.tensor_sub(wlo[buf][:], wtile[buf][:], whi[buf][:]).then_inc(s_v.h, 1)
            ph.em("vector", f_cv)
            wait_ea = epi_a.get(hc - 2, 0)
            wait_ev = epi_v.get(hc - 2, 0)
            s_t.v += 1
            mm_t[hc] = s_t.v
            def f_mm(e, hc=hc, buf=buf, cv=conv_v[hc], wait_ea=wait_ea, wait_ev=wait_ev):
                e.wait_ge(s_v.h, cv)
                e.wait_ge(s_a.h, wait_ea)
                e.wait_ge(s_v.h, wait_ev)
                for dc in range(NDC):
                    for th in range(2):
                        for w in (whi[buf], wlo[buf]):
                            li = e.matmul(ps_qkv[buf][:, 512 * th:512 * (th + 1)],
                                          w[:, 128 * dc:128 * (dc + 1)],
                                          b.rT[:, dc * T + 512 * th: dc * T + 512 * (th + 1)],
                                          start=(dc == 0 and w is whi[buf]),
                                          stop=(dc == NDC - 1 and w is wlo[buf]))
                li.then_inc(s_t.h, 1)
            ph.em("tensor", f_mm)
            if hc < 16:
                s_a.v += 1
                a_ev = s_a.v
                def f_ev(e, buf=buf, mmv=mm_t[hc], pv=epi_v.get(hc - 1, 0)):
                    e.wait_ge(s_t.h, mmv)
                    e.wait_ge(s_v.h, pv)
                    e.activation(out=qf[:], in_=ps_qkv[buf][:], func=ACT.Copy,
                                 bias=0.0, scale=1.0).then_inc(s_a.h, 1)
                ph.em("scalar", f_ev)
                s_v.v += 4
                v_rope = s_v.v
                def f_rope(e, a_ev=a_ev):
                    e.wait_ge(s_a.h, a_ev)
                    e.stream_shuffle(qsw[:], qf[:], swapmask)
                    e.tensor_mul(qr[:], qf[:], b.ctabQ[:])
                    e.tensor_mul(qsw[:], qsw[:], b.stabQ[:])
                    e.tensor_add(qr[:], qr[:], qsw[:]).then_inc(s_v.h, 4)
                ph.em("vector", f_rope)
                s_a.v += 1
                a_hi = s_a.v
                def f_hi(e, v_rope=v_rope, pstore=b.s_ld2.v):
                    e.wait_ge(s_v.h, v_rope)
                    e.wait_ge(s_ld2.h, pstore)
                    e.activation(out=b.stage[:, 0:T], in_=qr[:],
                                 func=ACT.Copy, bias=0.0, scale=1.0).then_inc(s_a.h, 1)
                ph.em("scalar", f_hi)
                s_v.v += 1
                v_lo = s_v.v
                def f_lo(e, a_hi=a_hi):
                    e.wait_ge(s_a.h, a_hi)
                    e.tensor_sub(b.stage2[:, 0:T], qr[:], b.stage[:, 0:T]).then_inc(s_v.h, 1)
                ph.em("vector", f_lo)
                s_ld2.v += 32
                def f_st(e, hc=hc, v_lo=v_lo, a_hi=a_hi):
                    e.wait_ge(s_v.h, v_lo)
                    e.wait_ge(s_a.h, a_hi)
                    e.dma_start(out=b.qhi_d[128 * hc:128 * (hc + 1), :],
                                in_=b.stage[:, 0:T]).then_inc(s_ld2.h, 16)
                    e.dma_start(out=b.qlo_d[128 * hc:128 * (hc + 1), :],
                                in_=b.stage2[:, 0:T]).then_inc(s_ld2.h, 16)
                ph.em("scalar", f_st)
                epi_a[hc] = a_hi
                epi_v[hc] = v_lo
            elif hc < 20:
                g = hc - 16
                s_a.v += 1
                a_ev = s_a.v
                def f_ev(e, buf=buf, mmv=mm_t[hc], pv=epi_v.get(hc - 1, 0), pst_=0):
                    e.wait_ge(s_t.h, mmv)
                    e.wait_ge(s_v.h, pv)
                    e.activation(out=qf[:], in_=ps_qkv[buf][:], func=ACT.Copy,
                                 bias=0.0, scale=1.0).then_inc(s_a.h, 1)
                ph.em("scalar", f_ev)
                s_v.v += 4
                v_rope = s_v.v
                def f_rope(e, a_ev=a_ev, pstore=b.s_ld2.v):
                    e.wait_ge(s_a.h, a_ev)
                    e.wait_ge(s_ld2.h, pstore)
                    e.stream_shuffle(qsw[:], qf[:], swapmask)
                    e.tensor_mul(qr[:], qf[:], b.ctabK[:])
                    e.tensor_mul(qsw[:], qsw[:], b.stabK[:])
                    e.tensor_add(qr[:], qr[:], qsw[:]).then_inc(s_v.h, 4)
                ph.em("vector", f_rope)
                s_a.v += 1
                a_hi = s_a.v
                def f_hi(e, v_rope=v_rope):
                    e.wait_ge(s_v.h, v_rope)
                    e.activation(out=kst_hi[:], in_=qr[:], func=ACT.Copy,
                                 bias=0.0, scale=1.0).then_inc(s_a.h, 1)
                ph.em("scalar", f_hi)
                s_v.v += 1
                v_lo = s_v.v
                def f_lo(e, a_hi=a_hi):
                    e.wait_ge(s_a.h, a_hi)
                    e.tensor_sub(kst_lo[:], qr[:], kst_hi[:]).then_inc(s_v.h, 1)
                ph.em("vector", f_lo)
                s_ld2.v += 32
                def f_st(e, g=g, v_lo=v_lo, a_hi=a_hi):
                    e.wait_ge(s_v.h, v_lo)
                    e.wait_ge(s_a.h, a_hi)
                    e.dma_start(out=b.ag_in[128 * g:128 * (g + 1), :], in_=kst_hi[:]).then_inc(s_ld2.h, 16)
                    e.dma_start(out=b.ag_in[512 + 128 * g:512 + 128 * (g + 1), :], in_=kst_lo[:]).then_inc(s_ld2.h, 16)
                ph.em("scalar", f_st)
                epi_a[hc] = a_hi
                epi_v[hc] = v_lo
            else:
                g = hc - 20
                s_v.v += 1
                v_dq = s_v.v
                def f_dq(e, buf=buf, mmv=mm_t[hc], pv=epi_v.get(hc - 1, 0)):
                    e.wait_ge(s_t.h, mmv)
                    e.wait_ge(s_v.h, pv)
                    e.tensor_mul(qf[:], ps_qkv[buf][:], b.dqb[:]).then_inc(s_v.h, 1)
                ph.em("vector", f_dq)
                tr_t = []
                for kb in range(2):
                    s_t.v += 1
                    tr_t.append(s_t.v)
                    def f_tr(e, kb=kb, v_dq=v_dq, pa=epi_a.get(hc - 1, 0), pvv=epi_v.get(hc - 1, 0)):
                        e.wait_ge(s_v.h, v_dq)
                        e.wait_ge(s_a.h, pa)
                        e.wait_ge(s_v.h, pvv)
                        for j in range(4):
                            ts = 4 * kb + j
                            li = e.transpose(trpv[kb][:, 128 * j:128 * (j + 1)],
                                             qf[:, 128 * ts:128 * (ts + 1)], b.idf[:])
                        li.then_inc(s_t.h, 1)
                    ph.em("tensor", f_tr)
                s_a.v += 2
                a_hi = s_a.v
                def f_hi(e, tr_t=tuple(tr_t), pst2=b.s_ld2.v):
                    e.wait_ge(s_ld2.h, pst2)
                    for kb in range(2):
                        e.wait_ge(s_t.h, tr_t[kb])
                        e.activation(out=vst_hi[:, 512 * kb:512 * (kb + 1)], in_=trpv[kb][:],
                                     func=ACT.Copy, bias=0.0, scale=1.0).then_inc(s_a.h, 1)
                ph.em("scalar", f_hi)
                s_v.v += 2
                v_lo = s_v.v
                def f_lo(e, a_hi=a_hi):
                    e.wait_ge(s_a.h, a_hi)
                    for kb in range(2):
                        e.tensor_sub(vst_lo[:, 512 * kb:512 * (kb + 1)], trpv[kb][:],
                                     vst_hi[:, 512 * kb:512 * (kb + 1)]).then_inc(s_v.h, 1)
                ph.em("vector", f_lo)
                s_ld2.v += 32
                def f_st(e, g=g, v_lo=v_lo, a_hi=a_hi):
                    e.wait_ge(s_v.h, v_lo)
                    e.wait_ge(s_a.h, a_hi)
                    for src, base in ((vst_hi, 1024 * T), (vst_lo, 1536 * T)):
                        out = bass.AP(b.ag_in, base + 128 * g,
                                      [[512, 128], [128 * 512, 8], [1, 128]])
                        in_ = bass.AP(src, 0, [[T, 128], [128, 8], [1, 128]])
                        e.dma_start(out=out, in_=in_).then_inc(s_ld2.h, 16)
                ph.em("scalar", f_st)
                epi_a[hc] = a_hi
                epi_v[hc] = v_lo

    def p3(self, ph):
        b = self
        need = [(b.s_ld2.h, b.s_ld2.v)]
        b.s_g.v += 1
        gv = b.s_g.v
        def f_ag(e, need=tuple(need)):
            for h, v in need:
                e.wait_ge(h, v)
            e.collective_compute(
                "AllGather", ALU.bypass,
                replica_groups=[list(range(NC))],
                ins=[b.ag_in[:, :]],
                outs=[b.ag_out[:, :]],
            ).then_inc(b.s_g.h, 1)
        ph.em("gpsimd", f_ag)
        for eng in ("sync", "vector", "scalar", "tensor"):
            ph.em(eng, lambda e, gv=gv: e.wait_ge(b.s_g.h, gv))

    def declare_pB(self, stack):
        sb = self.sbuf
        self.r2T = sb("r2T", [128, NDC * T], BF16, stack=stack, side="right")

    def declare_p4locals(self, stack):
        sb = self.sbuf
        self.qtlhi = [sb("qtlhi0", [128, D], BF16, stack=stack, side="right")] * 2
        self.qtllo = [sb("qtllo0", [128, D], BF16, stack=stack, side="right")] * 2
        self.khi = [sb(f"khi{i}", [128, NKV * 1024], BF16, stack=stack, side="right") for i in range(2)]
        self.klo = [sb(f"klo{i}", [128, NKV * 1024], BF16, stack=stack, side="right") for i in range(2)]
        self.vhi = [sb(f"vhi{i}", [128, NC * 512], BF16, stack=stack, side="right") for i in range(2)]
        self.vlo = [sb(f"vlo{i}", [128, NC * 512], BF16, stack=stack, side="right") for i in range(2)]
        self.phiT = sb("phiT", [128, 8 * 512], BF16, stack=stack, side="right")
        self.ploT = sb("ploT", [128, 8 * 512], BF16, stack=stack, side="right")
        self.fTrow = sb("fTrow", [1, 512], F32, stack=stack, side="right")
        self.fb_sb = sb("fb_sb", [128, 512], F32, stack=stack, side="right")
        self.q2b = sb("q2b", [128, 128], F32, stack=stack, side="right")
        self.colsum = sb("colsum", [128, 128], F32, stack=stack, side="right")
        self.O_sb = sb("O_sb", [128, NH * 128], F32, stack=stack, side="right")
        self.m_st = sb("m_st", [128, NH], F32, stack=stack, side="right")
        self.l_st = sb("l_st", [128, NH], F32, stack=stack, side="right")
        self.f_st = sb("f_st", [128, NH], F32, stack=stack, side="right")
        self.cs_st = sb("cs_st", [128, NH], F32, stack=stack, side="right")
        self.nmn_st = sb("nmn_st", [128, NH], F32, stack=stack, side="right")
        self.tm1 = sb("tm1", [128, 1], F32, stack=stack, side="right")

    def p4(self, ph, pst):
        b = self
        self.declare_p4locals(pst)
        s_ld, s_ld2, s_v, s_a, s_t = b.s_ld, b.s_ld2, b.s_v, b.s_a, b.s_t
        ps_sc = [b.psum(f"p4sc{i}", [128, 1024], F32, pst) for i in range(2)]
        trp = [b.psum(f"p4tr{i}", [128, 512], F32, pst) for i in range(2)]
        ps_pv = [b.psum(f"p4pv{i}", [128, 512], F32, pst) for i in range(2)]
        p32 = b.ta

        n_qt = self.cfg.get("n_qt", NT)
        hseq = 0
        exp_a = {}
        msk_v = {}
        pv_done_v = {}
        tr_a = {}
        tr_v = {}

        s_v.v += 3
        def f_init(e):
            e.memset(b.O_sb[:], 0.0)
            e.memset(b.l_st[:], 0.0)
            e.memset(b.m_st[:], NEG_BIG).then_inc(s_v.h, 3)
        ph.em("vector", f_init)
        s_ld2.v += 32
        def f_q0(e):
            for src, dst in ((b.qhi_d, b.qtlhi[0]), (b.qlo_d, b.qtllo[0])):
                in_ = bass.AP(src, 0, [[T, 128], [128 * T, NDC], [1, 128]])
                e.dma_start(out=dst[:], in_=in_).then_inc(s_ld2.h, 16)
        ph.em("scalar", f_q0)

        vis = 0
        sc_t = {}
        trp_t = {}
        pv_t = {}
        kv_free = {}
        kv_ld = {}
        pend = []

        gq_pv_t = {}
        gq_oadd = {}
        def emit_head_pe_tail(hp):
            info = pend.pop(0)
            assert info[0] == hp
            g_, vb_, visidx, hh = info[1], info[2], info[3], info[4]
            p_ = hh % 4
            gseq = hp // 4
            t_trs = []
            for kb in range(2):
                s_t.v += 1
                t_trs.append(s_t.v)
                def f_tr(e, kb=kb, hp=hp, ea=exp_a[hp], pa=tr_a.get(hp - 1, 0),
                         pv_=tr_v.get(hp - 1, 0)):
                    e.wait_ge(s_a.h, ea)
                    e.wait_ge(s_a.h, pa)
                    e.wait_ge(s_v.h, pv_)
                    p32v = b.ta[:, 1024 * (hp % 2):1024 * (hp % 2 + 1)]
                    for j in range(4):
                        kc = 4 * kb + j
                        li = e.transpose(trp[kb][:, 128 * j:128 * (j + 1)],
                                         p32v[:, 128 * kc:128 * (kc + 1)], b.idf[:])
                    li.then_inc(s_t.h, 1)
                ph.em("tensor", f_tr)
            trp_t[hp] = tuple(t_trs)
            s_a.v += 2
            a_phi = s_a.v
            def f_phi(e, t_trs=tuple(t_trs), p_=p_, ppv=gq_pv_t.get(gseq - 1, 0)):
                e.wait_ge(s_t.h, ppv)
                for kb in range(2):
                    e.wait_ge(s_t.h, t_trs[kb])
                    out = bass.AP(b.phiT, (4 * kb) * 512 + 128 * p_,
                                  [[8 * 512, 128], [512, 4], [1, 128]])
                    e.activation(out=out, in_=trp[kb][:], func=ACT.Copy,
                                 bias=0.0, scale=1.0).then_inc(s_a.h, 1)
            ph.em("scalar", f_phi)
            tr_a[hp] = s_a.v
            s_v.v += 2
            def f_plo(e, a_phi=a_phi, p_=p_):
                e.wait_ge(s_a.h, a_phi)
                for kb in range(2):
                    out = bass.AP(b.ploT, (4 * kb) * 512 + 128 * p_,
                                  [[8 * 512, 128], [512, 4], [1, 128]])
                    in1 = bass.AP(b.phiT, (4 * kb) * 512 + 128 * p_,
                                  [[8 * 512, 128], [512, 4], [1, 128]])
                    e.tensor_sub(out, trp[kb][:], in1).then_inc(s_v.h, 1)
            ph.em("vector", f_plo)
            tr_v[hp] = s_v.v
            if p_ != 3:
                return
            pvb = ps_pv[gseq % 2]
            s_t.v += 1
            gq_pv_t[gseq] = s_t.v
            def f_pv(e, g=g_, pvb=pvb, vb=vb_, kvld=kv_ld[visidx],
                     ta_=tr_a[hp], tv_=tr_v[hp], po=gq_oadd.get(gseq - 2, 0)):
                e.wait_ge(s_a.h, ta_)
                e.wait_ge(s_v.h, tv_)
                e.wait_ge(s_v.h, po)
                e.wait_ge(s_ld.h, kvld)
                for kc in range(NC):
                    vh = b.vhi[vb][:, 512 * kc + 128 * g:512 * kc + 128 * (g + 1)]
                    vl = b.vlo[vb][:, 512 * kc + 128 * g:512 * kc + 128 * (g + 1)]
                    pT = b.phiT[:, 512 * kc:512 * (kc + 1)]
                    pTl = b.ploT[:, 512 * kc:512 * (kc + 1)]
                    e.matmul(pvb[:], vh, pT, start=(kc == 0), stop=False)
                    e.matmul(pvb[:], vl, pT, start=False, stop=False)
                    li = e.matmul(pvb[:], vh, pTl, start=False, stop=(kc == NC - 1))
                li.then_inc(s_t.h, 1)
            ph.em("tensor", f_pv)
            s_a.v += 1
            a_pe = s_a.v
            def f_pe(e, pvb=pvb, pvt=gq_pv_t[gseq], po=gq_oadd.get(gseq - 1, 0)):
                e.wait_ge(s_t.h, pvt)
                e.wait_ge(s_v.h, po)
                e.activation(out=b.fb_sb[:], in_=pvb[:], func=ACT.Copy,
                             bias=0.0, scale=1.0).then_inc(s_a.h, 1)
            ph.em("scalar", f_pe)
            s_t.v += 1
            t_pvt = s_t.v
            def f_pvt(e, pvb=pvb, a_pe=a_pe):
                e.wait_ge(s_a.h, a_pe)
                for p2 in range(4):
                    li = e.transpose(pvb[:, 128 * p2:128 * (p2 + 1)],
                                     b.fb_sb[:, 128 * p2:128 * (p2 + 1)], b.idf[:])
                li.then_inc(s_t.h, 1)
            ph.em("tensor", f_pvt)
            s_v.v += 4
            def f_oa(e, g=g_, pvb=pvb, t_pvt=t_pvt):
                e.wait_ge(s_t.h, t_pvt)
                for p2 in range(4):
                    hh2 = 4 * g + p2
                    li = e.scalar_tensor_tensor(b.O_sb[:, 128 * hh2:128 * (hh2 + 1)],
                                                b.O_sb[:, 128 * hh2:128 * (hh2 + 1)],
                                                b.f_st[:, hh2:hh2 + 1],
                                                pvb[:, 128 * p2:128 * (p2 + 1)],
                                                op0=ALU.mult, op1=ALU.add)
                li.then_inc(s_v.h, 4)
            ph.em("vector", f_oa)
            gq_oadd[gseq] = s_v.v
            if hh == NH - 1:
                kv_free[visidx] = s_t.v

        for I in range(n_qt):
            qb = I % 2
            q_ld2 = s_ld2.v
            for J in range(I + 1):
                vb = vis % 2
                s_ld.v += 16 * 10
                kv_ld[vis] = s_ld.v
                def f_kv(e, J=J, vb=vb, free_t=kv_free.get(vis - 2, 0)):
                    e.wait_ge(s_t.h, free_t)
                    for g in range(NKV):
                        e.dma_start(out=b.khi[vb][:, 1024 * g:1024 * (g + 1)],
                                    in_=bass.AP(b.ag_out, 128 * g * T + 128 * J,
                                                [[T, 128], [2048 * T, NC], [1, 128]])
                                    ).then_inc(s_ld.h, 16)
                    for g in range(NKV):
                        e.dma_start(out=b.klo[vb][:, 1024 * g:1024 * (g + 1)],
                                    in_=bass.AP(b.ag_out, (512 + 128 * g) * T + 128 * J,
                                                [[T, 128], [2048 * T, NC], [1, 128]])
                                    ).then_inc(s_ld.h, 16)
                    e.dma_start(out=b.vhi[vb][:],
                                in_=bass.AP(b.ag_out, 1024 * T + 128 * J * 512, [[512, 128], [2048 * T, NC], [1, 512]])
                                ).then_inc(s_ld.h, 16)
                    e.dma_start(out=b.vlo[vb][:],
                                in_=bass.AP(b.ag_out, 1536 * T + 128 * J * 512, [[512, 128], [2048 * T, NC], [1, 512]])
                                ).then_inc(s_ld.h, 16)
                ph.em("sync", f_kv)
                for h in range(NH):
                    g = h // 4
                    bank = ps_sc[hseq % 2]
                    s_t.v += 1
                    sc_t[hseq] = s_t.v
                    def f_sc(e, h=h, g=g, bank=bank, kvld=kv_ld[vis], qb=qb, q_ld2=q_ld2,
                             pe=exp_a.get(hseq - 2, 0), pm=msk_v.get(hseq - 2, 0), vb=vb):
                        e.wait_ge(s_ld.h, kvld - 32)
                        e.wait_ge(s_ld2.h, q_ld2)
                        e.wait_ge(s_a.h, pe)
                        e.wait_ge(s_v.h, pm)
                        for kh in range(2):
                            out = bank[:, 512 * kh:512 * (kh + 1)]
                            rhs = b.khi[vb][:, 1024 * g + 512 * kh:1024 * g + 512 * (kh + 1)]
                            rhs_lo = b.klo[vb][:, 1024 * g + 512 * kh:1024 * g + 512 * (kh + 1)]
                            lh = b.qtlhi[qb][:, 128 * h:128 * (h + 1)]
                            ll = b.qtllo[qb][:, 128 * h:128 * (h + 1)]
                            e.matmul(out, lh, rhs, start=True, stop=False)
                            e.matmul(out, lh, rhs_lo, start=False, stop=False)
                            li = e.matmul(out, ll, rhs, start=False, stop=True)
                        li.then_inc(s_t.h, 1)
                    ph.em("tensor", f_sc)
                    if pend:
                        emit_head_pe_tail(hseq - 1)
                    nv = 5 if J == I else 4
                    s_v.v += nv
                    def f_mx(e, h=h, bank=bank, sct=sc_t[hseq], diag=(J == I), nv=nv):
                        e.wait_ge(s_t.h, sct)
                        if diag:
                            e.tensor_add(bank[:], bank[:], b.masksb[:])
                        e.reduce_max(b.tm1[:], bank[:], axis=AXX)
                        Build.sp(e)
                        e.tensor_scalar(b.f_st[:, h:h + 1], b.m_st[:, h:h + 1], b.tm1[:, 0:1],
                                        0.0, op0=ALU.subtract, op1=ALU.min)
                        e.tensor_scalar(b.nmn_st[:, h:h + 1], b.m_st[:, h:h + 1], b.tm1[:, 0:1],
                                        -1.0, op0=ALU.max, op1=ALU.mult)
                        e.tensor_scalar(b.m_st[:, h:h + 1], b.m_st[:, h:h + 1], b.tm1[:, 0:1],
                                        None, op0=ALU.max).then_inc(s_v.h, nv)
                    ph.em("vector", f_mx)
                    msk_v[hseq] = s_v.v
                    s_a.v += 2
                    def f_exp(e, h=h, hp=hseq, bank=bank, mv=msk_v[hseq],
                              ptr_v=trp_t.get(hseq - 2, (0, 0))):
                        e.wait_ge(s_v.h, mv)
                        e.wait_ge(s_t.h, ptr_v[1])
                        p32v = b.ta[:, 1024 * (hp % 2):1024 * (hp % 2 + 1)]
                        e.activation(out=b.f_st[:, h:h + 1], in_=b.f_st[:, h:h + 1],
                                     func=ACT.Exp, bias=0.0, scale=1.0).then_inc(s_a.h, 1)
                        e.activation(out=p32v, in_=bank[:], func=ACT.Exp,
                                     bias=b.nmn_st[:, h:h + 1], scale=1.0,
                                     accum_out=b.cs_st[:, h:h + 1]).then_inc(s_a.h, 1)
                    ph.em("scalar", f_exp)
                    exp_a[hseq] = s_a.v
                    pend.append((hseq, g, vb, vis, h))
                    hseq += 1
                s_v.v += 2
                def f_l(e, ea=s_a.v):
                    e.wait_ge(s_a.h, ea)
                    e.tensor_mul(b.l_st[:], b.l_st[:], b.f_st[:])
                    Build.sp(e)
                    e.tensor_add(b.l_st[:], b.l_st[:], b.cs_st[:]).then_inc(s_v.h, 2)
                ph.em("vector", f_l)
                vis += 1
            if pend:
                emit_head_pe_tail(hseq - 1)
            if I + 1 < n_qt:
                s_ld2.v += 32
                def f_qn(e, I=I, pe=s_t.v):
                    e.wait_ge(s_t.h, pe)
                    for src, dst in ((b.qhi_d, b.qtlhi[0]), (b.qlo_d, b.qtllo[0])):
                        in_ = bass.AP(src, 128 * (I + 1), [[T, 128], [128 * T, NDC], [1, 128]])
                        e.dma_start(out=dst[:], in_=in_).then_inc(s_ld2.h, 16)
                ph.em("scalar", f_qn)
            s_v.v += NH + 1
            v_att = s_v.v
            def f_att(e, prev=s_v.v):
                Build.sp(e)
                Build.sp(e)
                e.reciprocal(b.cs_st[:], b.l_st[:])
                Build.sp(e)
                Build.sp(e)
                for h in range(NH):
                    li = e.tensor_scalar(b.tb[:, 128 * h:128 * (h + 1)],
                                         b.O_sb[:, 128 * h:128 * (h + 1)],
                                         b.cs_st[:, h:h + 1], None, op0=ALU.mult)
                li.then_inc(s_v.h, NH + 1)
            ph.em("vector", f_att)
            s_v.v += 3
            v_ss = s_v.v
            def f_ss(e):
                e.tensor_mul(b.tc_[:], b.tb[:], b.tb[:])
                e.tensor_reduce(b.c1[:], b.tc_[:], axis=AXX, op=ALU.add)
                Build.sp(e)
                Build.sp(e)
                e.tensor_scalar_mul(b.c1[:], b.c1[:], 1.0 / D).then_inc(s_v.h, 3)
            ph.em("vector", f_ss)
            s_a.v += 1
            a_srt = s_a.v
            def f_srt(e, v_ss=v_ss):
                e.wait_ge(s_v.h, v_ss)
                e.activation(out=b.c2[:], in_=b.c1[:], func=ACT.Sqrt,
                             bias=b.epsc[:, 0:1], scale=1.0).then_inc(s_a.h, 1)
            ph.em("scalar", f_srt)
            s_v.v += 1
            v_quant = None
            def f_q(e, a_srt=a_srt, I=I):
                e.wait_ge(s_a.h, a_srt)
                e.reciprocal(b.c2[:], b.c2[:])
                Build.sp(e)
                e.tensor_scalar(b.tc_[:], b.tb[:], b.c2[:, 0:1], None, op0=ALU.mult)
                e.tensor_mul(b.tc_[:], b.tc_[:], b.nwsb[:])
                e.reduce_max(b.c1[:], b.tc_[:], axis=AXX, apply_absolute_value=True)
                Build.sp(e)
                e.tensor_scalar_max(b.c1[:], b.c1[:], 1e-5)
                Build.sp(e)
                e.reciprocal(b.c3[:], b.c1[:])
                Build.sp(e)
                e.tensor_scalar_mul(b.c3[:], b.c3[:], 127.0)
                Build.sp(e)
                e.reciprocal(b.c4[:], b.c3[:])
                Build.sp(e)
                e.tensor_copy(b.ds2_cols[:, I:I + 1], b.c4[:])
                e.tensor_scalar(b.te[:], b.tc_[:], b.c3[:, 0:1], None, op0=ALU.mult)
                e.tensor_copy(b.ti[:], b.te[:])
                e.tensor_copy(b.tc_[:], b.ti[:])
                e.tensor_sub(b.td[:], b.te[:], b.tc_[:])
                e.tensor_single_scalar(b.te[:], b.td[:], 0.5, op=ALU.is_ge)
                e.tensor_add(b.tc_[:], b.tc_[:], b.te[:])
                e.tensor_single_scalar(b.te[:], b.td[:], -0.5, op=ALU.is_le)
                e.tensor_sub(b.td[:], b.tc_[:], b.te[:])
                if self.cfg.get("p4_no_reset"):
                    e.memset(b.j0[:], 0.0)
                    e.memset(b.j0[:], 0.0)
                    e.memset(b.j1[:], 0.0).then_inc(s_v.h, 1)
                else:
                    e.memset(b.O_sb[:], 0.0)
                    e.memset(b.l_st[:], 0.0)
                    e.memset(b.m_st[:], NEG_BIG).then_inc(s_v.h, 1)
            ph.em("vector", f_q)
            v_quant = s_v.v
            for k in range(4):
                tbank = trp[k % 2]
                s_t.v += 1
                t_tr = s_t.v
                def f_tr2(e, k=k, tbank=tbank, vq=v_quant, pa=s_a.v):
                    e.wait_ge(s_v.h, vq)
                    e.wait_ge(s_a.h, pa)
                    for j in range(4):
                        dc = 4 * k + j
                        li = e.transpose(tbank[:, 128 * j:128 * (j + 1)],
                                         b.td[:, 128 * dc:128 * (dc + 1)], b.idf[:])
                    li.then_inc(s_t.h, 1)
                ph.em("tensor", f_tr2)
                s_a.v += 1
                def f_ev2(e, k=k, tbank=tbank, t_tr=t_tr, I=I):
                    e.wait_ge(s_t.h, t_tr)
                    out = bass.AP(b.r2T, (4 * k) * T + 128 * I,
                                  [[NDC * T, 128], [T, 4], [1, 128]])
                    e.activation(out=out, in_=tbank[:], func=ACT.Copy,
                                 bias=0.0, scale=1.0).then_inc(s_a.h, 1)
                ph.em("scalar", f_ev2)
            tr_a[hseq - 1] = s_a.v
            tr_v[hseq - 1] = s_v.v
            gq_oadd[(hseq - 1) // 4] = s_v.v
            gq_pv_t[(hseq - 1) // 4] = s_t.v

    def p5(self, ph, pst):
        b = self
        s_ld, s_ld2, s_v, s_a, s_t, s_st = b.s_ld, b.s_ld2, b.s_v, b.s_a, b.s_t, b.s_st
        ps_wo = [b.psum(f"p5wo{i}", [128, 512], F32, pst) for i in range(NT)]
        wot = [b.sbuf(f"p5wot{i}", [128, 512], F32, stack=pst, side="right") for i in range(2)]
        woev = b.sbuf("p5woev", [128, 512], F32, stack=pst, side="right")
        conv_v = {}
        mm_t = {}
        ld_v = {}
        for dm in range(4):
            evprev_a = s_a.v
            evprev_st = s_st.v
            for dc in range(NDC):
                i = dm * NDC + dc
                buf = i % 2
                s_ld.v += 16
                ld_v[i] = s_ld.v
                def f_ld(e, dm=dm, dc=dc, buf=buf, wc=conv_v.get(i - 2, 0)):
                    e.wait_ge(s_v.h, wc)
                    e.dma_start(out=wot[buf][:],
                                in_=b.woT[128 * dc:128 * (dc + 1), 512 * dm:512 * (dm + 1)]
                                ).then_inc(s_ld.h, 16)
                ph.em("sync", f_ld)
                s_v.v += 2
                conv_v[i] = s_v.v
                def f_cv(e, buf=buf, ldv=ld_v[i], wm=mm_t.get(i - 2, 0)):
                    e.wait_ge(s_ld.h, ldv)
                    e.wait_ge(s_t.h, wm)
                    e.tensor_copy(b.stage[:, 512 * buf:512 * (buf + 1)], wot[buf][:]).then_inc(s_v.h, 1)
                    e.tensor_sub(b.stage2[:, 512 * buf:512 * (buf + 1)], wot[buf][:],
                                 b.stage[:, 512 * buf:512 * (buf + 1)]).then_inc(s_v.h, 1)
                ph.em("vector", f_cv)
                s_t.v += 1
                mm_t[i] = s_t.v
                def f_mm(e, dm=dm, dc=dc, buf=buf, cv=conv_v[i], ea=evprev_a):
                    e.wait_ge(s_v.h, cv)
                    e.wait_ge(s_a.h, ea)
                    for w_i, w in enumerate((b.stage[:, 512 * buf:512 * (buf + 1)],
                                             b.stage2[:, 512 * buf:512 * (buf + 1)])):
                        for tt in range(NT):
                            lhsT = b.r2T[:, dc * T + 128 * tt:dc * T + 128 * (tt + 1)]
                            li = e.matmul(ps_wo[tt][:], lhsT, w,
                                          start=(dc == 0 and w_i == 0), stop=(dc == NDC - 1 and w_i == 1))
                    li.then_inc(s_t.h, 1)
                ph.em("tensor", f_mm)
            for tt in range(NT):
                s_a.v += 1
                a_ev = s_a.v
                def f_ev(e, tt=tt, dm=dm, mmv=s_t.v, pst_=b.s_ld3.v):
                    e.wait_ge(s_t.h, mmv)
                    e.wait_ge(b.s_ld3.h, pst_)
                    e.activation(out=woev[:], in_=ps_wo[tt][:], func=ACT.Copy, bias=0.0,
                                 scale=b.ds2_cols[:, tt:tt + 1]).then_inc(s_a.h, 1)
                ph.em("scalar", f_ev)
                b.s_ld3.v += 16
                def f_acc(e, tt=tt, dm=dm, a_ev=a_ev):
                    e.wait_ge(s_a.h, a_ev)
                    e.dma_start(out=b.y_out[128 * tt:128 * (tt + 1), 512 * dm:512 * (dm + 1)],
                                in_=woev[:], accum_op=ALU.add).then_inc(b.s_ld3.h, 16)
                ph.em("gpsimd", f_acc)

    def declare_pC(self, stack):
        sb = self.sbuf
        self.xnthi = sb("xnthi", [128, NDC * 512], BF16, stack=stack, side="right")
        self.xntlo = sb("xntlo", [128, NDC * 512], BF16, stack=stack, side="right")
        self.gubuf_hi = sb("gubuf_hi", [128, 11 * 512], BF16, stack=stack, side="right")
        self.gubuf_lo = sb("gubuf_lo", [128, 11 * 512], BF16, stack=stack, side="right")
        self.wgi = [sb(f"wgi{i}", [128, D], I32, stack=stack, side="right") for i in range(2)]
        self.wui = [sb(f"wui{i}", [128, D], I32, stack=stack, side="right") for i in range(2)]
        self.wgbf = [sb(f"wgbf{i}", [128, D], BF16, stack=stack, side="right") for i in range(2)]
        self.wubf = [sb(f"wubf{i}", [128, D], BF16, stack=stack, side="right") for i in range(2)]
        self.widm = [sb(f"widm{i}", [128, 11 * 128], I32, stack=stack, side="right") for i in range(2)]
        self.wdbf = [sb(f"wdbf{i}", [128, 11 * 128], BF16, stack=stack, side="right") for i in range(2)]
        self.gsil = sb("gsil", [128, 512], F32, stack=stack, side="right")
        self.gu32 = sb("gu32", [128, 512], F32, stack=stack, side="right")
        self.ffev = sb("ffev", [128, 512], F32, stack=stack, side="right")
        self.fftr = sb("fftr", [128, 512], F32, stack=stack, side="right")

    def p6(self, ph, pst):
        b = self
        s_ld, s_ld2, s_v, s_a, s_t, s_st = b.s_ld, b.s_ld2, b.s_v, b.s_a, b.s_t, b.s_st
        ps_g_ = [b.psum(f"p6g{i}", [128, 512], F32, pst) for i in range(2)]
        ps_u_ = [b.psum(f"p6u{i}", [128, 512], F32, pst) for i in range(2)]
        ps_d = [b.psum(f"p6d{i}", [128, 512], F32, pst) for i in range(2)]
        trp6 = [b.psum(f"p6tr{i}", [128, 512], F32, pst) for i in range(2)]

        NQ = 4
        HCQ = NHC // NQ

        for th in range(2):
            s_ld2.v += 32
            xn_ld = s_ld2.v
            prev_all = (s_t.v, s_v.v, s_a.v)
            def f_xn(e, th=th, prev_all=prev_all):
                e.wait_ge(s_t.h, prev_all[0])
                e.wait_ge(s_v.h, prev_all[1])
                e.wait_ge(s_a.h, prev_all[2])
                for src, dst in ((b.xnhi_d, b.xnthi), (b.xnlo_d, b.xntlo)):
                    in_ = bass.AP(src, 512 * th, [[T, 128], [128 * T, NDC], [1, 512]])
                    e.dma_start(out=dst[:], in_=in_).then_inc(s_ld2.h, 16)
            ph.em("scalar", f_xn)
            guv = {}
            gua = {}
            ldg = {}
            cvv = {}
            mmgu_t = {}
            for q in range(NQ):
                for hc in range(HCQ):
                    hqc = q * HCQ + hc
                    buf = hc % 2
                    s_ld.v += 32
                    ldg[hqc] = s_ld.v
                    def f_ld(e, hqc=hqc, buf=buf, wc=cvv.get(hqc - 2, 0)):
                        e.wait_ge(s_v.h, wc)
                        for w_in, dst in ((b.wgT, b.wgi[buf]), (b.wuT, b.wui[buf])):
                            in_ = bass.AP(w_in, 128 * hqc, [[HID, 128], [128 * HID, NDC], [1, 128]])
                            e.dma_start(out=dst[:], in_=in_).then_inc(s_ld.h, 16)
                    ph.em("sync", f_ld)
                    s_v.v += 2
                    cvv[hqc] = s_v.v
                    def f_cv(e, buf=buf, ldv=ldg[hqc], wm=mmgu_t.get(hqc - 2, 0)):
                        e.wait_ge(s_ld.h, ldv)
                        e.wait_ge(s_t.h, wm)
                        e.tensor_scalar(b.wgbf[buf][:], b.wgi[buf][:], 128.0, None,
                                        op0=ALU.subtract).then_inc(s_v.h, 1)
                        e.tensor_scalar(b.wubf[buf][:], b.wui[buf][:], 128.0, None,
                                        op0=ALU.subtract).then_inc(s_v.h, 1)
                    ph.em("vector", f_cv)
                    s_t.v += 1
                    mmgu_t[hqc] = s_t.v
                    def f_mm(e, th=th, buf=buf, cv=cvv[hqc], xn_ld=xn_ld,
                             pg=gua.get(hqc - 2, 0), pv=guv.get(hqc - 2, 0)):
                        e.wait_ge(s_v.h, cv)
                        e.wait_ge(s_ld2.h, xn_ld)
                        e.wait_ge(s_a.h, pg)
                        e.wait_ge(s_v.h, pv)
                        for W, psx in ((b.wgbf[buf], ps_g_[buf]), (b.wubf[buf], ps_u_[buf])):
                            for dc in range(NDC):
                                for xn in (b.xnthi, b.xntlo):
                                    li = e.matmul(psx[:], W[:, 128 * dc:128 * (dc + 1)],
                                                  xn[:, 512 * dc:512 * (dc + 1)],
                                                  start=(dc == 0 and xn is b.xnthi),
                                                  stop=(dc == NDC - 1 and xn is b.xntlo))
                        li.then_inc(s_t.h, 1)
                    ph.em("tensor", f_mm)
                    s_a.v += 1
                    a_sil = s_a.v
                    def f_sil(e, hqc=hqc, buf=buf, mm=mmgu_t[hqc], pv=guv.get(hqc - 1, 0)):
                        e.wait_ge(s_t.h, mm)
                        e.wait_ge(s_v.h, pv)
                        e.activation(out=b.gsil[:], in_=ps_g_[buf][:], func=ACT.Silu, bias=0.0,
                                     scale=b.sgsb[:, hqc:hqc + 1]).then_inc(s_a.h, 1)
                    ph.em("scalar", f_sil)
                    s_v.v += 1
                    v_gu = s_v.v
                    def f_gu(e, hqc=hqc, buf=buf, a_sil=a_sil):
                        e.wait_ge(s_a.h, a_sil)
                        e.scalar_tensor_tensor(b.gu32[:], ps_u_[buf][:], b.susb[:, hqc:hqc + 1],
                                               b.gsil[:], op0=ALU.mult, op1=ALU.mult).then_inc(s_v.h, 1)
                    ph.em("vector", f_gu)
                    s_a.v += 1
                    a_hi = s_a.v
                    def f_hi(e, hc=hc, v_gu=v_gu):
                        e.wait_ge(s_v.h, v_gu)
                        e.activation(out=b.gubuf_hi[:, 512 * hc:512 * (hc + 1)], in_=b.gu32[:],
                                     func=ACT.Copy, bias=0.0, scale=1.0).then_inc(s_a.h, 1)
                    ph.em("scalar", f_hi)
                    gua[hqc] = s_a.v
                    s_v.v += 1
                    def f_lo(e, hc=hc, a_hi=a_hi):
                        e.wait_ge(s_a.h, a_hi)
                        e.tensor_sub(b.gubuf_lo[:, 512 * hc:512 * (hc + 1)], b.gu32[:],
                                     b.gubuf_hi[:, 512 * hc:512 * (hc + 1)]).then_inc(s_v.h, 1)
                    ph.em("vector", f_lo)
                    guv[hqc] = s_v.v
                gu_all_v = s_v.v
                gu_all_a = s_a.v
                dconv = {}
                dmm = {}
                for dm in range(NDC):
                    buf = dm % 2
                    s_ld.v += 16
                    dl = s_ld.v
                    def f_ld2(e, q=q, dm=dm, buf=buf, wc=dconv.get(dm - 2, 0)):
                        e.wait_ge(s_v.h, wc)
                        in_ = bass.AP(b.wdT, (q * HCQ * 128) * D + 128 * dm,
                                      [[D, 128], [128 * D, HCQ], [1, 128]])
                        e.dma_start(out=b.widm[buf][:], in_=in_).then_inc(s_ld.h, 16)
                    ph.em("sync", f_ld2)
                    s_v.v += 1
                    dconv[dm] = s_v.v
                    def f_cv2(e, buf=buf, dl=dl, wm=dmm.get(dm - 2, 0)):
                        e.wait_ge(s_ld.h, dl)
                        e.wait_ge(s_t.h, wm)
                        e.tensor_scalar(b.wdbf[buf][:], b.widm[buf][:], 128.0, None,
                                        op0=ALU.subtract).then_inc(s_v.h, 1)
                    ph.em("vector", f_cv2)
                    s_t.v += 1
                    dmm[dm] = s_t.v
                    def f_dmm(e, dm=dm, buf=buf, cv=dconv[dm], guv_=gu_all_v, gua_=gu_all_a,
                              pe=s_a.v):
                        e.wait_ge(s_v.h, cv)
                        e.wait_ge(s_v.h, guv_)
                        e.wait_ge(s_a.h, gua_)
                        e.wait_ge(s_a.h, pe)
                        for hc in range(HCQ):
                            for gu in (b.gubuf_hi, b.gubuf_lo):
                                li = e.matmul(ps_d[buf][:], b.wdbf[buf][:, 128 * hc:128 * (hc + 1)],
                                              gu[:, 512 * hc:512 * (hc + 1)],
                                              start=(hc == 0 and gu is b.gubuf_hi),
                                              stop=(hc == HCQ - 1 and gu is b.gubuf_lo))
                        li.then_inc(s_t.h, 1)
                    ph.em("tensor", f_dmm)
                    s_a.v += 1
                    a_ev = s_a.v
                    def f_ev(e, dm=dm, buf=buf, mm=dmm[dm], ptr=s_t.v):
                        e.wait_ge(s_t.h, mm)
                        e.wait_ge(s_t.h, ptr)
                        e.activation(out=b.ffev[:], in_=ps_d[buf][:], func=ACT.Copy, bias=0.0,
                                     scale=b.sdsb[:, dm:dm + 1]).then_inc(s_a.h, 1)
                    ph.em("scalar", f_ev)
                    s_t.v += 1
                    t_tr0 = s_t.v
                    def f_tr(e, a_ev=a_ev, pa=s_a.v):
                        e.wait_ge(s_a.h, a_ev)
                        e.wait_ge(s_a.h, pa)
                        for j in range(4):
                            li = e.transpose(trp6[0][:, 128 * j:128 * (j + 1)],
                                             b.ffev[:, 128 * j:128 * (j + 1)], b.idf[:])
                        li.then_inc(s_t.h, 1)
                    ph.em("tensor", f_tr)
                    s_a.v += 1
                    a_ev2 = s_a.v
                    def f_ev2(e, t_tr=t_tr0, pst_=b.s_ld3.v):
                        e.wait_ge(s_t.h, t_tr)
                        e.wait_ge(b.s_ld3.h, pst_)
                        e.activation(out=b.fftr[:], in_=trp6[0][:], func=ACT.Copy, bias=0.0,
                                     scale=1.0).then_inc(s_a.h, 1)
                    ph.em("scalar", f_ev2)
                    b.s_ld3.v += 16
                    def f_acc(e, th=th, dm=dm, a_ev2=a_ev2):
                        e.wait_ge(s_a.h, a_ev2)
                        out = bass.AP(b.y_out, (512 * th) * D + 128 * dm,
                                      [[D, 128], [128 * D, 4], [1, 128]])
                        e.dma_start(out=out, in_=b.fftr[:], accum_op=ALU.add).then_inc(b.s_ld3.h, 16)
                    ph.em("gpsimd", f_acc)



def make_core_inputs(inputs, core):
    x = np.asarray(inputs["L0_attn_x"])[0]
    wqkv = np.asarray(inputs["L0_attn_wqkv"])
    wo = np.asarray(inputs["L0_attn_wo"])
    nw = np.asarray(inputs["L0_attn_norm_w"])
    wg = np.asarray(inputs["L0_ffn_wq_gate"])
    sgv = np.asarray(inputs["L0_ffn_s_gate"])
    wu = np.asarray(inputs["L0_ffn_wq_up"])
    suv = np.asarray(inputs["L0_ffn_s_up"])
    wd = np.asarray(inputs["L0_ffn_wq_down"])
    sdv = np.asarray(inputs["L0_ffn_s_down"])

    shared = dict(
        wqkvT=np.ascontiguousarray(wqkv.T),
        woT=np.ascontiguousarray(wo.T),
        nwrep=np.ascontiguousarray(
            np.broadcast_to(nw.reshape(NH, 128).T[:, :, None], (128, NH, 128)).reshape(128, D)
        ).astype(np.float32),
        wgT=np.ascontiguousarray(wg.T),
        wuT=np.ascontiguousarray(wu.T),
        wdT=np.ascontiguousarray(wd.T),
        sg=np.ascontiguousarray(sgv.reshape(NHC, 128).T).astype(np.float32),
        su=np.ascontiguousarray(suv.reshape(NHC, 128).T).astype(np.float32),
        sd=np.ascontiguousarray(sdv.reshape(NDC, 128).T).astype(np.float32),
        xfr=np.ascontiguousarray((-(np.arange(128) // 2) / 64.0).reshape(128, 1)).astype(np.float32),
        sgn=np.ascontiguousarray(np.where(np.arange(128) % 2 == 0, -1.0, 1.0).reshape(128, 1)).astype(np.float32),
        idf=np.eye(128, dtype=np.float32),
        onesf=np.ones((128, 128), dtype=np.float32),
        onec=np.ones((128, 1), dtype=np.float32),
    )
    xc = np.ascontiguousarray(x[core::NC])
    pos = (core + NC * np.arange(T)).astype(np.float32).reshape(1, T)
    i_idx = np.arange(128)[:, None]
    r_idx = (np.arange(1024)[None, :] // 128)
    m_idx = (np.arange(1024)[None, :] % 128)
    allowed = (m_idx < i_idx) | ((m_idx == i_idx) & (r_idx <= core))
    mask = np.where(allowed, 0.0, NEG_BIG).astype(np.float32)
    return dict(shared, x=xc, pos=pos, mask=mask)


def build_full(cfg=None):
    b = Build(cfg or dict(debug_outputs={}))
    ph = Phase(b.nc)
    b.p0(ph)
    ph.run()
    with ExitStack() as sA:
        b.declare_pA(sA)
        with ExitStack() as sP1:
            ph = Phase(b.nc); b.p1(ph, sP1); ph.run()
            ph = Phase(b.nc); b.p15(ph, sP1); ph.run()
        with ExitStack() as sP2:
            ph = Phase(b.nc); b.guard_all(ph); b.p2(ph, sP2); ph.run()
            ph = Phase(b.nc); b.p3(ph); ph.run()
    with ExitStack() as sB:
        b.declare_pB(sB)
        with ExitStack() as sP4:
            ph = Phase(b.nc); b.guard_all(ph); b.p4(ph, sP4); ph.run()
        with ExitStack() as sP5:
            ph = Phase(b.nc); b.guard_all(ph); b.p5(ph, sP5); ph.run()
    with ExitStack() as sC:
        b.declare_pC(sC)
        with ExitStack() as sP6:
            ph = Phase(b.nc); b.guard_all(ph); b.p6(ph, sP6); ph.run()
    ph = Phase(b.nc)
    b.guard_all(ph)
    ph.run()
    b.es.close()
    return b


def kernel(**inputs):
    b = build_full()
    in_maps = [make_core_inputs(inputs, c) for c in range(NC)]
    res = run_bass_kernel_spmd(b.nc, in_maps, core_ids=list(range(NC)))
    y = np.empty((S, D), dtype=np.float32)
    for c in range(NC):
        y[c::NC] = res.results[c]["y"]
    return y.reshape(1, S, D)


# version 12
# speedup vs baseline: 1.0697x; 1.0069x over previous
import numpy as np
import ml_dtypes
from contextlib import ExitStack

import concourse.bass as bass
import concourse.mybir as mybir
from concourse.bass_utils import run_bass_kernel_spmd

F32 = mybir.dt.float32
BF16 = mybir.dt.bfloat16
I32 = mybir.dt.int32
ALU = mybir.AluOpType
ACT = mybir.ActivationFunctionType
AXX = mybir.AxisListType.X

S, D, NH, NKV, HD = 8192, 2048, 16, 4, 128
HQKV, HID, NC = 3072, 5632, 8
T = S // NC
NT = T // 128
NDC = D // 128
NHC = HID // 128
EPS = 1e-5
NEG_BIG = -30000.0
SCORE_SCALE = float(1.0 / np.float32(np.sqrt(np.float64(128.0))))
THETA = 500000.0

ENGS = ("sync", "vector", "scalar", "tensor", "gpsimd")


def _cw2pi():
    two_pi = 2.0 * np.pi
    c1 = float(np.float32(np.round(two_pi * 512) / 512.0))
    r = two_pi - c1
    c2 = float(np.float32(np.round(r * 2.0**32) / 2.0**32))
    c3 = float(np.float32(two_pi - c1 - c2))
    return c1, c2, c3


def _halfpi2():
    hp = 0.5 * np.pi
    a = float(np.float32(hp))
    b = float(np.float32(hp - a))
    return a, b


class Sem:
    def __init__(self, es, nc, name):
        self.h = es.enter_context(nc.semaphore(name))
        self.v = 0


class Phase:

    def __init__(self, nc):
        self.nc = nc
        self.ops = {k: [] for k in ENGS}

    def em(self, eng, fn):
        self.ops[eng].append(fn)

    def run(self):
        with self.nc.Block() as block:
            for eng in ENGS:
                fns = self.ops[eng]
                if not fns:
                    continue
                getattr(block, eng)(lambda e, fns=fns: [f(e) for f in fns])


class Build:

    def __init__(self, cfg):
        self.cfg = cfg
        self.nc = bass.Bass("TRN2", target_bir_lowering=False, detect_race_conditions=False)
        self.es = ExitStack()
        self._declare_io()
        self._declare_sems()
        self._declare_persistent()

    def _declare_io(self):
        nc = self.nc
        self.x_in = nc.dram_tensor("x", [T, D], F32, kind="ExternalInput")
        self.wqkvT = nc.dram_tensor("wqkvT", [D, HQKV], F32, kind="ExternalInput")
        self.woT = nc.dram_tensor("woT", [D, D], F32, kind="ExternalInput")
        self.nwrep = nc.dram_tensor("nwrep", [128, D], F32, kind="ExternalInput")
        self.wgT = nc.dram_tensor("wgT", [D, HID], I32, kind="ExternalInput")
        self.wuT = nc.dram_tensor("wuT", [D, HID], I32, kind="ExternalInput")
        self.wdT = nc.dram_tensor("wdT", [HID, D], I32, kind="ExternalInput")
        self.sg_in = nc.dram_tensor("sg", [128, NHC], F32, kind="ExternalInput")
        self.su_in = nc.dram_tensor("su", [128, NHC], F32, kind="ExternalInput")
        self.sd_in = nc.dram_tensor("sd", [128, NDC], F32, kind="ExternalInput")
        self.pos_in = nc.dram_tensor("pos", [1, T], F32, kind="ExternalInput")
        self.mask_in = nc.dram_tensor("mask", [128, 1024], F32, kind="ExternalInput")
        self.xfr_in = nc.dram_tensor("xfr", [128, 1], F32, kind="ExternalInput")
        self.sgn_in = nc.dram_tensor("sgn", [128, 1], F32, kind="ExternalInput")
        self.idf_in = nc.dram_tensor("idf", [128, 128], F32, kind="ExternalInput")
        self.onesf_in = nc.dram_tensor("onesf", [128, 128], F32, kind="ExternalInput")
        self.onec_in = nc.dram_tensor("onec", [128, 1], F32, kind="ExternalInput")
        self.y_out = nc.dram_tensor("y", [T, D], F32, kind="ExternalOutput")
        self.ag_in = nc.dram_tensor("ag_in", [2048, T], BF16)
        self.ag_out = nc.dram_tensor("ag_out", [NC * 2048, T], BF16)
        self.xnhi_d = nc.dram_tensor("xnhi_d", [D, T], BF16)
        self.xnlo_d = nc.dram_tensor("xnlo_d", [D, T], BF16)
        self.qhi_d = nc.dram_tensor("qhi_d", [D, T], BF16)
        self.qlo_d = nc.dram_tensor("qlo_d", [D, T], BF16)
        self.row_d = nc.dram_tensor("row_d", [1, 2048], F32)
        self.dbg = {}
        for name, shape in self.cfg.get("debug_outputs", {}).items():
            self.dbg[name] = self.nc.dram_tensor(name, list(shape), F32, kind="ExternalOutput")

    def _declare_sems(self):
        es, nc = self.es, self.nc
        self.s_ld = Sem(es, nc, "s_ld")
        self.s_ld2 = Sem(es, nc, "s_ld2")
        self.s_ld3 = Sem(es, nc, "s_ld3")
        self.s_v = Sem(es, nc, "s_v")
        self.s_a = Sem(es, nc, "s_a")
        self.s_t = Sem(es, nc, "s_t")
        self.s_g = Sem(es, nc, "s_g")
        self.s_st = Sem(es, nc, "s_st")
        self.all_sems = [self.s_ld, self.s_ld2, self.s_ld3, self.s_v, self.s_a,
                         self.s_t, self.s_g, self.s_st]

    def sbuf(self, name, shape, dtype, stack=None, side=None):
        kw = {} if side is None else {"side": side}
        return (stack or self.es).enter_context(self.nc.sbuf_tensor(name, shape, dtype, **kw))

    def psum(self, name, shape, dtype, stack):
        return stack.enter_context(self.nc.psum_tensor(name, shape, dtype))

    def _declare_persistent(self):
        sb = self.sbuf
        self.dq_cols = sb("dq_cols", [128, NT], F32)
        self.ds2_cols = sb("ds2_cols", [128, NT], F32)
        self.masksb = sb("masksb", [128, 1024], F32)
        self.nwsb = sb("nwsb", [128, D], F32)
        self.sgsb = sb("sgsb", [128, NHC], F32)
        self.susb = sb("susb", [128, NHC], F32)
        self.sdsb = sb("sdsb", [128, NDC], F32)
        self.xfrsb = sb("xfrsb", [128, 1], F32)
        self.sgnsb = sb("sgnsb", [128, 1], F32)
        self.idf = sb("idf_s", [128, 128], F32)
        self.onescol = sb("onescol", [128, 1], F32)
        self.onesrow1 = sb("onesrow1", [1, 128], F32)
        self.ta = sb("ta", [128, D], F32)
        self.tb = sb("tb", [128, D], F32)
        self.tc_ = sb("tc_", [128, D], F32)
        self.ti = sb("ti", [128, D], I32)
        self.td = sb("td", [128, D], F32)
        self.te = sb("te", [128, D], F32)
        self.st24 = sb("st24", [128, 24], F32)
        self.st8 = sb("st8", [128, 8], F32)
        self.c1 = sb("c1", [128, 1], F32)
        self.c2 = sb("c2", [128, 1], F32)
        self.c3 = sb("c3", [128, 1], F32)
        self.c4 = sb("c4", [128, 1], F32)
        self.stage = sb("stage", [128, D], BF16)
        self.stage2 = sb("stage2", [128, D], BF16)
        self.evrow = sb("evrow", [16, 128], F32)
        self.rowa = sb("rowa", [1, 128], F32)
        self.rowb = sb("rowb", [1, 128], F32)
        self.rowc = sb("rowc", [1, 128], F32)
        self.epsc = sb("epsc", [128, 1], F32)
        self.j0 = sb("j0", [128, 1], F32)
        self.j1 = sb("j1", [128, 1], F32)
        Build._sp_j0 = self.j0[:]
        Build._sp_j1 = self.j1[:]

    def declare_pA(self, stack):
        sb = self.sbuf
        self.rT = sb("rT", [128, NDC * T], BF16, stack=stack, side="right")
        self.dqb = sb("dqb", [128, T], F32, stack=stack, side="right")
        self.possb = sb("possb", [128, T], F32, stack=stack, side="right")
        self.ctabQ = sb("ctabQ", [128, T], F32, stack=stack, side="right")
        self.stabQ = sb("stabQ", [128, T], F32, stack=stack, side="right")
        self.ctabK = sb("ctabK", [128, T], F32, stack=stack, side="right")
        self.stabK = sb("stabK", [128, T], F32, stack=stack, side="right")

    @staticmethod
    def sp(e):
        e.tensor_copy(Build._sp_j1, Build._sp_j0)

    def dump_now(self, ph, names):
        for nm in names:
            t = getattr(self, nm)
            shp = list(t.ap().shape)
            d = self.nc.dram_tensor("dbg_" + nm, shp, t.ap().dtype, kind="ExternalOutput")
            self.s_st.v += 16
            def f(e, t=t, d=d):
                e.dma_start(out=d.ap(), in_=t.ap()).then_inc(self.s_st.h, 16)
            ph.em("sync", f)

    def guard_all(self, ph):
        vals = [(s.h, s.v) for s in self.all_sems if s.v > 0]
        for eng in ENGS:
            def f(e, vals=vals):
                for h, v in vals:
                    e.wait_ge(h, v)
            ph.em(eng, f)

    def p0(self, ph):
        b = self
        s_ld = b.s_ld
        loads = [
            (b.masksb[:], b.mask_in[:, :]),
            (b.nwsb[:], b.nwrep[:, :]),
            (b.sgsb[:], b.sg_in[:, :]),
            (b.susb[:], b.su_in[:, :]),
            (b.sdsb[:], b.sd_in[:, :]),
            (b.xfrsb[:], b.xfr_in[:, :]),
            (b.sgnsb[:], b.sgn_in[:, :]),
            (b.idf[:], b.idf_in[:, :]),
            (b.onescol[:], b.onec_in[:, :]),
            (b.onesrow1[:], b.onesf_in[0:1, 0:128]),
        ]
        def f(e):
            for out, in_ in loads:
                e.dma_start(out=out, in_=in_).then_inc(s_ld.h, 16)
        ph.em("sync", f)
        s_ld.v += 16 * len(loads)
        b.s_v.v += 1
        ph.em("vector", lambda e: (e.memset(b.epsc[:], EPS), e.memset(b.j0[:], 0.0),
                                   e.memset(b.j1[:], 0.0).then_inc(b.s_v.h, 1)))

    def p1(self, ph, pst):
        b = self
        nc = self.nc
        rT = b.rT
        b.xs = b.sbuf("xs", [128, D], F32, stack=pst)
        trp = [b.psum(f"p1trp{i}", [128, 512], F32, pst) for i in range(2)]
        s_ld, s_ld2, s_v, s_a, s_t, s_st = b.s_ld, b.s_ld2, b.s_v, b.s_a, b.s_t, b.s_st
        skip = self.cfg.get("p1_skip", set())
        n_tiles = self.cfg.get("p1_tiles", NT)

        for it in range(n_tiles):
            prev_v, prev_a, prev_st, prev_t = s_v.v, s_a.v, s_st.v, s_t.v
            s_ld.v += 16
            ldv = s_ld.v
            ph.em("sync", lambda e, it=it, prev_v=prev_v, prev_a=prev_a, prev_st=prev_st, ldv=ldv: (
                e.wait_ge(s_v.h, prev_v),
                e.wait_ge(s_a.h, prev_a),
                e.wait_ge(s_st.h, prev_st),
                e.dma_start(out=b.xs[:], in_=b.x_in[128 * it:128 * (it + 1), :]).then_inc(s_ld.h, 16),
            ))
            s_st.v += 16
            ph.em("scalar", lambda e, it=it, ldv=ldv: (
                e.wait_ge(s_ld.h, ldv),
                e.dma_start(out=b.y_out[128 * it:128 * (it + 1), :], in_=b.xs[:]).then_inc(s_st.h, 16),
            ))
            s_v.v += 1
            v_stats = s_v.v
            def f_stats(e, ldv=ldv):
                e.wait_ge(s_ld.h, ldv)
                e.tensor_reduce(b.c1[:], b.xs[:], axis=AXX, op=ALU.add)
                e.tensor_mul(b.te[:], b.xs[:], b.xs[:])
                e.tensor_reduce(b.c3[:], b.te[:], axis=AXX, op=ALU.add)
                Build.sp(e)
                e.tensor_scalar_mul(b.c1[:], b.c1[:], 1.0 / D)
                Build.sp(e)
                e.tensor_scalar_mul(b.c3[:], b.c3[:], 1.0 / D)
                e.tensor_mul(b.c2[:], b.c1[:], b.c1[:])
                Build.sp(e)
                e.tensor_sub(b.c3[:], b.c3[:], b.c2[:]).then_inc(s_v.h, 1)
            ph.em("vector", f_stats)
            s_a.v += 1
            a_rsq = s_a.v
            def f_rsq(e, v_stats=v_stats):
                e.wait_ge(s_v.h, v_stats)
                e.activation(out=b.c4[:], in_=b.c3[:], func=ACT.Sqrt, bias=b.epsc[:, 0:1], scale=1.0).then_inc(s_a.h, 1)
            ph.em("scalar", f_rsq)
            s_v.v += 1
            v_nmr = s_v.v
            def f_nmr(e, a_rsq=a_rsq):
                e.wait_ge(s_a.h, a_rsq)
                e.reciprocal(b.c4[:], b.c4[:]).then_inc(s_v.h, 1)
            ph.em("vector", f_nmr)
            s_v.v += 1
            v_xn = s_v.v
            def f_xn(e, prev_t=prev_t):
                e.wait_ge(s_t.h, prev_t)
                e.tensor_scalar(b.te[:], b.xs[:], b.c1[:, 0:1], None, op0=ALU.subtract)
                e.tensor_scalar(b.ta[:], b.te[:], b.c4[:, 0:1], None, op0=ALU.mult).then_inc(s_v.h, 1)
            ph.em("vector", f_xn)
            a_xn = v_xn
            s_v.v += 1
            v_qs = s_v.v
            def f_qs(e, it=it):
                e.reduce_max(b.c1[:], b.ta[:], axis=AXX, apply_absolute_value=True)
                Build.sp(e)
                e.tensor_scalar_max(b.c1[:], b.c1[:], 1e-5)
                Build.sp(e)
                e.reciprocal(b.c2[:], b.c1[:])
                Build.sp(e)
                e.tensor_scalar_mul(b.c2[:], b.c2[:], 127.0)
                Build.sp(e)
                e.reciprocal(b.c3[:], b.c2[:])
                Build.sp(e)
                e.tensor_copy(b.dq_cols[:, it:it + 1], b.c3[:]).then_inc(s_v.h, 1)
            ph.em("vector", f_qs)
            s_v.v += 1
            def f_yq(e):
                e.tensor_scalar(b.tb[:], b.ta[:], b.c2[:, 0:1], None, op0=ALU.mult).then_inc(s_v.h, 1)
            ph.em("vector", f_yq)
            s_v.v += 1
            v_rr = s_v.v
            def f_rr(e, prev_t=prev_t):
                e.wait_ge(s_t.h, prev_t)
                e.tensor_copy(b.ti[:], b.tb[:])
                e.tensor_copy(b.tc_[:], b.ti[:])
                e.tensor_sub(b.td[:], b.tb[:], b.tc_[:])
                e.tensor_single_scalar(b.te[:], b.td[:], 0.5, op=ALU.is_ge)
                e.tensor_add(b.tc_[:], b.tc_[:], b.te[:])
                e.tensor_single_scalar(b.te[:], b.td[:], -0.5, op=ALU.is_le)
                e.tensor_sub(b.td[:], b.tc_[:], b.te[:]).then_inc(s_v.h, 1)
            ph.em("vector", f_rr)
            for src_i in ([] if "tr" in skip else [i_ for i_ in range(2) if f"tr{i_}" not in skip]):
                for k in range(4):
                    bank = trp[(it * 8 + src_i * 4 + k) % 2]
                    src_sem, src_val = (s_v, v_rr) if src_i == 0 else (s_v, a_xn)
                    s_t.v += 1
                    t_tr = s_t.v
                    need_a = s_a.v
                    need_v = s_v.v
                    def f_tr(e, k=k, bank=bank, src_i=src_i, src_sem=src_sem,
                             src_val=src_val, need_a=need_a, need_v=need_v):
                        e.wait_ge(src_sem.h, src_val)
                        e.wait_ge(s_a.h, need_a)
                        e.wait_ge(s_v.h, need_v)
                        src = b.td if src_i == 0 else b.ta
                        for j in range(4):
                            dc = 4 * k + j
                            li = e.transpose(bank[:, 128 * j:128 * (j + 1)],
                                             src[:, 128 * dc:128 * (dc + 1)], b.idf[:])
                        li.then_inc(s_t.h, 1)
                    ph.em("tensor", f_tr)
                    if src_i == 0:
                        s_a.v += 1
                        def f_ev(e, k=k, bank=bank, it=it, t_tr=t_tr):
                            e.wait_ge(s_t.h, t_tr)
                            out = bass.AP(rT, (4 * k) * T + 128 * it,
                                          [[NDC * T, 128], [T, 4], [1, 128]])
                            e.activation(out=out, in_=bank[:], func=ACT.Copy,
                                         bias=0.0, scale=1.0).then_inc(s_a.h, 1)
                        ph.em("scalar", f_ev)
                    else:
                        s_a.v += 1
                        a_hi = s_a.v
                        def f_hi(e, k=k, bank=bank, t_tr=t_tr, prev_st=prev_st):
                            e.wait_ge(s_t.h, t_tr)
                            e.wait_ge(s_st.h, prev_st)
                            e.activation(out=b.stage[:, 512 * k:512 * (k + 1)], in_=bank[:],
                                         func=ACT.Copy, bias=0.0, scale=1.0).then_inc(s_a.h, 1)
                        ph.em("scalar", f_hi)
                        s_v.v += 1
                        def f_lo(e, k=k, bank=bank, a_hi=a_hi):
                            e.wait_ge(s_a.h, a_hi)
                            e.tensor_sub(b.stage2[:, 512 * k:512 * (k + 1)], bank[:],
                                         b.stage[:, 512 * k:512 * (k + 1)]).then_inc(s_v.h, 1)
                        ph.em("vector", f_lo)
            if "spill" in skip or "tr" in skip or "tr1" in skip:
                continue
            s_st.v += 32
            need_v = s_v.v
            need_a = s_a.v
            def f_spill(e, it=it, need_v=need_v, need_a=need_a):
                e.wait_ge(s_v.h, need_v)
                e.wait_ge(s_a.h, need_a)
                for src, dst in ((b.stage, b.xnhi_d), (b.stage2, b.xnlo_d)):
                    out = bass.AP(dst, 128 * it, [[T, 128], [128 * T, 16], [1, 128]])
                    in_ = bass.AP(src, 0, [[D, 128], [128, 16], [1, 128]])
                    e.dma_start(out=out, in_=in_).then_inc(s_st.h, 16)
            ph.em("sync", f_spill)

    def p15(self, ph, pst):
        b = self
        s_v, s_a, s_t, s_ld2, s_ld = b.s_v, b.s_a, b.s_t, b.s_ld2, b.s_ld
        ptr = b.psum(f"p15tr", [128, 128], F32, pst)
        s_ld.v += 16
        ph.em("sync", lambda e: e.dma_start(
            out=b.possb[:], in_=bass.AP(b.pos_in, 0, [[0, 128], [1, T]])).then_inc(s_ld.h, 16))

        s_t.v += 1
        t_tr = s_t.v
        need_v = s_v.v
        ph.em("tensor", lambda e, t=t_tr, nv=need_v: (
            e.wait_ge(s_v.h, nv),
            e.transpose(ptr[0:NT, :], b.dq_cols[:], b.idf[:]).then_inc(s_t.h, 1),
        ))
        s_a.v += 1
        a_ev = s_a.v
        ph.em("scalar", lambda e, t=t_tr: (
            e.wait_ge(s_t.h, t),
            e.activation(out=b.evrow[0:NT, :], in_=ptr[0:NT, :], func=ACT.Copy,
                         bias=0.0, scale=1.0).then_inc(s_a.h, 1),
        ))
        s_ld2.v += 32
        ld2 = s_ld2.v
        ph.em("scalar", lambda e, a=a_ev, ld2=ld2: (
            e.wait_ge(s_a.h, a),
            e.dma_start(out=bass.AP(b.row_d, 0, [[128, NT], [1, 128]]),
                        in_=bass.AP(b.evrow, 0, [[128, NT], [1, 128]])).then_inc(s_ld2.h, 16),
            e.wait_ge(s_ld2.h, ld2 - 16),
            e.dma_start(out=b.dqb[:], in_=bass.AP(b.row_d, 0, [[0, 128], [1, T]])).then_inc(s_ld2.h, 16),
        ))

        c1, c2, c3 = _cw2pi()
        hp1, hp2 = _halfpi2()
        inv2pi = float(np.float32(1.0 / (2.0 * np.pi)))
        pi_clamp = 3.1415925
        prev_a0 = s_a.v
        s_v.v += 1
        v_ms = s_v.v
        def f_ms(e, prev_a0=prev_a0):
            e.wait_ge(s_a.h, prev_a0)
            e.memset(b.c1[:], THETA).then_inc(s_v.h, 1)
        ph.em("vector", f_ms)
        s_a.v += 1
        a_ln = s_a.v
        def f_ln(e, v_ms=v_ms):
            e.wait_ge(s_v.h, v_ms)
            e.activation(out=b.c2[:], in_=b.c1[:], func=ACT.Ln, bias=0.0, scale=1.0).then_inc(s_a.h, 1)
        ph.em("scalar", f_ln)
        s_v.v += 1
        v_fr = s_v.v
        def f_fr(e, a_ln=a_ln):
            e.wait_ge(s_a.h, a_ln)
            e.tensor_scalar(b.c3[:], b.xfrsb[:], b.c2[:, 0:1], None, op0=ALU.mult).then_inc(s_v.h, 1)
        ph.em("vector", f_fr)
        s_a.v += 1
        a_fr = s_a.v
        ph.em("scalar", lambda e, v=v_fr: (
            e.wait_ge(s_v.h, v),
            e.activation(out=b.c4[:], in_=b.c3[:], func=ACT.Exp, bias=0.0, scale=1.0).then_inc(s_a.h, 1),
        ))
        ld_now = b.s_ld.v
        for ipass in range(2):
            prev_a2 = s_a.v
            s_v.v += 1
            v_ang = s_v.v
            def f_ang(e, ipass=ipass, prev_a2=prev_a2, ld_now=ld_now):
                e.wait_ge(s_a.h, prev_a2)
                e.wait_ge(b.s_ld.h, ld_now)
                ap_t = b.possb[:, 0:T]
                e.tensor_scalar(b.ta[:, 0:T], ap_t, b.c4[:, 0:1], None, op0=ALU.mult)
                if ipass == 1:
                    e.tensor_scalar(b.ta[:, 0:T], b.ta[:, 0:T], hp1, hp2, op0=ALU.add, op1=ALU.add)
                e.tensor_scalar(b.tb[:, 0:T], b.ta[:, 0:T], inv2pi, None, op0=ALU.mult)
                e.tensor_copy(b.ti[:, 0:T], b.tb[:, 0:T])
                e.tensor_copy(b.tb[:, 0:T], b.ti[:, 0:T])
                e.scalar_tensor_tensor(b.tc_[:, 0:T], b.tb[:, 0:T], -c1, b.ta[:, 0:T],
                                       op0=ALU.mult, op1=ALU.add)
                e.scalar_tensor_tensor(b.tc_[:, 0:T], b.tb[:, 0:T], -c2, b.tc_[:, 0:T],
                                       op0=ALU.mult, op1=ALU.add)
                e.scalar_tensor_tensor(b.tc_[:, 0:T], b.tb[:, 0:T], -c3, b.tc_[:, 0:T],
                                       op0=ALU.mult, op1=ALU.add)
                e.tensor_single_scalar(b.td[:, 0:T], b.tc_[:, 0:T], float(np.float32(np.pi)), op=ALU.is_gt)
                e.tensor_single_scalar(b.te[:, 0:T], b.tc_[:, 0:T], float(np.float32(-np.pi)), op=ALU.is_lt)
                e.tensor_sub(b.td[:, 0:T], b.te[:, 0:T], b.td[:, 0:T])
                e.scalar_tensor_tensor(b.tc_[:, 0:T], b.td[:, 0:T], float(np.float32(2 * np.pi)),
                                       b.tc_[:, 0:T], op0=ALU.mult, op1=ALU.add)
                e.tensor_scalar_min(b.tc_[:, 0:T], b.tc_[:, 0:T], pi_clamp)
                e.tensor_scalar_max(b.tc_[:, 0:T], b.tc_[:, 0:T], -pi_clamp).then_inc(s_v.h, 1)
            ph.em("vector", f_ang)
            s_a.v += 1
            a_sin = s_a.v
            tgt = [b.stabK, b.ctabK][ipass]
            ph.em("scalar", lambda e, v=v_ang, tgt=tgt: (
                e.wait_ge(s_v.h, v),
                e.activation(out=tgt[:], in_=b.tc_[:, 0:T], func=ACT.Sin, bias=0.0, scale=1.0).then_inc(s_a.h, 1),
            ))
        s_v.v += 1
        def f_tab(e, a=s_a.v, ld2=s_ld2.v):
            e.wait_ge(s_a.h, a)
            e.wait_ge(s_ld2.h, ld2)
            e.tensor_scalar(b.stabK[:], b.stabK[:], b.sgnsb[:, 0:1], None, op0=ALU.mult)
            e.tensor_mul(b.stabK[:], b.stabK[:], b.dqb[:])
            e.tensor_mul(b.ctabK[:], b.ctabK[:], b.dqb[:])
            e.tensor_scalar_mul(b.stabQ[:], b.stabK[:], SCORE_SCALE)
            e.tensor_scalar_mul(b.ctabQ[:], b.ctabK[:], SCORE_SCALE).then_inc(s_v.h, 1)
        ph.em("vector", f_tab)

    def p2(self, ph, pst):
        b = self
        s_ld, s_ld2, s_v, s_a, s_t = b.s_ld, b.s_ld2, b.s_v, b.s_a, b.s_t
        ps_qkv = [b.psum(f"p2qkv{i}", [128, 1024], F32, pst) for i in range(2)]
        trpv = [b.psum(f"p2trv{i}", [128, 512], F32, pst) for i in range(2)]
        wtile = [b.sbuf(f"p2wt{i}", [128, D], F32, stack=pst, side="right") for i in range(2)]
        whi = [b.sbuf(f"p2whi{i}", [128, D], BF16, stack=pst, side="right") for i in range(2)]
        wlo = [b.sbuf(f"p2wlo{i}", [128, D], BF16, stack=pst, side="right") for i in range(2)]
        qf = b.sbuf("p2qf", [128, T], F32, stack=pst, side="right")
        qsw = b.sbuf("p2qsw", [128, T], F32, stack=pst, side="right")
        qr = b.sbuf("p2qr", [128, T], F32, stack=pst, side="right")
        kst_hi = b.sbuf("p2ksth", [128, T], BF16, stack=pst, side="right")
        kst_lo = b.sbuf("p2kstl", [128, T], BF16, stack=pst, side="right")
        vst_hi = b.sbuf("p2vsth", [128, T], BF16, stack=pst, side="right")
        vst_lo = b.sbuf("p2vstl", [128, T], BF16, stack=pst, side="right")
        swapmask = [i ^ 1 for i in range(32)]

        conv_v = {}
        mm_t = {}
        epi_a = {}
        epi_v = {}
        ld_v = {}

        for hc in range(24):
            buf = hc % 2
            s_ld.v += 16
            ld_v[hc] = s_ld.v
            wait_conv = conv_v.get(hc - 2, 0)
            def f_ld(e, hc=hc, buf=buf, wait_conv=wait_conv):
                e.wait_ge(s_v.h, wait_conv)
                in_ = bass.AP(b.wqkvT, 128 * hc, [[HQKV, 128], [128 * HQKV, NDC], [1, 128]])
                e.dma_start(out=wtile[buf][:], in_=in_).then_inc(s_ld.h, 16)
            ph.em("sync", f_ld)
            wait_mm = mm_t.get(hc - 2, 0)
            s_v.v += 2
            conv_v[hc] = s_v.v
            def f_cv(e, hc=hc, buf=buf, wait_mm=wait_mm, ldv=ld_v[hc]):
                e.wait_ge(s_ld.h, ldv)
                e.wait_ge(s_t.h, wait_mm)
                e.tensor_copy(whi[buf][:], wtile[buf][:]).then_inc(s_v.h, 1)
                e.tensor_sub(wlo[buf][:], wtile[buf][:], whi[buf][:]).then_inc(s_v.h, 1)
            ph.em("vector", f_cv)
            wait_ea = epi_a.get(hc - 2, 0)
            wait_ev = epi_v.get(hc - 2, 0)
            s_t.v += 1
            mm_t[hc] = s_t.v
            def f_mm(e, hc=hc, buf=buf, cv=conv_v[hc], wait_ea=wait_ea, wait_ev=wait_ev):
                e.wait_ge(s_v.h, cv)
                e.wait_ge(s_a.h, wait_ea)
                e.wait_ge(s_v.h, wait_ev)
                for dc in range(NDC):
                    for w in (whi[buf], wlo[buf]):
                        for th in range(2):
                            li = e.matmul(ps_qkv[buf][:, 512 * th:512 * (th + 1)],
                                          w[:, 128 * dc:128 * (dc + 1)],
                                          b.rT[:, dc * T + 512 * th: dc * T + 512 * (th + 1)],
                                          start=(dc == 0 and w is whi[buf]),
                                          stop=(dc == NDC - 1 and w is wlo[buf]))
                li.then_inc(s_t.h, 1)
            ph.em("tensor", f_mm)
            if hc < 16:
                s_a.v += 1
                a_ev = s_a.v
                def f_ev(e, buf=buf, mmv=mm_t[hc], pv=epi_v.get(hc - 1, 0)):
                    e.wait_ge(s_t.h, mmv)
                    e.wait_ge(s_v.h, pv)
                    e.activation(out=qf[:], in_=ps_qkv[buf][:], func=ACT.Copy,
                                 bias=0.0, scale=1.0).then_inc(s_a.h, 1)
                ph.em("scalar", f_ev)
                s_v.v += 4
                v_rope = s_v.v
                def f_rope(e, a_ev=a_ev):
                    e.wait_ge(s_a.h, a_ev)
                    e.stream_shuffle(qsw[:], qf[:], swapmask)
                    e.tensor_mul(qr[:], qf[:], b.ctabQ[:])
                    e.tensor_mul(qsw[:], qsw[:], b.stabQ[:])
                    e.tensor_add(qr[:], qr[:], qsw[:]).then_inc(s_v.h, 4)
                ph.em("vector", f_rope)
                s_a.v += 1
                a_hi = s_a.v
                def f_hi(e, v_rope=v_rope, pstore=b.s_ld2.v):
                    e.wait_ge(s_v.h, v_rope)
                    e.wait_ge(s_ld2.h, pstore)
                    e.activation(out=b.stage[:, 0:T], in_=qr[:],
                                 func=ACT.Copy, bias=0.0, scale=1.0).then_inc(s_a.h, 1)
                ph.em("scalar", f_hi)
                s_v.v += 1
                v_lo = s_v.v
                def f_lo(e, a_hi=a_hi):
                    e.wait_ge(s_a.h, a_hi)
                    e.tensor_sub(b.stage2[:, 0:T], qr[:], b.stage[:, 0:T]).then_inc(s_v.h, 1)
                ph.em("vector", f_lo)
                s_ld2.v += 32
                def f_st(e, hc=hc, v_lo=v_lo, a_hi=a_hi):
                    e.wait_ge(s_v.h, v_lo)
                    e.wait_ge(s_a.h, a_hi)
                    e.dma_start(out=b.qhi_d[128 * hc:128 * (hc + 1), :],
                                in_=b.stage[:, 0:T]).then_inc(s_ld2.h, 16)
                    e.dma_start(out=b.qlo_d[128 * hc:128 * (hc + 1), :],
                                in_=b.stage2[:, 0:T]).then_inc(s_ld2.h, 16)
                ph.em("scalar", f_st)
                epi_a[hc] = a_hi
                epi_v[hc] = v_lo
            elif hc < 20:
                g = hc - 16
                s_a.v += 1
                a_ev = s_a.v
                def f_ev(e, buf=buf, mmv=mm_t[hc], pv=epi_v.get(hc - 1, 0), pst_=0):
                    e.wait_ge(s_t.h, mmv)
                    e.wait_ge(s_v.h, pv)
                    e.activation(out=qf[:], in_=ps_qkv[buf][:], func=ACT.Copy,
                                 bias=0.0, scale=1.0).then_inc(s_a.h, 1)
                ph.em("scalar", f_ev)
                s_v.v += 4
                v_rope = s_v.v
                def f_rope(e, a_ev=a_ev, pstore=b.s_ld2.v):
                    e.wait_ge(s_a.h, a_ev)
                    e.wait_ge(s_ld2.h, pstore)
                    e.stream_shuffle(qsw[:], qf[:], swapmask)
                    e.tensor_mul(qr[:], qf[:], b.ctabK[:])
                    e.tensor_mul(qsw[:], qsw[:], b.stabK[:])
                    e.tensor_add(qr[:], qr[:], qsw[:]).then_inc(s_v.h, 4)
                ph.em("vector", f_rope)
                s_a.v += 1
                a_hi = s_a.v
                def f_hi(e, v_rope=v_rope):
                    e.wait_ge(s_v.h, v_rope)
                    e.activation(out=kst_hi[:], in_=qr[:], func=ACT.Copy,
                                 bias=0.0, scale=1.0).then_inc(s_a.h, 1)
                ph.em("scalar", f_hi)
                s_v.v += 1
                v_lo = s_v.v
                def f_lo(e, a_hi=a_hi):
                    e.wait_ge(s_a.h, a_hi)
                    e.tensor_sub(kst_lo[:], qr[:], kst_hi[:]).then_inc(s_v.h, 1)
                ph.em("vector", f_lo)
                s_ld2.v += 32
                def f_st(e, g=g, v_lo=v_lo, a_hi=a_hi):
                    e.wait_ge(s_v.h, v_lo)
                    e.wait_ge(s_a.h, a_hi)
                    e.dma_start(out=b.ag_in[128 * g:128 * (g + 1), :], in_=kst_hi[:]).then_inc(s_ld2.h, 16)
                    e.dma_start(out=b.ag_in[512 + 128 * g:512 + 128 * (g + 1), :], in_=kst_lo[:]).then_inc(s_ld2.h, 16)
                ph.em("scalar", f_st)
                epi_a[hc] = a_hi
                epi_v[hc] = v_lo
            else:
                g = hc - 20
                s_v.v += 1
                v_dq = s_v.v
                def f_dq(e, buf=buf, mmv=mm_t[hc], pv=epi_v.get(hc - 1, 0)):
                    e.wait_ge(s_t.h, mmv)
                    e.wait_ge(s_v.h, pv)
                    e.tensor_mul(qf[:], ps_qkv[buf][:], b.dqb[:]).then_inc(s_v.h, 1)
                ph.em("vector", f_dq)
                tr_t = []
                for kb in range(2):
                    s_t.v += 1
                    tr_t.append(s_t.v)
                    def f_tr(e, kb=kb, v_dq=v_dq, pa=epi_a.get(hc - 1, 0), pvv=epi_v.get(hc - 1, 0)):
                        e.wait_ge(s_v.h, v_dq)
                        e.wait_ge(s_a.h, pa)
                        e.wait_ge(s_v.h, pvv)
                        for j in range(4):
                            ts = 4 * kb + j
                            li = e.transpose(trpv[kb][:, 128 * j:128 * (j + 1)],
                                             qf[:, 128 * ts:128 * (ts + 1)], b.idf[:])
                        li.then_inc(s_t.h, 1)
                    ph.em("tensor", f_tr)
                s_a.v += 2
                a_hi = s_a.v
                def f_hi(e, tr_t=tuple(tr_t), pst2=b.s_ld2.v):
                    e.wait_ge(s_ld2.h, pst2)
                    for kb in range(2):
                        e.wait_ge(s_t.h, tr_t[kb])
                        e.activation(out=vst_hi[:, 512 * kb:512 * (kb + 1)], in_=trpv[kb][:],
                                     func=ACT.Copy, bias=0.0, scale=1.0).then_inc(s_a.h, 1)
                ph.em("scalar", f_hi)
                s_v.v += 2
                v_lo = s_v.v
                def f_lo(e, a_hi=a_hi):
                    e.wait_ge(s_a.h, a_hi)
                    for kb in range(2):
                        e.tensor_sub(vst_lo[:, 512 * kb:512 * (kb + 1)], trpv[kb][:],
                                     vst_hi[:, 512 * kb:512 * (kb + 1)]).then_inc(s_v.h, 1)
                ph.em("vector", f_lo)
                s_ld2.v += 32
                def f_st(e, g=g, v_lo=v_lo, a_hi=a_hi):
                    e.wait_ge(s_v.h, v_lo)
                    e.wait_ge(s_a.h, a_hi)
                    for src, base in ((vst_hi, 1024 * T), (vst_lo, 1536 * T)):
                        out = bass.AP(b.ag_in, base + 128 * g,
                                      [[512, 128], [128 * 512, 8], [1, 128]])
                        in_ = bass.AP(src, 0, [[T, 128], [128, 8], [1, 128]])
                        e.dma_start(out=out, in_=in_).then_inc(s_ld2.h, 16)
                ph.em("scalar", f_st)
                epi_a[hc] = a_hi
                epi_v[hc] = v_lo

    def p3(self, ph):
        b = self
        need = [(b.s_ld2.h, b.s_ld2.v)]
        b.s_g.v += 1
        gv = b.s_g.v
        def f_ag(e, need=tuple(need)):
            for h, v in need:
                e.wait_ge(h, v)
            e.collective_compute(
                "AllGather", ALU.bypass,
                replica_groups=[list(range(NC))],
                ins=[b.ag_in[:, :]],
                outs=[b.ag_out[:, :]],
            ).then_inc(b.s_g.h, 1)
        ph.em("gpsimd", f_ag)
        for eng in ("sync", "vector", "scalar", "tensor"):
            ph.em(eng, lambda e, gv=gv: e.wait_ge(b.s_g.h, gv))

    def declare_pB(self, stack):
        sb = self.sbuf
        self.r2T = sb("r2T", [128, NDC * T], BF16, stack=stack, side="right")

    def declare_p4locals(self, stack):
        sb = self.sbuf
        self.qtlhi = [sb("qtlhi0", [128, D], BF16, stack=stack, side="right")] * 2
        self.qtllo = [sb("qtllo0", [128, D], BF16, stack=stack, side="right")] * 2
        self.khi = [sb(f"khi{i}", [128, NKV * 1024], BF16, stack=stack, side="right") for i in range(2)]
        self.klo = [sb(f"klo{i}", [128, NKV * 1024], BF16, stack=stack, side="right") for i in range(2)]
        self.vhi = [sb(f"vhi{i}", [128, NC * 512], BF16, stack=stack, side="right") for i in range(2)]
        self.vlo = [sb(f"vlo{i}", [128, NC * 512], BF16, stack=stack, side="right") for i in range(2)]
        self.phiT = sb("phiT", [128, 8 * 512], BF16, stack=stack, side="right")
        self.ploT = sb("ploT", [128, 8 * 512], BF16, stack=stack, side="right")
        self.fTrow = sb("fTrow", [1, 512], F32, stack=stack, side="right")
        self.fb_sb = sb("fb_sb", [128, 512], F32, stack=stack, side="right")
        self.q2b = sb("q2b", [128, 128], F32, stack=stack, side="right")
        self.colsum = sb("colsum", [128, 128], F32, stack=stack, side="right")
        self.O_sb = sb("O_sb", [128, NH * 128], F32, stack=stack, side="right")
        self.m_st = sb("m_st", [128, NH], F32, stack=stack, side="right")
        self.l_st = sb("l_st", [128, NH], F32, stack=stack, side="right")
        self.f_st = sb("f_st", [128, NH], F32, stack=stack, side="right")
        self.cs_st = sb("cs_st", [128, NH], F32, stack=stack, side="right")
        self.nmn_st = sb("nmn_st", [128, NH], F32, stack=stack, side="right")
        self.tm1 = sb("tm1", [128, 1], F32, stack=stack, side="right")

    def p4(self, ph, pst):
        b = self
        self.declare_p4locals(pst)
        s_ld, s_ld2, s_v, s_a, s_t = b.s_ld, b.s_ld2, b.s_v, b.s_a, b.s_t
        ps_sc = [b.psum(f"p4sc{i}", [128, 1024], F32, pst) for i in range(2)]
        trp = [b.psum(f"p4tr{i}", [128, 512], F32, pst) for i in range(2)]
        ps_pv = [b.psum(f"p4pv{i}", [128, 512], F32, pst) for i in range(2)]
        p32 = b.ta

        n_qt = self.cfg.get("n_qt", NT)
        hseq = 0
        exp_a = {}
        msk_v = {}
        pv_done_v = {}
        tr_a = {}
        tr_v = {}

        s_v.v += 3
        def f_init(e):
            e.memset(b.O_sb[:], 0.0)
            e.memset(b.l_st[:], 0.0)
            e.memset(b.m_st[:], NEG_BIG).then_inc(s_v.h, 3)
        ph.em("vector", f_init)
        s_ld2.v += 32
        def f_q0(e):
            for src, dst in ((b.qhi_d, b.qtlhi[0]), (b.qlo_d, b.qtllo[0])):
                in_ = bass.AP(src, 0, [[T, 128], [128 * T, NDC], [1, 128]])
                e.dma_start(out=dst[:], in_=in_).then_inc(s_ld2.h, 16)
        ph.em("scalar", f_q0)

        vis = 0
        sc_t = {}
        trp_t = {}
        pv_t = {}
        kv_free = {}
        kv_ld = {}
        pend = []

        gq_pv_t = {}
        gq_oadd = {}
        def emit_head_pe_tail(hp):
            info = pend.pop(0)
            assert info[0] == hp
            g_, vb_, visidx, hh = info[1], info[2], info[3], info[4]
            p_ = hh % 4
            gseq = hp // 4
            t_trs = []
            for kb in range(2):
                s_t.v += 1
                t_trs.append(s_t.v)
                def f_tr(e, kb=kb, hp=hp, ea=exp_a[hp], pa=tr_a.get(hp - 1, 0),
                         pv_=tr_v.get(hp - 1, 0)):
                    e.wait_ge(s_a.h, ea)
                    e.wait_ge(s_a.h, pa)
                    e.wait_ge(s_v.h, pv_)
                    p32v = b.ta[:, 1024 * (hp % 2):1024 * (hp % 2 + 1)]
                    for j in range(4):
                        kc = 4 * kb + j
                        li = e.transpose(trp[kb][:, 128 * j:128 * (j + 1)],
                                         p32v[:, 128 * kc:128 * (kc + 1)], b.idf[:])
                    li.then_inc(s_t.h, 1)
                ph.em("tensor", f_tr)
            trp_t[hp] = tuple(t_trs)
            s_a.v += 2
            a_phi = s_a.v
            def f_phi(e, t_trs=tuple(t_trs), p_=p_, ppv=gq_pv_t.get(gseq - 1, 0)):
                e.wait_ge(s_t.h, ppv)
                for kb in range(2):
                    e.wait_ge(s_t.h, t_trs[kb])
                    out = bass.AP(b.phiT, (4 * kb) * 512 + 128 * p_,
                                  [[8 * 512, 128], [512, 4], [1, 128]])
                    e.activation(out=out, in_=trp[kb][:], func=ACT.Copy,
                                 bias=0.0, scale=1.0).then_inc(s_a.h, 1)
            ph.em("scalar", f_phi)
            tr_a[hp] = s_a.v
            s_v.v += 2
            def f_plo(e, a_phi=a_phi, p_=p_):
                e.wait_ge(s_a.h, a_phi)
                for kb in range(2):
                    out = bass.AP(b.ploT, (4 * kb) * 512 + 128 * p_,
                                  [[8 * 512, 128], [512, 4], [1, 128]])
                    in1 = bass.AP(b.phiT, (4 * kb) * 512 + 128 * p_,
                                  [[8 * 512, 128], [512, 4], [1, 128]])
                    e.tensor_sub(out, trp[kb][:], in1).then_inc(s_v.h, 1)
            ph.em("vector", f_plo)
            tr_v[hp] = s_v.v
            if p_ != 3:
                return
            pvb = ps_pv[gseq % 2]
            s_t.v += 1
            gq_pv_t[gseq] = s_t.v
            def f_pv(e, g=g_, pvb=pvb, vb=vb_, kvld=kv_ld[visidx],
                     ta_=tr_a[hp], tv_=tr_v[hp], po=gq_oadd.get(gseq - 2, 0)):
                e.wait_ge(s_a.h, ta_)
                e.wait_ge(s_v.h, tv_)
                e.wait_ge(s_v.h, po)
                e.wait_ge(s_ld.h, kvld)
                for kc in range(NC):
                    vh = b.vhi[vb][:, 512 * kc + 128 * g:512 * kc + 128 * (g + 1)]
                    vl = b.vlo[vb][:, 512 * kc + 128 * g:512 * kc + 128 * (g + 1)]
                    pT = b.phiT[:, 512 * kc:512 * (kc + 1)]
                    pTl = b.ploT[:, 512 * kc:512 * (kc + 1)]
                    e.matmul(pvb[:], vh, pT, start=(kc == 0), stop=False)
                    e.matmul(pvb[:], vh, pTl, start=False, stop=False)
                    li = e.matmul(pvb[:], vl, pT, start=False, stop=(kc == NC - 1))
                li.then_inc(s_t.h, 1)
            ph.em("tensor", f_pv)
            s_a.v += 1
            a_pe = s_a.v
            def f_pe(e, pvb=pvb, pvt=gq_pv_t[gseq], po=gq_oadd.get(gseq - 1, 0)):
                e.wait_ge(s_t.h, pvt)
                e.wait_ge(s_v.h, po)
                e.activation(out=b.fb_sb[:], in_=pvb[:], func=ACT.Copy,
                             bias=0.0, scale=1.0).then_inc(s_a.h, 1)
            ph.em("scalar", f_pe)
            s_t.v += 1
            t_pvt = s_t.v
            def f_pvt(e, pvb=pvb, a_pe=a_pe):
                e.wait_ge(s_a.h, a_pe)
                for p2 in range(4):
                    li = e.transpose(pvb[:, 128 * p2:128 * (p2 + 1)],
                                     b.fb_sb[:, 128 * p2:128 * (p2 + 1)], b.idf[:])
                li.then_inc(s_t.h, 1)
            ph.em("tensor", f_pvt)
            s_v.v += 4
            def f_oa(e, g=g_, pvb=pvb, t_pvt=t_pvt):
                e.wait_ge(s_t.h, t_pvt)
                for p2 in range(4):
                    hh2 = 4 * g + p2
                    li = e.scalar_tensor_tensor(b.O_sb[:, 128 * hh2:128 * (hh2 + 1)],
                                                b.O_sb[:, 128 * hh2:128 * (hh2 + 1)],
                                                b.f_st[:, hh2:hh2 + 1],
                                                pvb[:, 128 * p2:128 * (p2 + 1)],
                                                op0=ALU.mult, op1=ALU.add)
                li.then_inc(s_v.h, 4)
            ph.em("vector", f_oa)
            gq_oadd[gseq] = s_v.v
            if hh == NH - 1:
                kv_free[visidx] = s_t.v

        for I in range(n_qt):
            qb = I % 2
            q_ld2 = s_ld2.v
            for J in range(I + 1):
                vb = vis % 2
                s_ld.v += 16 * 10
                kv_ld[vis] = s_ld.v
                def f_kv(e, J=J, vb=vb, free_t=kv_free.get(vis - 2, 0)):
                    e.wait_ge(s_t.h, free_t)
                    for g in range(NKV):
                        e.dma_start(out=b.khi[vb][:, 1024 * g:1024 * (g + 1)],
                                    in_=bass.AP(b.ag_out, 128 * g * T + 128 * J,
                                                [[T, 128], [2048 * T, NC], [1, 128]])
                                    ).then_inc(s_ld.h, 16)
                    for g in range(NKV):
                        e.dma_start(out=b.klo[vb][:, 1024 * g:1024 * (g + 1)],
                                    in_=bass.AP(b.ag_out, (512 + 128 * g) * T + 128 * J,
                                                [[T, 128], [2048 * T, NC], [1, 128]])
                                    ).then_inc(s_ld.h, 16)
                    e.dma_start(out=b.vhi[vb][:],
                                in_=bass.AP(b.ag_out, 1024 * T + 128 * J * 512, [[512, 128], [2048 * T, NC], [1, 512]])
                                ).then_inc(s_ld.h, 16)
                    e.dma_start(out=b.vlo[vb][:],
                                in_=bass.AP(b.ag_out, 1536 * T + 128 * J * 512, [[512, 128], [2048 * T, NC], [1, 512]])
                                ).then_inc(s_ld.h, 16)
                ph.em("sync", f_kv)
                for h in range(NH):
                    g = h // 4
                    bank = ps_sc[hseq % 2]
                    s_t.v += 1
                    sc_t[hseq] = s_t.v
                    def f_sc(e, h=h, g=g, bank=bank, kvld=kv_ld[vis], qb=qb, q_ld2=q_ld2,
                             pe=exp_a.get(hseq - 2, 0), pm=msk_v.get(hseq - 2, 0), vb=vb):
                        e.wait_ge(s_ld.h, kvld - 32)
                        e.wait_ge(s_ld2.h, q_ld2)
                        e.wait_ge(s_a.h, pe)
                        e.wait_ge(s_v.h, pm)
                        lh = b.qtlhi[qb][:, 128 * h:128 * (h + 1)]
                        ll = b.qtllo[qb][:, 128 * h:128 * (h + 1)]
                        outs, rhss, rlos = [], [], []
                        for kh in range(2):
                            outs.append(bank[:, 512 * kh:512 * (kh + 1)])
                            rhss.append(b.khi[vb][:, 1024 * g + 512 * kh:1024 * g + 512 * (kh + 1)])
                            rlos.append(b.klo[vb][:, 1024 * g + 512 * kh:1024 * g + 512 * (kh + 1)])
                        for kh in range(2):
                            e.matmul(outs[kh], lh, rhss[kh], start=True, stop=False)
                        for kh in range(2):
                            e.matmul(outs[kh], lh, rlos[kh], start=False, stop=False)
                        for kh in range(2):
                            li = e.matmul(outs[kh], ll, rhss[kh], start=False, stop=True)
                        li.then_inc(s_t.h, 1)
                    ph.em("tensor", f_sc)
                    if pend:
                        emit_head_pe_tail(hseq - 1)
                    nv = 5 if J == I else 4
                    s_v.v += nv
                    def f_mx(e, h=h, bank=bank, sct=sc_t[hseq], diag=(J == I), nv=nv):
                        e.wait_ge(s_t.h, sct)
                        if diag:
                            e.tensor_add(bank[:], bank[:], b.masksb[:])
                        e.reduce_max(b.tm1[:], bank[:], axis=AXX)
                        Build.sp(e)
                        e.tensor_scalar(b.f_st[:, h:h + 1], b.m_st[:, h:h + 1], b.tm1[:, 0:1],
                                        0.0, op0=ALU.subtract, op1=ALU.min)
                        e.tensor_scalar(b.nmn_st[:, h:h + 1], b.m_st[:, h:h + 1], b.tm1[:, 0:1],
                                        -1.0, op0=ALU.max, op1=ALU.mult)
                        e.tensor_scalar(b.m_st[:, h:h + 1], b.m_st[:, h:h + 1], b.tm1[:, 0:1],
                                        None, op0=ALU.max).then_inc(s_v.h, nv)
                    ph.em("vector", f_mx)
                    msk_v[hseq] = s_v.v
                    s_a.v += 2
                    def f_exp(e, h=h, hp=hseq, bank=bank, mv=msk_v[hseq],
                              ptr_v=trp_t.get(hseq - 2, (0, 0))):
                        e.wait_ge(s_v.h, mv)
                        e.wait_ge(s_t.h, ptr_v[1])
                        p32v = b.ta[:, 1024 * (hp % 2):1024 * (hp % 2 + 1)]
                        e.activation(out=b.f_st[:, h:h + 1], in_=b.f_st[:, h:h + 1],
                                     func=ACT.Exp, bias=0.0, scale=1.0).then_inc(s_a.h, 1)
                        e.activation(out=p32v, in_=bank[:], func=ACT.Exp,
                                     bias=b.nmn_st[:, h:h + 1], scale=1.0,
                                     accum_out=b.cs_st[:, h:h + 1]).then_inc(s_a.h, 1)
                    ph.em("scalar", f_exp)
                    exp_a[hseq] = s_a.v
                    pend.append((hseq, g, vb, vis, h))
                    hseq += 1
                s_v.v += 2
                def f_l(e, ea=s_a.v):
                    e.wait_ge(s_a.h, ea)
                    e.tensor_mul(b.l_st[:], b.l_st[:], b.f_st[:])
                    Build.sp(e)
                    e.tensor_add(b.l_st[:], b.l_st[:], b.cs_st[:]).then_inc(s_v.h, 2)
                ph.em("vector", f_l)
                vis += 1
            if pend:
                emit_head_pe_tail(hseq - 1)
            if I + 1 < n_qt:
                s_ld2.v += 32
                def f_qn(e, I=I, pe=s_t.v):
                    e.wait_ge(s_t.h, pe)
                    for src, dst in ((b.qhi_d, b.qtlhi[0]), (b.qlo_d, b.qtllo[0])):
                        in_ = bass.AP(src, 128 * (I + 1), [[T, 128], [128 * T, NDC], [1, 128]])
                        e.dma_start(out=dst[:], in_=in_).then_inc(s_ld2.h, 16)
                ph.em("scalar", f_qn)
            s_v.v += NH + 1
            v_att = s_v.v
            def f_att(e, prev=s_v.v):
                Build.sp(e)
                Build.sp(e)
                e.reciprocal(b.cs_st[:], b.l_st[:])
                Build.sp(e)
                Build.sp(e)
                for h in range(NH):
                    li = e.tensor_scalar(b.tb[:, 128 * h:128 * (h + 1)],
                                         b.O_sb[:, 128 * h:128 * (h + 1)],
                                         b.cs_st[:, h:h + 1], None, op0=ALU.mult)
                li.then_inc(s_v.h, NH + 1)
            ph.em("vector", f_att)
            s_v.v += 3
            v_ss = s_v.v
            def f_ss(e):
                e.tensor_mul(b.tc_[:], b.tb[:], b.tb[:])
                e.tensor_reduce(b.c1[:], b.tc_[:], axis=AXX, op=ALU.add)
                Build.sp(e)
                Build.sp(e)
                e.tensor_scalar_mul(b.c1[:], b.c1[:], 1.0 / D).then_inc(s_v.h, 3)
            ph.em("vector", f_ss)
            s_a.v += 1
            a_srt = s_a.v
            def f_srt(e, v_ss=v_ss):
                e.wait_ge(s_v.h, v_ss)
                e.activation(out=b.c2[:], in_=b.c1[:], func=ACT.Sqrt,
                             bias=b.epsc[:, 0:1], scale=1.0).then_inc(s_a.h, 1)
            ph.em("scalar", f_srt)
            s_v.v += 1
            v_quant = None
            def f_q(e, a_srt=a_srt, I=I):
                e.wait_ge(s_a.h, a_srt)
                e.reciprocal(b.c2[:], b.c2[:])
                Build.sp(e)
                e.tensor_scalar(b.tc_[:], b.tb[:], b.c2[:, 0:1], None, op0=ALU.mult)
                e.tensor_mul(b.tc_[:], b.tc_[:], b.nwsb[:])
                e.reduce_max(b.c1[:], b.tc_[:], axis=AXX, apply_absolute_value=True)
                Build.sp(e)
                e.tensor_scalar_max(b.c1[:], b.c1[:], 1e-5)
                Build.sp(e)
                e.reciprocal(b.c3[:], b.c1[:])
                Build.sp(e)
                e.tensor_scalar_mul(b.c3[:], b.c3[:], 127.0)
                Build.sp(e)
                e.reciprocal(b.c4[:], b.c3[:])
                Build.sp(e)
                e.tensor_copy(b.ds2_cols[:, I:I + 1], b.c4[:])
                e.tensor_scalar(b.te[:], b.tc_[:], b.c3[:, 0:1], None, op0=ALU.mult)
                e.tensor_copy(b.ti[:], b.te[:])
                e.tensor_copy(b.tc_[:], b.ti[:])
                e.tensor_sub(b.td[:], b.te[:], b.tc_[:])
                e.tensor_single_scalar(b.te[:], b.td[:], 0.5, op=ALU.is_ge)
                e.tensor_add(b.tc_[:], b.tc_[:], b.te[:])
                e.tensor_single_scalar(b.te[:], b.td[:], -0.5, op=ALU.is_le)
                e.tensor_sub(b.td[:], b.tc_[:], b.te[:])
                if self.cfg.get("p4_no_reset"):
                    e.memset(b.j0[:], 0.0)
                    e.memset(b.j0[:], 0.0)
                    e.memset(b.j1[:], 0.0).then_inc(s_v.h, 1)
                else:
                    e.memset(b.O_sb[:], 0.0)
                    e.memset(b.l_st[:], 0.0)
                    e.memset(b.m_st[:], NEG_BIG).then_inc(s_v.h, 1)
            ph.em("vector", f_q)
            v_quant = s_v.v
            for k in range(4):
                tbank = trp[k % 2]
                s_t.v += 1
                t_tr = s_t.v
                def f_tr2(e, k=k, tbank=tbank, vq=v_quant, pa=s_a.v):
                    e.wait_ge(s_v.h, vq)
                    e.wait_ge(s_a.h, pa)
                    for j in range(4):
                        dc = 4 * k + j
                        li = e.transpose(tbank[:, 128 * j:128 * (j + 1)],
                                         b.td[:, 128 * dc:128 * (dc + 1)], b.idf[:])
                    li.then_inc(s_t.h, 1)
                ph.em("tensor", f_tr2)
                s_a.v += 1
                def f_ev2(e, k=k, tbank=tbank, t_tr=t_tr, I=I):
                    e.wait_ge(s_t.h, t_tr)
                    out = bass.AP(b.r2T, (4 * k) * T + 128 * I,
                                  [[NDC * T, 128], [T, 4], [1, 128]])
                    e.activation(out=out, in_=tbank[:], func=ACT.Copy,
                                 bias=0.0, scale=1.0).then_inc(s_a.h, 1)
                ph.em("scalar", f_ev2)
            tr_a[hseq - 1] = s_a.v
            tr_v[hseq - 1] = s_v.v
            gq_oadd[(hseq - 1) // 4] = s_v.v
            gq_pv_t[(hseq - 1) // 4] = s_t.v

    def p5(self, ph, pst):
        b = self
        s_ld, s_ld2, s_v, s_a, s_t, s_st = b.s_ld, b.s_ld2, b.s_v, b.s_a, b.s_t, b.s_st
        ps_wo = [b.psum(f"p5wo{i}", [128, 512], F32, pst) for i in range(NT)]
        wot = [b.sbuf(f"p5wot{i}", [128, 512], F32, stack=pst, side="right") for i in range(2)]
        woev = b.sbuf("p5woev", [128, 512], F32, stack=pst, side="right")
        conv_v = {}
        mm_t = {}
        ld_v = {}
        for dm in range(4):
            evprev_a = s_a.v
            evprev_st = s_st.v
            for dc in range(NDC):
                i = dm * NDC + dc
                buf = i % 2
                s_ld.v += 16
                ld_v[i] = s_ld.v
                def f_ld(e, dm=dm, dc=dc, buf=buf, wc=conv_v.get(i - 2, 0)):
                    e.wait_ge(s_v.h, wc)
                    e.dma_start(out=wot[buf][:],
                                in_=b.woT[128 * dc:128 * (dc + 1), 512 * dm:512 * (dm + 1)]
                                ).then_inc(s_ld.h, 16)
                ph.em("sync", f_ld)
                s_v.v += 2
                conv_v[i] = s_v.v
                def f_cv(e, buf=buf, ldv=ld_v[i], wm=mm_t.get(i - 2, 0)):
                    e.wait_ge(s_ld.h, ldv)
                    e.wait_ge(s_t.h, wm)
                    e.tensor_copy(b.stage[:, 512 * buf:512 * (buf + 1)], wot[buf][:]).then_inc(s_v.h, 1)
                    e.tensor_sub(b.stage2[:, 512 * buf:512 * (buf + 1)], wot[buf][:],
                                 b.stage[:, 512 * buf:512 * (buf + 1)]).then_inc(s_v.h, 1)
                ph.em("vector", f_cv)
                s_t.v += 1
                mm_t[i] = s_t.v
                def f_mm(e, dm=dm, dc=dc, buf=buf, cv=conv_v[i], ea=evprev_a):
                    e.wait_ge(s_v.h, cv)
                    e.wait_ge(s_a.h, ea)
                    for tt in range(NT):
                        lhsT = b.r2T[:, dc * T + 128 * tt:dc * T + 128 * (tt + 1)]
                        for w_i, w in enumerate((b.stage[:, 512 * buf:512 * (buf + 1)],
                                                 b.stage2[:, 512 * buf:512 * (buf + 1)])):
                            li = e.matmul(ps_wo[tt][:], lhsT, w,
                                          start=(dc == 0 and w_i == 0), stop=(dc == NDC - 1 and w_i == 1))
                    li.then_inc(s_t.h, 1)
                ph.em("tensor", f_mm)
            for tt in range(NT):
                s_a.v += 1
                a_ev = s_a.v
                def f_ev(e, tt=tt, dm=dm, mmv=s_t.v, pst_=b.s_ld3.v):
                    e.wait_ge(s_t.h, mmv)
                    e.wait_ge(b.s_ld3.h, pst_)
                    e.activation(out=woev[:], in_=ps_wo[tt][:], func=ACT.Copy, bias=0.0,
                                 scale=b.ds2_cols[:, tt:tt + 1]).then_inc(s_a.h, 1)
                ph.em("scalar", f_ev)
                b.s_ld3.v += 16
                def f_acc(e, tt=tt, dm=dm, a_ev=a_ev):
                    e.wait_ge(s_a.h, a_ev)
                    e.dma_start(out=b.y_out[128 * tt:128 * (tt + 1), 512 * dm:512 * (dm + 1)],
                                in_=woev[:], accum_op=ALU.add).then_inc(b.s_ld3.h, 16)
                ph.em("gpsimd", f_acc)

    def declare_pC(self, stack):
        sb = self.sbuf
        self.xnthi = sb("xnthi", [128, NDC * 512], BF16, stack=stack, side="right")
        self.xntlo = sb("xntlo", [128, NDC * 512], BF16, stack=stack, side="right")
        self.gubuf_hi = sb("gubuf_hi", [128, 11 * 512], BF16, stack=stack, side="right")
        self.gubuf_lo = sb("gubuf_lo", [128, 11 * 512], BF16, stack=stack, side="right")
        self.wgi = [sb(f"wgi{i}", [128, D], I32, stack=stack, side="right") for i in range(2)]
        self.wui = [sb(f"wui{i}", [128, D], I32, stack=stack, side="right") for i in range(2)]
        self.wgbf = [sb(f"wgbf{i}", [128, D], BF16, stack=stack, side="right") for i in range(2)]
        self.wubf = [sb(f"wubf{i}", [128, D], BF16, stack=stack, side="right") for i in range(2)]
        self.widm = [sb(f"widm{i}", [128, 11 * 128], I32, stack=stack, side="right") for i in range(2)]
        self.wdbf = [sb(f"wdbf{i}", [128, 11 * 128], BF16, stack=stack, side="right") for i in range(2)]
        self.gsil = sb("gsil", [128, 512], F32, stack=stack, side="right")
        self.gu32 = sb("gu32", [128, 512], F32, stack=stack, side="right")
        self.ffev = sb("ffev", [128, 512], F32, stack=stack, side="right")
        self.fftr = sb("fftr", [128, 512], F32, stack=stack, side="right")

    def p6(self, ph, pst):
        b = self
        s_ld, s_ld2, s_v, s_a, s_t, s_st = b.s_ld, b.s_ld2, b.s_v, b.s_a, b.s_t, b.s_st
        ps_g_ = [b.psum(f"p6g{i}", [128, 512], F32, pst) for i in range(2)]
        ps_u_ = [b.psum(f"p6u{i}", [128, 512], F32, pst) for i in range(2)]
        ps_d = [b.psum(f"p6d{i}", [128, 512], F32, pst) for i in range(2)]
        trp6 = [b.psum(f"p6tr{i}", [128, 512], F32, pst) for i in range(2)]

        NQ = 4
        HCQ = NHC // NQ

        for th in range(2):
            s_ld2.v += 32
            xn_ld = s_ld2.v
            prev_all = (s_t.v, s_v.v, s_a.v)
            def f_xn(e, th=th, prev_all=prev_all):
                e.wait_ge(s_t.h, prev_all[0])
                e.wait_ge(s_v.h, prev_all[1])
                e.wait_ge(s_a.h, prev_all[2])
                for src, dst in ((b.xnhi_d, b.xnthi), (b.xnlo_d, b.xntlo)):
                    in_ = bass.AP(src, 512 * th, [[T, 128], [128 * T, NDC], [1, 512]])
                    e.dma_start(out=dst[:], in_=in_).then_inc(s_ld2.h, 16)
            ph.em("scalar", f_xn)
            guv = {}
            gua = {}
            ldg = {}
            cvv = {}
            mmgu_t = {}
            for q in range(NQ):
                for hc in range(HCQ):
                    hqc = q * HCQ + hc
                    buf = hc % 2
                    s_ld.v += 32
                    ldg[hqc] = s_ld.v
                    def f_ld(e, hqc=hqc, buf=buf, wc=cvv.get(hqc - 2, 0)):
                        e.wait_ge(s_v.h, wc)
                        for w_in, dst in ((b.wgT, b.wgi[buf]), (b.wuT, b.wui[buf])):
                            in_ = bass.AP(w_in, 128 * hqc, [[HID, 128], [128 * HID, NDC], [1, 128]])
                            e.dma_start(out=dst[:], in_=in_).then_inc(s_ld.h, 16)
                    ph.em("sync", f_ld)
                    s_v.v += 2
                    cvv[hqc] = s_v.v
                    def f_cv(e, buf=buf, ldv=ldg[hqc], wm=mmgu_t.get(hqc - 2, 0)):
                        e.wait_ge(s_ld.h, ldv)
                        e.wait_ge(s_t.h, wm)
                        e.tensor_scalar(b.wgbf[buf][:], b.wgi[buf][:], 128.0, None,
                                        op0=ALU.subtract).then_inc(s_v.h, 1)
                        e.tensor_scalar(b.wubf[buf][:], b.wui[buf][:], 128.0, None,
                                        op0=ALU.subtract).then_inc(s_v.h, 1)
                    ph.em("vector", f_cv)
                    s_t.v += 1
                    mmgu_t[hqc] = s_t.v
                    def f_mm(e, th=th, buf=buf, cv=cvv[hqc], xn_ld=xn_ld,
                             pg=gua.get(hqc - 2, 0), pv=guv.get(hqc - 2, 0)):
                        e.wait_ge(s_v.h, cv)
                        e.wait_ge(s_ld2.h, xn_ld)
                        e.wait_ge(s_a.h, pg)
                        e.wait_ge(s_v.h, pv)
                        for W, psx in ((b.wgbf[buf], ps_g_[buf]), (b.wubf[buf], ps_u_[buf])):
                            for dc in range(NDC):
                                for xn in (b.xnthi, b.xntlo):
                                    li = e.matmul(psx[:], W[:, 128 * dc:128 * (dc + 1)],
                                                  xn[:, 512 * dc:512 * (dc + 1)],
                                                  start=(dc == 0 and xn is b.xnthi),
                                                  stop=(dc == NDC - 1 and xn is b.xntlo))
                        li.then_inc(s_t.h, 1)
                    ph.em("tensor", f_mm)
                    s_a.v += 1
                    a_sil = s_a.v
                    def f_sil(e, hqc=hqc, buf=buf, mm=mmgu_t[hqc], pv=guv.get(hqc - 1, 0)):
                        e.wait_ge(s_t.h, mm)
                        e.wait_ge(s_v.h, pv)
                        e.activation(out=b.gsil[:], in_=ps_g_[buf][:], func=ACT.Silu, bias=0.0,
                                     scale=b.sgsb[:, hqc:hqc + 1]).then_inc(s_a.h, 1)
                    ph.em("scalar", f_sil)
                    s_v.v += 1
                    v_gu = s_v.v
                    def f_gu(e, hqc=hqc, buf=buf, a_sil=a_sil):
                        e.wait_ge(s_a.h, a_sil)
                        e.scalar_tensor_tensor(b.gu32[:], ps_u_[buf][:], b.susb[:, hqc:hqc + 1],
                                               b.gsil[:], op0=ALU.mult, op1=ALU.mult).then_inc(s_v.h, 1)
                    ph.em("vector", f_gu)
                    s_a.v += 1
                    a_hi = s_a.v
                    def f_hi(e, hc=hc, v_gu=v_gu):
                        e.wait_ge(s_v.h, v_gu)
                        e.activation(out=b.gubuf_hi[:, 512 * hc:512 * (hc + 1)], in_=b.gu32[:],
                                     func=ACT.Copy, bias=0.0, scale=1.0).then_inc(s_a.h, 1)
                    ph.em("scalar", f_hi)
                    gua[hqc] = s_a.v
                    s_v.v += 1
                    def f_lo(e, hc=hc, a_hi=a_hi):
                        e.wait_ge(s_a.h, a_hi)
                        e.tensor_sub(b.gubuf_lo[:, 512 * hc:512 * (hc + 1)], b.gu32[:],
                                     b.gubuf_hi[:, 512 * hc:512 * (hc + 1)]).then_inc(s_v.h, 1)
                    ph.em("vector", f_lo)
                    guv[hqc] = s_v.v
                gu_all_v = s_v.v
                gu_all_a = s_a.v
                dconv = {}
                dmm = {}
                for dm in range(NDC):
                    buf = dm % 2
                    s_ld.v += 16
                    dl = s_ld.v
                    def f_ld2(e, q=q, dm=dm, buf=buf, wc=dconv.get(dm - 2, 0)):
                        e.wait_ge(s_v.h, wc)
                        in_ = bass.AP(b.wdT, (q * HCQ * 128) * D + 128 * dm,
                                      [[D, 128], [128 * D, HCQ], [1, 128]])
                        e.dma_start(out=b.widm[buf][:], in_=in_).then_inc(s_ld.h, 16)
                    ph.em("sync", f_ld2)
                    s_v.v += 1
                    dconv[dm] = s_v.v
                    def f_cv2(e, buf=buf, dl=dl, wm=dmm.get(dm - 2, 0)):
                        e.wait_ge(s_ld.h, dl)
                        e.wait_ge(s_t.h, wm)
                        e.tensor_scalar(b.wdbf[buf][:], b.widm[buf][:], 128.0, None,
                                        op0=ALU.subtract).then_inc(s_v.h, 1)
                    ph.em("vector", f_cv2)
                    s_t.v += 1
                    dmm[dm] = s_t.v
                    def f_dmm(e, dm=dm, buf=buf, cv=dconv[dm], guv_=gu_all_v, gua_=gu_all_a,
                              pe=s_a.v):
                        e.wait_ge(s_v.h, cv)
                        e.wait_ge(s_v.h, guv_)
                        e.wait_ge(s_a.h, gua_)
                        e.wait_ge(s_a.h, pe)
                        for hc in range(HCQ):
                            for gu in (b.gubuf_hi, b.gubuf_lo):
                                li = e.matmul(ps_d[buf][:], b.wdbf[buf][:, 128 * hc:128 * (hc + 1)],
                                              gu[:, 512 * hc:512 * (hc + 1)],
                                              start=(hc == 0 and gu is b.gubuf_hi),
                                              stop=(hc == HCQ - 1 and gu is b.gubuf_lo))
                        li.then_inc(s_t.h, 1)
                    ph.em("tensor", f_dmm)
                    s_a.v += 1
                    a_ev = s_a.v
                    def f_ev(e, dm=dm, buf=buf, mm=dmm[dm], ptr=s_t.v):
                        e.wait_ge(s_t.h, mm)
                        e.wait_ge(s_t.h, ptr)
                        e.activation(out=b.ffev[:], in_=ps_d[buf][:], func=ACT.Copy, bias=0.0,
                                     scale=b.sdsb[:, dm:dm + 1]).then_inc(s_a.h, 1)
                    ph.em("scalar", f_ev)
                    s_t.v += 1
                    t_tr0 = s_t.v
                    def f_tr(e, a_ev=a_ev, pa=s_a.v):
                        e.wait_ge(s_a.h, a_ev)
                        e.wait_ge(s_a.h, pa)
                        for j in range(4):
                            li = e.transpose(trp6[0][:, 128 * j:128 * (j + 1)],
                                             b.ffev[:, 128 * j:128 * (j + 1)], b.idf[:])
                        li.then_inc(s_t.h, 1)
                    ph.em("tensor", f_tr)
                    s_a.v += 1
                    a_ev2 = s_a.v
                    def f_ev2(e, t_tr=t_tr0, pst_=b.s_ld3.v):
                        e.wait_ge(s_t.h, t_tr)
                        e.wait_ge(b.s_ld3.h, pst_)
                        e.activation(out=b.fftr[:], in_=trp6[0][:], func=ACT.Copy, bias=0.0,
                                     scale=1.0).then_inc(s_a.h, 1)
                    ph.em("scalar", f_ev2)
                    b.s_ld3.v += 16
                    def f_acc(e, th=th, dm=dm, a_ev2=a_ev2):
                        e.wait_ge(s_a.h, a_ev2)
                        out = bass.AP(b.y_out, (512 * th) * D + 128 * dm,
                                      [[D, 128], [128 * D, 4], [1, 128]])
                        e.dma_start(out=out, in_=b.fftr[:], accum_op=ALU.add).then_inc(b.s_ld3.h, 16)
                    ph.em("gpsimd", f_acc)



def make_core_inputs(inputs, core):
    x = np.asarray(inputs["L0_attn_x"])[0]
    wqkv = np.asarray(inputs["L0_attn_wqkv"])
    wo = np.asarray(inputs["L0_attn_wo"])
    nw = np.asarray(inputs["L0_attn_norm_w"])
    wg = np.asarray(inputs["L0_ffn_wq_gate"])
    sgv = np.asarray(inputs["L0_ffn_s_gate"])
    wu = np.asarray(inputs["L0_ffn_wq_up"])
    suv = np.asarray(inputs["L0_ffn_s_up"])
    wd = np.asarray(inputs["L0_ffn_wq_down"])
    sdv = np.asarray(inputs["L0_ffn_s_down"])

    shared = dict(
        wqkvT=np.ascontiguousarray(wqkv.T),
        woT=np.ascontiguousarray(wo.T),
        nwrep=np.ascontiguousarray(
            np.broadcast_to(nw.reshape(NH, 128).T[:, :, None], (128, NH, 128)).reshape(128, D)
        ).astype(np.float32),
        wgT=np.ascontiguousarray(wg.T),
        wuT=np.ascontiguousarray(wu.T),
        wdT=np.ascontiguousarray(wd.T),
        sg=np.ascontiguousarray(sgv.reshape(NHC, 128).T).astype(np.float32),
        su=np.ascontiguousarray(suv.reshape(NHC, 128).T).astype(np.float32),
        sd=np.ascontiguousarray(sdv.reshape(NDC, 128).T).astype(np.float32),
        xfr=np.ascontiguousarray((-(np.arange(128) // 2) / 64.0).reshape(128, 1)).astype(np.float32),
        sgn=np.ascontiguousarray(np.where(np.arange(128) % 2 == 0, -1.0, 1.0).reshape(128, 1)).astype(np.float32),
        idf=np.eye(128, dtype=np.float32),
        onesf=np.ones((128, 128), dtype=np.float32),
        onec=np.ones((128, 1), dtype=np.float32),
    )
    xc = np.ascontiguousarray(x[core::NC])
    pos = (core + NC * np.arange(T)).astype(np.float32).reshape(1, T)
    i_idx = np.arange(128)[:, None]
    r_idx = (np.arange(1024)[None, :] // 128)
    m_idx = (np.arange(1024)[None, :] % 128)
    allowed = (m_idx < i_idx) | ((m_idx == i_idx) & (r_idx <= core))
    mask = np.where(allowed, 0.0, NEG_BIG).astype(np.float32)
    return dict(shared, x=xc, pos=pos, mask=mask)


def build_full(cfg=None):
    b = Build(cfg or dict(debug_outputs={}))
    ph = Phase(b.nc)
    b.p0(ph)
    ph.run()
    with ExitStack() as sA:
        b.declare_pA(sA)
        with ExitStack() as sP1:
            ph = Phase(b.nc); b.p1(ph, sP1); ph.run()
            ph = Phase(b.nc); b.p15(ph, sP1); ph.run()
        with ExitStack() as sP2:
            ph = Phase(b.nc); b.guard_all(ph); b.p2(ph, sP2); ph.run()
            ph = Phase(b.nc); b.p3(ph); ph.run()
    with ExitStack() as sB:
        b.declare_pB(sB)
        with ExitStack() as sP4:
            ph = Phase(b.nc); b.guard_all(ph); b.p4(ph, sP4); ph.run()
        with ExitStack() as sP5:
            ph = Phase(b.nc); b.guard_all(ph); b.p5(ph, sP5); ph.run()
    with ExitStack() as sC:
        b.declare_pC(sC)
        with ExitStack() as sP6:
            ph = Phase(b.nc); b.guard_all(ph); b.p6(ph, sP6); ph.run()
    ph = Phase(b.nc)
    b.guard_all(ph)
    ph.run()
    b.es.close()
    return b


def kernel(**inputs):
    b = build_full()
    in_maps = [make_core_inputs(inputs, c) for c in range(NC)]
    res = run_bass_kernel_spmd(b.nc, in_maps, core_ids=list(range(NC)))
    y = np.empty((S, D), dtype=np.float32)
    for c in range(NC):
        y[c::NC] = res.results[c]["y"]
    return y.reshape(1, S, D)


# version 13
# speedup vs baseline: 1.0858x; 1.0151x over previous
import numpy as np
import ml_dtypes
from contextlib import ExitStack

import concourse.bass as bass
import concourse.mybir as mybir
from concourse.bass_utils import run_bass_kernel_spmd

F32 = mybir.dt.float32
BF16 = mybir.dt.bfloat16
I32 = mybir.dt.int32
ALU = mybir.AluOpType
ACT = mybir.ActivationFunctionType
AXX = mybir.AxisListType.X

S, D, NH, NKV, HD = 8192, 2048, 16, 4, 128
HQKV, HID, NC = 3072, 5632, 8
T = S // NC
NT = T // 128
NDC = D // 128
NHC = HID // 128
EPS = 1e-5
NEG_BIG = -30000.0
SCORE_SCALE = float(1.0 / np.float32(np.sqrt(np.float64(128.0))))
THETA = 500000.0

ENGS = ("sync", "vector", "scalar", "tensor", "gpsimd")


def _cw2pi():
    two_pi = 2.0 * np.pi
    c1 = float(np.float32(np.round(two_pi * 512) / 512.0))
    r = two_pi - c1
    c2 = float(np.float32(np.round(r * 2.0**32) / 2.0**32))
    c3 = float(np.float32(two_pi - c1 - c2))
    return c1, c2, c3


def _halfpi2():
    hp = 0.5 * np.pi
    a = float(np.float32(hp))
    b = float(np.float32(hp - a))
    return a, b


class Sem:
    def __init__(self, es, nc, name):
        self.h = es.enter_context(nc.semaphore(name))
        self.v = 0


class Phase:

    def __init__(self, nc):
        self.nc = nc
        self.ops = {k: [] for k in ENGS}

    def em(self, eng, fn):
        self.ops[eng].append(fn)

    def run(self):
        with self.nc.Block() as block:
            for eng in ENGS:
                fns = self.ops[eng]
                if not fns:
                    continue
                getattr(block, eng)(lambda e, fns=fns: [f(e) for f in fns])


class Build:

    def __init__(self, cfg):
        self.cfg = cfg
        self.nc = bass.Bass("TRN2", target_bir_lowering=False, detect_race_conditions=False)
        self.es = ExitStack()
        self._declare_io()
        self._declare_sems()
        self._declare_persistent()

    def _declare_io(self):
        nc = self.nc
        self.x_in = nc.dram_tensor("x", [T, D], F32, kind="ExternalInput")
        self.wqkvT = nc.dram_tensor("wqkvT", [D, HQKV], F32, kind="ExternalInput")
        self.woT = nc.dram_tensor("woT", [D, D], F32, kind="ExternalInput")
        self.nwrep = nc.dram_tensor("nwrep", [128, D], F32, kind="ExternalInput")
        self.wgT = nc.dram_tensor("wgT", [D, HID], I32, kind="ExternalInput")
        self.wuT = nc.dram_tensor("wuT", [D, HID], I32, kind="ExternalInput")
        self.wdT = nc.dram_tensor("wdT", [HID, D], I32, kind="ExternalInput")
        self.sg_in = nc.dram_tensor("sg", [128, NHC], F32, kind="ExternalInput")
        self.su_in = nc.dram_tensor("su", [128, NHC], F32, kind="ExternalInput")
        self.sd_in = nc.dram_tensor("sd", [128, NDC], F32, kind="ExternalInput")
        self.pos_in = nc.dram_tensor("pos", [1, T], F32, kind="ExternalInput")
        self.mask_in = nc.dram_tensor("mask", [128, 1024], F32, kind="ExternalInput")
        self.xfr_in = nc.dram_tensor("xfr", [128, 1], F32, kind="ExternalInput")
        self.sgn_in = nc.dram_tensor("sgn", [128, 1], F32, kind="ExternalInput")
        self.idf_in = nc.dram_tensor("idf", [128, 128], F32, kind="ExternalInput")
        self.onesf_in = nc.dram_tensor("onesf", [128, 128], F32, kind="ExternalInput")
        self.onec_in = nc.dram_tensor("onec", [128, 1], F32, kind="ExternalInput")
        self.y_out = nc.dram_tensor("y", [T, D], F32, kind="ExternalOutput")
        self.ag_in = nc.dram_tensor("ag_in", [2048, T], BF16)
        self.ag_out = nc.dram_tensor("ag_out", [NC * 2048, T], BF16)
        self.xnhi_d = nc.dram_tensor("xnhi_d", [D, T], BF16)
        self.xnlo_d = nc.dram_tensor("xnlo_d", [D, T], BF16)
        self.qhi_d = nc.dram_tensor("qhi_d", [D, T], BF16)
        self.qlo_d = nc.dram_tensor("qlo_d", [D, T], BF16)
        self.row_d = nc.dram_tensor("row_d", [1, 2048], F32)
        self.dbg = {}
        for name, shape in self.cfg.get("debug_outputs", {}).items():
            self.dbg[name] = self.nc.dram_tensor(name, list(shape), F32, kind="ExternalOutput")

    def _declare_sems(self):
        es, nc = self.es, self.nc
        self.s_ld = Sem(es, nc, "s_ld")
        self.s_ld2 = Sem(es, nc, "s_ld2")
        self.s_ld3 = Sem(es, nc, "s_ld3")
        self.s_v = Sem(es, nc, "s_v")
        self.s_a = Sem(es, nc, "s_a")
        self.s_t = Sem(es, nc, "s_t")
        self.s_g = Sem(es, nc, "s_g")
        self.s_st = Sem(es, nc, "s_st")
        self.all_sems = [self.s_ld, self.s_ld2, self.s_ld3, self.s_v, self.s_a,
                         self.s_t, self.s_g, self.s_st]

    def sbuf(self, name, shape, dtype, stack=None, side=None):
        kw = {} if side is None else {"side": side}
        return (stack or self.es).enter_context(self.nc.sbuf_tensor(name, shape, dtype, **kw))

    def psum(self, name, shape, dtype, stack):
        return stack.enter_context(self.nc.psum_tensor(name, shape, dtype))

    def _declare_persistent(self):
        sb = self.sbuf
        self.dq_cols = sb("dq_cols", [128, NT], F32)
        self.ds2_cols = sb("ds2_cols", [128, NT], F32)
        self.masksb = sb("masksb", [128, 1024], F32)
        self.nwsb = sb("nwsb", [128, D], F32)
        self.sgsb = sb("sgsb", [128, NHC], F32)
        self.susb = sb("susb", [128, NHC], F32)
        self.sdsb = sb("sdsb", [128, NDC], F32)
        self.xfrsb = sb("xfrsb", [128, 1], F32)
        self.sgnsb = sb("sgnsb", [128, 1], F32)
        self.idf = sb("idf_s", [128, 128], F32)
        self.onescol = sb("onescol", [128, 1], F32)
        self.onesrow1 = sb("onesrow1", [1, 128], F32)
        self.ta = sb("ta", [128, D], F32)
        self.tb = sb("tb", [128, D], F32)
        self.tc_ = sb("tc_", [128, D], F32)
        self.ti = sb("ti", [128, D], I32)
        self.td = sb("td", [128, D], F32)
        self.te = sb("te", [128, D], F32)
        self.st24 = sb("st24", [128, 24], F32)
        self.st8 = sb("st8", [128, 8], F32)
        self.c1 = sb("c1", [128, 1], F32)
        self.c2 = sb("c2", [128, 1], F32)
        self.c3 = sb("c3", [128, 1], F32)
        self.c4 = sb("c4", [128, 1], F32)
        self.stage = sb("stage", [128, D], BF16)
        self.stage2 = sb("stage2", [128, D], BF16)
        self.evrow = sb("evrow", [16, 128], F32)
        self.rowa = sb("rowa", [1, 128], F32)
        self.rowb = sb("rowb", [1, 128], F32)
        self.rowc = sb("rowc", [1, 128], F32)
        self.epsc = sb("epsc", [128, 1], F32)
        self.j0 = sb("j0", [128, 1], F32)
        self.j1 = sb("j1", [128, 1], F32)
        Build._sp_j0 = self.j0[:]
        Build._sp_j1 = self.j1[:]

    def declare_pA(self, stack):
        sb = self.sbuf
        self.rT = sb("rT", [128, NDC * T], BF16, stack=stack, side="right")
        self.dqb = sb("dqb", [128, T], F32, stack=stack, side="right")
        self.possb = sb("possb", [128, T], F32, stack=stack, side="right")
        self.ctabQ = sb("ctabQ", [128, T], F32, stack=stack, side="right")
        self.stabQ = sb("stabQ", [128, T], F32, stack=stack, side="right")
        self.ctabK = sb("ctabK", [128, T], F32, stack=stack, side="right")
        self.stabK = sb("stabK", [128, T], F32, stack=stack, side="right")

    @staticmethod
    def sp(e):
        e.tensor_copy(Build._sp_j1, Build._sp_j0)

    def dump_now(self, ph, names):
        for nm in names:
            t = getattr(self, nm)
            shp = list(t.ap().shape)
            d = self.nc.dram_tensor("dbg_" + nm, shp, t.ap().dtype, kind="ExternalOutput")
            self.s_st.v += 16
            def f(e, t=t, d=d):
                e.dma_start(out=d.ap(), in_=t.ap()).then_inc(self.s_st.h, 16)
            ph.em("sync", f)

    def guard_all(self, ph):
        vals = [(s.h, s.v) for s in self.all_sems if s.v > 0]
        for eng in ENGS:
            def f(e, vals=vals):
                for h, v in vals:
                    e.wait_ge(h, v)
            ph.em(eng, f)

    def p0(self, ph):
        b = self
        s_ld = b.s_ld
        loads = [
            (b.masksb[:], b.mask_in[:, :]),
            (b.nwsb[:], b.nwrep[:, :]),
            (b.sgsb[:], b.sg_in[:, :]),
            (b.susb[:], b.su_in[:, :]),
            (b.sdsb[:], b.sd_in[:, :]),
            (b.xfrsb[:], b.xfr_in[:, :]),
            (b.sgnsb[:], b.sgn_in[:, :]),
            (b.idf[:], b.idf_in[:, :]),
            (b.onescol[:], b.onec_in[:, :]),
            (b.onesrow1[:], b.onesf_in[0:1, 0:128]),
        ]
        def f(e):
            for out, in_ in loads:
                e.dma_start(out=out, in_=in_).then_inc(s_ld.h, 16)
        ph.em("sync", f)
        s_ld.v += 16 * len(loads)
        b.s_v.v += 1
        ph.em("vector", lambda e: (e.memset(b.epsc[:], EPS), e.memset(b.j0[:], 0.0),
                                   e.memset(b.j1[:], 0.0).then_inc(b.s_v.h, 1)))

    def p1(self, ph, pst):
        b = self
        nc = self.nc
        rT = b.rT
        b.xs = b.sbuf("xs", [128, D], F32, stack=pst)
        trp = [b.psum(f"p1trp{i}", [128, 512], F32, pst) for i in range(2)]
        s_ld, s_ld2, s_v, s_a, s_t, s_st = b.s_ld, b.s_ld2, b.s_v, b.s_a, b.s_t, b.s_st
        skip = self.cfg.get("p1_skip", set())
        n_tiles = self.cfg.get("p1_tiles", NT)

        for it in range(n_tiles):
            prev_v, prev_a, prev_st, prev_t = s_v.v, s_a.v, s_st.v, s_t.v
            s_ld.v += 16
            ldv = s_ld.v
            ph.em("sync", lambda e, it=it, prev_v=prev_v, prev_a=prev_a, prev_st=prev_st, ldv=ldv: (
                e.wait_ge(s_v.h, prev_v),
                e.wait_ge(s_a.h, prev_a),
                e.wait_ge(s_st.h, prev_st),
                e.dma_start(out=b.xs[:], in_=b.x_in[128 * it:128 * (it + 1), :]).then_inc(s_ld.h, 16),
            ))
            s_st.v += 16
            ph.em("scalar", lambda e, it=it, ldv=ldv: (
                e.wait_ge(s_ld.h, ldv),
                e.dma_start(out=b.y_out[128 * it:128 * (it + 1), :], in_=b.xs[:]).then_inc(s_st.h, 16),
            ))
            s_v.v += 1
            v_stats = s_v.v
            def f_stats(e, ldv=ldv):
                e.wait_ge(s_ld.h, ldv)
                e.tensor_reduce(b.c1[:], b.xs[:], axis=AXX, op=ALU.add)
                e.tensor_mul(b.te[:], b.xs[:], b.xs[:])
                e.tensor_reduce(b.c3[:], b.te[:], axis=AXX, op=ALU.add)
                Build.sp(e)
                e.tensor_scalar_mul(b.c1[:], b.c1[:], 1.0 / D)
                Build.sp(e)
                e.tensor_scalar_mul(b.c3[:], b.c3[:], 1.0 / D)
                e.tensor_mul(b.c2[:], b.c1[:], b.c1[:])
                Build.sp(e)
                e.tensor_sub(b.c3[:], b.c3[:], b.c2[:]).then_inc(s_v.h, 1)
            ph.em("vector", f_stats)
            s_a.v += 1
            a_rsq = s_a.v
            def f_rsq(e, v_stats=v_stats):
                e.wait_ge(s_v.h, v_stats)
                e.activation(out=b.c4[:], in_=b.c3[:], func=ACT.Sqrt, bias=b.epsc[:, 0:1], scale=1.0).then_inc(s_a.h, 1)
            ph.em("scalar", f_rsq)
            s_v.v += 1
            v_nmr = s_v.v
            def f_nmr(e, a_rsq=a_rsq):
                e.wait_ge(s_a.h, a_rsq)
                e.reciprocal(b.c4[:], b.c4[:]).then_inc(s_v.h, 1)
            ph.em("vector", f_nmr)
            s_v.v += 1
            v_xn = s_v.v
            def f_xn(e, prev_t=prev_t):
                e.wait_ge(s_t.h, prev_t)
                e.tensor_scalar(b.te[:], b.xs[:], b.c1[:, 0:1], None, op0=ALU.subtract)
                e.tensor_scalar(b.ta[:], b.te[:], b.c4[:, 0:1], None, op0=ALU.mult).then_inc(s_v.h, 1)
            ph.em("vector", f_xn)
            a_xn = v_xn
            s_v.v += 1
            v_qs = s_v.v
            def f_qs(e, it=it):
                e.reduce_max(b.c1[:], b.ta[:], axis=AXX, apply_absolute_value=True)
                Build.sp(e)
                e.tensor_scalar_max(b.c1[:], b.c1[:], 1e-5)
                Build.sp(e)
                e.reciprocal(b.c2[:], b.c1[:])
                Build.sp(e)
                e.tensor_scalar_mul(b.c2[:], b.c2[:], 127.0)
                Build.sp(e)
                e.reciprocal(b.c3[:], b.c2[:])
                Build.sp(e)
                e.tensor_copy(b.dq_cols[:, it:it + 1], b.c3[:]).then_inc(s_v.h, 1)
            ph.em("vector", f_qs)
            s_v.v += 1
            def f_yq(e):
                e.tensor_scalar(b.tb[:], b.ta[:], b.c2[:, 0:1], None, op0=ALU.mult).then_inc(s_v.h, 1)
            ph.em("vector", f_yq)
            s_v.v += 1
            v_rr = s_v.v
            def f_rr(e, prev_t=prev_t):
                e.wait_ge(s_t.h, prev_t)
                e.tensor_copy(b.ti[:], b.tb[:])
                e.tensor_copy(b.tc_[:], b.ti[:])
                e.tensor_sub(b.td[:], b.tb[:], b.tc_[:])
                e.tensor_single_scalar(b.te[:], b.td[:], 0.5, op=ALU.is_ge)
                e.tensor_add(b.tc_[:], b.tc_[:], b.te[:])
                e.tensor_single_scalar(b.te[:], b.td[:], -0.5, op=ALU.is_le)
                e.tensor_sub(b.td[:], b.tc_[:], b.te[:]).then_inc(s_v.h, 1)
            ph.em("vector", f_rr)
            for src_i in ([] if "tr" in skip else [i_ for i_ in range(2) if f"tr{i_}" not in skip]):
                for k in range(4):
                    bank = trp[(it * 8 + src_i * 4 + k) % 2]
                    src_sem, src_val = (s_v, v_rr) if src_i == 0 else (s_v, a_xn)
                    s_t.v += 1
                    t_tr = s_t.v
                    need_a = s_a.v
                    need_v = s_v.v
                    def f_tr(e, k=k, bank=bank, src_i=src_i, src_sem=src_sem,
                             src_val=src_val, need_a=need_a, need_v=need_v):
                        e.wait_ge(src_sem.h, src_val)
                        e.wait_ge(s_a.h, need_a)
                        e.wait_ge(s_v.h, need_v)
                        src = b.td if src_i == 0 else b.ta
                        for j in range(4):
                            dc = 4 * k + j
                            li = e.transpose(bank[:, 128 * j:128 * (j + 1)],
                                             src[:, 128 * dc:128 * (dc + 1)], b.idf[:])
                        li.then_inc(s_t.h, 1)
                    ph.em("tensor", f_tr)
                    if src_i == 0:
                        s_a.v += 1
                        def f_ev(e, k=k, bank=bank, it=it, t_tr=t_tr):
                            e.wait_ge(s_t.h, t_tr)
                            out = bass.AP(rT, (4 * k) * T + 128 * it,
                                          [[NDC * T, 128], [T, 4], [1, 128]])
                            e.activation(out=out, in_=bank[:], func=ACT.Copy,
                                         bias=0.0, scale=1.0).then_inc(s_a.h, 1)
                        ph.em("scalar", f_ev)
                    else:
                        s_a.v += 1
                        a_hi = s_a.v
                        def f_hi(e, k=k, bank=bank, t_tr=t_tr, prev_st=prev_st):
                            e.wait_ge(s_t.h, t_tr)
                            e.wait_ge(s_st.h, prev_st)
                            e.activation(out=b.stage[:, 512 * k:512 * (k + 1)], in_=bank[:],
                                         func=ACT.Copy, bias=0.0, scale=1.0).then_inc(s_a.h, 1)
                        ph.em("scalar", f_hi)
                        s_v.v += 1
                        def f_lo(e, k=k, bank=bank, a_hi=a_hi):
                            e.wait_ge(s_a.h, a_hi)
                            e.tensor_sub(b.stage2[:, 512 * k:512 * (k + 1)], bank[:],
                                         b.stage[:, 512 * k:512 * (k + 1)]).then_inc(s_v.h, 1)
                        ph.em("vector", f_lo)
            if "spill" in skip or "tr" in skip or "tr1" in skip:
                continue
            s_st.v += 32
            need_v = s_v.v
            need_a = s_a.v
            def f_spill(e, it=it, need_v=need_v, need_a=need_a):
                e.wait_ge(s_v.h, need_v)
                e.wait_ge(s_a.h, need_a)
                for src, dst in ((b.stage, b.xnhi_d), (b.stage2, b.xnlo_d)):
                    out = bass.AP(dst, 128 * it, [[T, 128], [128 * T, 16], [1, 128]])
                    in_ = bass.AP(src, 0, [[D, 128], [128, 16], [1, 128]])
                    e.dma_start(out=out, in_=in_).then_inc(s_st.h, 16)
            ph.em("sync", f_spill)

    def p15(self, ph, pst):
        b = self
        s_v, s_a, s_t, s_ld2, s_ld = b.s_v, b.s_a, b.s_t, b.s_ld2, b.s_ld
        ptr = b.psum(f"p15tr", [128, 128], F32, pst)
        s_ld.v += 16
        ph.em("sync", lambda e: e.dma_start(
            out=b.possb[:], in_=bass.AP(b.pos_in, 0, [[0, 128], [1, T]])).then_inc(s_ld.h, 16))

        s_t.v += 1
        t_tr = s_t.v
        need_v = s_v.v
        ph.em("tensor", lambda e, t=t_tr, nv=need_v: (
            e.wait_ge(s_v.h, nv),
            e.transpose(ptr[0:NT, :], b.dq_cols[:], b.idf[:]).then_inc(s_t.h, 1),
        ))
        s_a.v += 1
        a_ev = s_a.v
        ph.em("scalar", lambda e, t=t_tr: (
            e.wait_ge(s_t.h, t),
            e.activation(out=b.evrow[0:NT, :], in_=ptr[0:NT, :], func=ACT.Copy,
                         bias=0.0, scale=1.0).then_inc(s_a.h, 1),
        ))
        s_ld2.v += 32
        ld2 = s_ld2.v
        ph.em("scalar", lambda e, a=a_ev, ld2=ld2: (
            e.wait_ge(s_a.h, a),
            e.dma_start(out=bass.AP(b.row_d, 0, [[128, NT], [1, 128]]),
                        in_=bass.AP(b.evrow, 0, [[128, NT], [1, 128]])).then_inc(s_ld2.h, 16),
            e.wait_ge(s_ld2.h, ld2 - 16),
            e.dma_start(out=b.dqb[:], in_=bass.AP(b.row_d, 0, [[0, 128], [1, T]])).then_inc(s_ld2.h, 16),
        ))

        c1, c2, c3 = _cw2pi()
        hp1, hp2 = _halfpi2()
        inv2pi = float(np.float32(1.0 / (2.0 * np.pi)))
        pi_clamp = 3.1415925
        prev_a0 = s_a.v
        s_v.v += 1
        v_ms = s_v.v
        def f_ms(e, prev_a0=prev_a0):
            e.wait_ge(s_a.h, prev_a0)
            e.memset(b.c1[:], THETA).then_inc(s_v.h, 1)
        ph.em("vector", f_ms)
        s_a.v += 1
        a_ln = s_a.v
        def f_ln(e, v_ms=v_ms):
            e.wait_ge(s_v.h, v_ms)
            e.activation(out=b.c2[:], in_=b.c1[:], func=ACT.Ln, bias=0.0, scale=1.0).then_inc(s_a.h, 1)
        ph.em("scalar", f_ln)
        s_v.v += 1
        v_fr = s_v.v
        def f_fr(e, a_ln=a_ln):
            e.wait_ge(s_a.h, a_ln)
            e.tensor_scalar(b.c3[:], b.xfrsb[:], b.c2[:, 0:1], None, op0=ALU.mult).then_inc(s_v.h, 1)
        ph.em("vector", f_fr)
        s_a.v += 1
        a_fr = s_a.v
        ph.em("scalar", lambda e, v=v_fr: (
            e.wait_ge(s_v.h, v),
            e.activation(out=b.c4[:], in_=b.c3[:], func=ACT.Exp, bias=0.0, scale=1.0).then_inc(s_a.h, 1),
        ))
        ld_now = b.s_ld.v
        for ipass in range(2):
            prev_a2 = s_a.v
            s_v.v += 1
            v_ang = s_v.v
            def f_ang(e, ipass=ipass, prev_a2=prev_a2, ld_now=ld_now):
                e.wait_ge(s_a.h, prev_a2)
                e.wait_ge(b.s_ld.h, ld_now)
                ap_t = b.possb[:, 0:T]
                e.tensor_scalar(b.ta[:, 0:T], ap_t, b.c4[:, 0:1], None, op0=ALU.mult)
                if ipass == 1:
                    e.tensor_scalar(b.ta[:, 0:T], b.ta[:, 0:T], hp1, hp2, op0=ALU.add, op1=ALU.add)
                e.tensor_scalar(b.tb[:, 0:T], b.ta[:, 0:T], inv2pi, None, op0=ALU.mult)
                e.tensor_copy(b.ti[:, 0:T], b.tb[:, 0:T])
                e.tensor_copy(b.tb[:, 0:T], b.ti[:, 0:T])
                e.scalar_tensor_tensor(b.tc_[:, 0:T], b.tb[:, 0:T], -c1, b.ta[:, 0:T],
                                       op0=ALU.mult, op1=ALU.add)
                e.scalar_tensor_tensor(b.tc_[:, 0:T], b.tb[:, 0:T], -c2, b.tc_[:, 0:T],
                                       op0=ALU.mult, op1=ALU.add)
                e.scalar_tensor_tensor(b.tc_[:, 0:T], b.tb[:, 0:T], -c3, b.tc_[:, 0:T],
                                       op0=ALU.mult, op1=ALU.add)
                e.tensor_single_scalar(b.td[:, 0:T], b.tc_[:, 0:T], float(np.float32(np.pi)), op=ALU.is_gt)
                e.tensor_single_scalar(b.te[:, 0:T], b.tc_[:, 0:T], float(np.float32(-np.pi)), op=ALU.is_lt)
                e.tensor_sub(b.td[:, 0:T], b.te[:, 0:T], b.td[:, 0:T])
                e.scalar_tensor_tensor(b.tc_[:, 0:T], b.td[:, 0:T], float(np.float32(2 * np.pi)),
                                       b.tc_[:, 0:T], op0=ALU.mult, op1=ALU.add)
                e.tensor_scalar_min(b.tc_[:, 0:T], b.tc_[:, 0:T], pi_clamp)
                e.tensor_scalar_max(b.tc_[:, 0:T], b.tc_[:, 0:T], -pi_clamp).then_inc(s_v.h, 1)
            ph.em("vector", f_ang)
            s_a.v += 1
            a_sin = s_a.v
            tgt = [b.stabK, b.ctabK][ipass]
            ph.em("scalar", lambda e, v=v_ang, tgt=tgt: (
                e.wait_ge(s_v.h, v),
                e.activation(out=tgt[:], in_=b.tc_[:, 0:T], func=ACT.Sin, bias=0.0, scale=1.0).then_inc(s_a.h, 1),
            ))
        s_v.v += 1
        def f_tab(e, a=s_a.v, ld2=s_ld2.v):
            e.wait_ge(s_a.h, a)
            e.wait_ge(s_ld2.h, ld2)
            e.tensor_scalar(b.stabK[:], b.stabK[:], b.sgnsb[:, 0:1], None, op0=ALU.mult)
            e.tensor_mul(b.stabK[:], b.stabK[:], b.dqb[:])
            e.tensor_mul(b.ctabK[:], b.ctabK[:], b.dqb[:])
            e.tensor_scalar_mul(b.stabQ[:], b.stabK[:], SCORE_SCALE)
            e.tensor_scalar_mul(b.ctabQ[:], b.ctabK[:], SCORE_SCALE).then_inc(s_v.h, 1)
        ph.em("vector", f_tab)

    def p2(self, ph, pst):
        b = self
        s_ld, s_ld2, s_v, s_a, s_t = b.s_ld, b.s_ld2, b.s_v, b.s_a, b.s_t
        ps_qkv = [b.psum(f"p2qkv{i}", [128, 1024], F32, pst) for i in range(2)]
        trpv = [b.psum(f"p2trv{i}", [128, 512], F32, pst) for i in range(2)]
        wtile = [b.sbuf(f"p2wt{i}", [128, D], F32, stack=pst, side="right") for i in range(2)]
        whi = [b.sbuf(f"p2whi{i}", [128, D], BF16, stack=pst, side="right") for i in range(2)]
        wlo = [b.sbuf(f"p2wlo{i}", [128, D], BF16, stack=pst, side="right") for i in range(2)]
        qf = b.sbuf("p2qf", [128, T], F32, stack=pst, side="right")
        qsw = b.sbuf("p2qsw", [128, T], F32, stack=pst, side="right")
        qr = b.sbuf("p2qr", [128, T], F32, stack=pst, side="right")
        kst_hi = b.sbuf("p2ksth", [128, T], BF16, stack=pst, side="right")
        kst_lo = b.sbuf("p2kstl", [128, T], BF16, stack=pst, side="right")
        vst_hi = b.sbuf("p2vsth", [128, T], BF16, stack=pst, side="right")
        vst_lo = b.sbuf("p2vstl", [128, T], BF16, stack=pst, side="right")
        swapmask = [i ^ 1 for i in range(32)]

        conv_v = {}
        mm_t = {}
        epi_a = {}
        epi_v = {}
        ld_v = {}

        for hc in range(24):
            buf = hc % 2
            s_ld.v += 16
            ld_v[hc] = s_ld.v
            wait_conv = conv_v.get(hc - 2, 0)
            def f_ld(e, hc=hc, buf=buf, wait_conv=wait_conv):
                e.wait_ge(s_v.h, wait_conv)
                in_ = bass.AP(b.wqkvT, 128 * hc, [[HQKV, 128], [128 * HQKV, NDC], [1, 128]])
                e.dma_start(out=wtile[buf][:], in_=in_).then_inc(s_ld.h, 16)
            ph.em("sync", f_ld)
            wait_mm = mm_t.get(hc - 2, 0)
            s_v.v += 2
            conv_v[hc] = s_v.v
            def f_cv(e, hc=hc, buf=buf, wait_mm=wait_mm, ldv=ld_v[hc]):
                e.wait_ge(s_ld.h, ldv)
                e.wait_ge(s_t.h, wait_mm)
                e.tensor_copy(whi[buf][:], wtile[buf][:]).then_inc(s_v.h, 1)
                e.tensor_sub(wlo[buf][:], wtile[buf][:], whi[buf][:]).then_inc(s_v.h, 1)
            ph.em("vector", f_cv)
            wait_ea = epi_a.get(hc - 2, 0)
            wait_ev = epi_v.get(hc - 2, 0)
            s_t.v += 1
            mm_t[hc] = s_t.v
            def f_mm(e, hc=hc, buf=buf, cv=conv_v[hc], wait_ea=wait_ea, wait_ev=wait_ev):
                e.wait_ge(s_v.h, cv)
                e.wait_ge(s_a.h, wait_ea)
                e.wait_ge(s_v.h, wait_ev)
                for dc in range(NDC):
                    for w in (whi[buf], wlo[buf]):
                        for th in range(2):
                            li = e.matmul(ps_qkv[buf][:, 512 * th:512 * (th + 1)],
                                          w[:, 128 * dc:128 * (dc + 1)],
                                          b.rT[:, dc * T + 512 * th: dc * T + 512 * (th + 1)],
                                          start=(dc == 0 and w is whi[buf]),
                                          stop=(dc == NDC - 1 and w is wlo[buf]))
                li.then_inc(s_t.h, 1)
            ph.em("tensor", f_mm)
            if hc < 16:
                s_a.v += 1
                a_ev = s_a.v
                def f_ev(e, buf=buf, mmv=mm_t[hc], pv=epi_v.get(hc - 1, 0)):
                    e.wait_ge(s_t.h, mmv)
                    e.wait_ge(s_v.h, pv)
                    e.activation(out=qf[:], in_=ps_qkv[buf][:], func=ACT.Copy,
                                 bias=0.0, scale=1.0).then_inc(s_a.h, 1)
                ph.em("scalar", f_ev)
                s_v.v += 4
                v_rope = s_v.v
                def f_rope(e, a_ev=a_ev):
                    e.wait_ge(s_a.h, a_ev)
                    e.stream_shuffle(qsw[:], qf[:], swapmask)
                    e.tensor_mul(qr[:], qf[:], b.ctabQ[:])
                    e.tensor_mul(qsw[:], qsw[:], b.stabQ[:])
                    e.tensor_add(qr[:], qr[:], qsw[:]).then_inc(s_v.h, 4)
                ph.em("vector", f_rope)
                s_a.v += 1
                a_hi = s_a.v
                def f_hi(e, v_rope=v_rope, pstore=b.s_ld2.v):
                    e.wait_ge(s_v.h, v_rope)
                    e.wait_ge(s_ld2.h, pstore)
                    e.activation(out=b.stage[:, 0:T], in_=qr[:],
                                 func=ACT.Copy, bias=0.0, scale=1.0).then_inc(s_a.h, 1)
                ph.em("scalar", f_hi)
                s_v.v += 1
                v_lo = s_v.v
                def f_lo(e, a_hi=a_hi):
                    e.wait_ge(s_a.h, a_hi)
                    e.tensor_sub(b.stage2[:, 0:T], qr[:], b.stage[:, 0:T]).then_inc(s_v.h, 1)
                ph.em("vector", f_lo)
                s_ld2.v += 32
                def f_st(e, hc=hc, v_lo=v_lo, a_hi=a_hi):
                    e.wait_ge(s_v.h, v_lo)
                    e.wait_ge(s_a.h, a_hi)
                    e.dma_start(out=b.qhi_d[128 * hc:128 * (hc + 1), :],
                                in_=b.stage[:, 0:T]).then_inc(s_ld2.h, 16)
                    e.dma_start(out=b.qlo_d[128 * hc:128 * (hc + 1), :],
                                in_=b.stage2[:, 0:T]).then_inc(s_ld2.h, 16)
                ph.em("scalar", f_st)
                epi_a[hc] = a_hi
                epi_v[hc] = v_lo
            elif hc < 20:
                g = hc - 16
                s_a.v += 1
                a_ev = s_a.v
                def f_ev(e, buf=buf, mmv=mm_t[hc], pv=epi_v.get(hc - 1, 0), pst_=0):
                    e.wait_ge(s_t.h, mmv)
                    e.wait_ge(s_v.h, pv)
                    e.activation(out=qf[:], in_=ps_qkv[buf][:], func=ACT.Copy,
                                 bias=0.0, scale=1.0).then_inc(s_a.h, 1)
                ph.em("scalar", f_ev)
                s_v.v += 4
                v_rope = s_v.v
                def f_rope(e, a_ev=a_ev, pstore=b.s_ld2.v):
                    e.wait_ge(s_a.h, a_ev)
                    e.wait_ge(s_ld2.h, pstore)
                    e.stream_shuffle(qsw[:], qf[:], swapmask)
                    e.tensor_mul(qr[:], qf[:], b.ctabK[:])
                    e.tensor_mul(qsw[:], qsw[:], b.stabK[:])
                    e.tensor_add(qr[:], qr[:], qsw[:]).then_inc(s_v.h, 4)
                ph.em("vector", f_rope)
                s_a.v += 1
                a_hi = s_a.v
                def f_hi(e, v_rope=v_rope):
                    e.wait_ge(s_v.h, v_rope)
                    e.activation(out=kst_hi[:], in_=qr[:], func=ACT.Copy,
                                 bias=0.0, scale=1.0).then_inc(s_a.h, 1)
                ph.em("scalar", f_hi)
                s_v.v += 1
                v_lo = s_v.v
                def f_lo(e, a_hi=a_hi):
                    e.wait_ge(s_a.h, a_hi)
                    e.tensor_sub(kst_lo[:], qr[:], kst_hi[:]).then_inc(s_v.h, 1)
                ph.em("vector", f_lo)
                s_ld2.v += 32
                def f_st(e, g=g, v_lo=v_lo, a_hi=a_hi):
                    e.wait_ge(s_v.h, v_lo)
                    e.wait_ge(s_a.h, a_hi)
                    e.dma_start(out=b.ag_in[128 * g:128 * (g + 1), :], in_=kst_hi[:]).then_inc(s_ld2.h, 16)
                    e.dma_start(out=b.ag_in[512 + 128 * g:512 + 128 * (g + 1), :], in_=kst_lo[:]).then_inc(s_ld2.h, 16)
                ph.em("scalar", f_st)
                epi_a[hc] = a_hi
                epi_v[hc] = v_lo
            else:
                g = hc - 20
                s_v.v += 1
                v_dq = s_v.v
                def f_dq(e, buf=buf, mmv=mm_t[hc], pv=epi_v.get(hc - 1, 0)):
                    e.wait_ge(s_t.h, mmv)
                    e.wait_ge(s_v.h, pv)
                    e.tensor_mul(qf[:], ps_qkv[buf][:], b.dqb[:]).then_inc(s_v.h, 1)
                ph.em("vector", f_dq)
                tr_t = []
                for kb in range(2):
                    s_t.v += 1
                    tr_t.append(s_t.v)
                    def f_tr(e, kb=kb, v_dq=v_dq, pa=epi_a.get(hc - 1, 0), pvv=epi_v.get(hc - 1, 0)):
                        e.wait_ge(s_v.h, v_dq)
                        e.wait_ge(s_a.h, pa)
                        e.wait_ge(s_v.h, pvv)
                        for j in range(4):
                            ts = 4 * kb + j
                            li = e.transpose(trpv[kb][:, 128 * j:128 * (j + 1)],
                                             qf[:, 128 * ts:128 * (ts + 1)], b.idf[:])
                        li.then_inc(s_t.h, 1)
                    ph.em("tensor", f_tr)
                s_a.v += 2
                a_hi = s_a.v
                def f_hi(e, tr_t=tuple(tr_t), pst2=b.s_ld2.v):
                    e.wait_ge(s_ld2.h, pst2)
                    for kb in range(2):
                        e.wait_ge(s_t.h, tr_t[kb])
                        e.activation(out=vst_hi[:, 512 * kb:512 * (kb + 1)], in_=trpv[kb][:],
                                     func=ACT.Copy, bias=0.0, scale=1.0).then_inc(s_a.h, 1)
                ph.em("scalar", f_hi)
                s_v.v += 2
                v_lo = s_v.v
                def f_lo(e, a_hi=a_hi):
                    e.wait_ge(s_a.h, a_hi)
                    for kb in range(2):
                        e.tensor_sub(vst_lo[:, 512 * kb:512 * (kb + 1)], trpv[kb][:],
                                     vst_hi[:, 512 * kb:512 * (kb + 1)]).then_inc(s_v.h, 1)
                ph.em("vector", f_lo)
                s_ld2.v += 32
                def f_st(e, g=g, v_lo=v_lo, a_hi=a_hi):
                    e.wait_ge(s_v.h, v_lo)
                    e.wait_ge(s_a.h, a_hi)
                    for src, base in ((vst_hi, 1024 * T), (vst_lo, 1536 * T)):
                        out = bass.AP(b.ag_in, base + 128 * g,
                                      [[512, 128], [128 * 512, 8], [1, 128]])
                        in_ = bass.AP(src, 0, [[T, 128], [128, 8], [1, 128]])
                        e.dma_start(out=out, in_=in_).then_inc(s_ld2.h, 16)
                ph.em("scalar", f_st)
                epi_a[hc] = a_hi
                epi_v[hc] = v_lo

    def p3(self, ph):
        b = self
        need = [(b.s_ld2.h, b.s_ld2.v)]
        b.s_g.v += 1
        gv = b.s_g.v
        def f_ag(e, need=tuple(need)):
            for h, v in need:
                e.wait_ge(h, v)
            e.collective_compute(
                "AllGather", ALU.bypass,
                replica_groups=[list(range(NC))],
                ins=[b.ag_in[:, :]],
                outs=[b.ag_out[:, :]],
            ).then_inc(b.s_g.h, 1)
        ph.em("gpsimd", f_ag)
        for eng in ("sync", "vector", "scalar", "tensor"):
            ph.em(eng, lambda e, gv=gv: e.wait_ge(b.s_g.h, gv))

    def declare_pB(self, stack):
        sb = self.sbuf
        self.r2T = sb("r2T", [128, NDC * T], BF16, stack=stack, side="right")

    def declare_p4locals(self, stack):
        sb = self.sbuf
        self.qtlhi = [sb("qtlhi0", [128, D], BF16, stack=stack, side="right")] * 2
        self.qtllo = [sb("qtllo0", [128, D], BF16, stack=stack, side="right")] * 2
        self.khi = [sb(f"khi{i}", [128, NKV * 1024], BF16, stack=stack, side="right") for i in range(2)]
        self.klo = [sb(f"klo{i}", [128, NKV * 1024], BF16, stack=stack, side="right") for i in range(2)]
        self.vhi = [sb(f"vhi{i}", [128, NC * 512], BF16, stack=stack, side="right") for i in range(2)]
        self.vlo = [sb(f"vlo{i}", [128, NC * 512], BF16, stack=stack, side="right") for i in range(2)]
        self.phiT = sb("phiT", [128, 8 * 512], BF16, stack=stack, side="right")
        self.ploT = sb("ploT", [128, 8 * 512], BF16, stack=stack, side="right")
        self.fTrow = sb("fTrow", [1, 512], F32, stack=stack, side="right")
        self.fb_sb = sb("fb_sb", [128, 512], F32, stack=stack, side="right")
        self.q2b = sb("q2b", [128, 128], F32, stack=stack, side="right")
        self.colsum = sb("colsum", [128, 128], F32, stack=stack, side="right")
        self.O_sb = sb("O_sb", [128, NH * 128], F32, stack=stack, side="right")
        self.m_st = sb("m_st", [128, NH], F32, stack=stack, side="right")
        self.l_st = sb("l_st", [128, NH], F32, stack=stack, side="right")
        self.f_st = sb("f_st", [128, NH], F32, stack=stack, side="right")
        self.cs_st = sb("cs_st", [128, NH], F32, stack=stack, side="right")
        self.nmn_st = sb("nmn_st", [128, NH], F32, stack=stack, side="right")
        self.tm1 = sb("tm1", [128, 1], F32, stack=stack, side="right")

    def p4(self, ph, pst):
        b = self
        self.declare_p4locals(pst)
        s_ld, s_ld2, s_v, s_a, s_t = b.s_ld, b.s_ld2, b.s_v, b.s_a, b.s_t
        ps_sc = [b.psum(f"p4sc{i}", [128, 1024], F32, pst) for i in range(2)]
        trp = [b.psum(f"p4tr{i}", [128, 512], F32, pst) for i in range(2)]
        ps_pv = [b.psum(f"p4pv{i}", [128, 512], F32, pst) for i in range(2)]
        p32 = b.ta

        n_qt = self.cfg.get("n_qt", NT)
        hseq = 0
        exp_a = {}
        msk_v = {}
        pv_done_v = {}
        tr_a = {}
        tr_v = {}

        s_v.v += 3
        def f_init(e):
            e.memset(b.O_sb[:], 0.0)
            e.memset(b.l_st[:], 0.0)
            e.memset(b.m_st[:], NEG_BIG).then_inc(s_v.h, 3)
        ph.em("vector", f_init)
        s_ld2.v += 32
        def f_q0(e):
            for src, dst in ((b.qhi_d, b.qtlhi[0]), (b.qlo_d, b.qtllo[0])):
                in_ = bass.AP(src, 0, [[T, 128], [128 * T, NDC], [1, 128]])
                e.dma_start(out=dst[:], in_=in_).then_inc(s_ld2.h, 16)
        ph.em("scalar", f_q0)

        vis = 0
        sc_t = {}
        trp_t = {}
        pv_t = {}
        kv_free = {}
        kv_ld = {}
        pend = []

        gq_pv_t = {}
        gq_oadd = {}
        pend_gt = [None]

        def flush_gt():
            if pend_gt[0] is None:
                return
            pg, pgrp, ppvb, pape = pend_gt[0]
            pend_gt[0] = None
            s_t.v += 1
            t_pvt = s_t.v
            def f_pvt(e, pvb=ppvb, a_pe=pape):
                e.wait_ge(s_a.h, a_pe)
                for p2 in range(4):
                    li = e.transpose(pvb[:, 128 * p2:128 * (p2 + 1)],
                                     b.fb_sb[:, 128 * p2:128 * (p2 + 1)], b.idf[:])
                li.then_inc(s_t.h, 1)
            ph.em("tensor", f_pvt)
            s_v.v += 4
            def f_oa(e, g=pgrp, pvb=ppvb, t_pvt=t_pvt):
                e.wait_ge(s_t.h, t_pvt)
                for p2 in range(4):
                    hh2 = 4 * g + p2
                    li = e.scalar_tensor_tensor(b.O_sb[:, 128 * hh2:128 * (hh2 + 1)],
                                                b.O_sb[:, 128 * hh2:128 * (hh2 + 1)],
                                                b.f_st[:, hh2:hh2 + 1],
                                                pvb[:, 128 * p2:128 * (p2 + 1)],
                                                op0=ALU.mult, op1=ALU.add)
                li.then_inc(s_v.h, 4)
            ph.em("vector", f_oa)
            gq_oadd[pg] = s_v.v

        def emit_head_pe_tail(hp):
            info = pend.pop(0)
            assert info[0] == hp
            g_, vb_, visidx, hh = info[1], info[2], info[3], info[4]
            p_ = hh % 4
            gseq = hp // 4
            t_trs = []
            for kb in range(2):
                s_t.v += 1
                t_trs.append(s_t.v)
                def f_tr(e, kb=kb, hp=hp, ea=exp_a[hp], pa=tr_a.get(hp - 1, 0),
                         pv_=tr_v.get(hp - 1, 0)):
                    e.wait_ge(s_a.h, ea)
                    e.wait_ge(s_a.h, pa)
                    e.wait_ge(s_v.h, pv_)
                    p32v = b.ta[:, 1024 * (hp % 2):1024 * (hp % 2 + 1)]
                    for j in range(4):
                        kc = 4 * kb + j
                        li = e.transpose(trp[kb][:, 128 * j:128 * (j + 1)],
                                         p32v[:, 128 * kc:128 * (kc + 1)], b.idf[:])
                    li.then_inc(s_t.h, 1)
                ph.em("tensor", f_tr)
            trp_t[hp] = tuple(t_trs)
            s_a.v += 2
            a_phi = s_a.v
            def f_phi(e, t_trs=tuple(t_trs), p_=p_, ppv=gq_pv_t.get(gseq - 1, 0)):
                e.wait_ge(s_t.h, ppv)
                for kb in range(2):
                    e.wait_ge(s_t.h, t_trs[kb])
                    out = bass.AP(b.phiT, (4 * kb) * 512 + 128 * p_,
                                  [[8 * 512, 128], [512, 4], [1, 128]])
                    e.activation(out=out, in_=trp[kb][:], func=ACT.Copy,
                                 bias=0.0, scale=1.0).then_inc(s_a.h, 1)
            ph.em("scalar", f_phi)
            tr_a[hp] = s_a.v
            s_v.v += 2
            def f_plo(e, a_phi=a_phi, p_=p_):
                e.wait_ge(s_a.h, a_phi)
                for kb in range(2):
                    out = bass.AP(b.ploT, (4 * kb) * 512 + 128 * p_,
                                  [[8 * 512, 128], [512, 4], [1, 128]])
                    in1 = bass.AP(b.phiT, (4 * kb) * 512 + 128 * p_,
                                  [[8 * 512, 128], [512, 4], [1, 128]])
                    e.tensor_sub(out, trp[kb][:], in1).then_inc(s_v.h, 1)
            ph.em("vector", f_plo)
            tr_v[hp] = s_v.v
            if p_ != 3:
                return
            pvb = ps_pv[gseq % 2]
            s_t.v += 1
            gq_pv_t[gseq] = s_t.v
            def f_pv(e, g=g_, pvb=pvb, vb=vb_, kvld=kv_ld[visidx],
                     ta_=tr_a[hp], tv_=tr_v[hp], po=gq_oadd.get(gseq - 2, 0)):
                e.wait_ge(s_a.h, ta_)
                e.wait_ge(s_v.h, tv_)
                e.wait_ge(s_v.h, po)
                e.wait_ge(s_ld.h, kvld)
                for kc in range(NC):
                    vh = b.vhi[vb][:, 512 * kc + 128 * g:512 * kc + 128 * (g + 1)]
                    vl = b.vlo[vb][:, 512 * kc + 128 * g:512 * kc + 128 * (g + 1)]
                    pT = b.phiT[:, 512 * kc:512 * (kc + 1)]
                    pTl = b.ploT[:, 512 * kc:512 * (kc + 1)]
                    e.matmul(pvb[:], vh, pT, start=(kc == 0), stop=False)
                    e.matmul(pvb[:], vh, pTl, start=False, stop=False)
                    li = e.matmul(pvb[:], vl, pT, start=False, stop=(kc == NC - 1))
                li.then_inc(s_t.h, 1)
            ph.em("tensor", f_pv)
            flush_gt()
            s_a.v += 1
            a_pe = s_a.v
            def f_pe(e, pvb=pvb, pvt=s_t.v):
                e.wait_ge(s_t.h, pvt)
                e.activation(out=b.fb_sb[:], in_=pvb[:], func=ACT.Copy,
                             bias=0.0, scale=1.0).then_inc(s_a.h, 1)
            ph.em("scalar", f_pe)
            pend_gt[0] = (gseq, g_, pvb, a_pe)
            if hh == NH - 1:
                kv_free[visidx] = gq_pv_t[gseq]

        for I in range(n_qt):
            qb = I % 2
            q_ld2 = s_ld2.v
            for J in range(I + 1):
                vb = vis % 2
                s_ld.v += 16 * 10
                kv_ld[vis] = s_ld.v
                def f_kv(e, J=J, vb=vb, free_t=kv_free.get(vis - 2, 0)):
                    e.wait_ge(s_t.h, free_t)
                    for g in range(NKV):
                        e.dma_start(out=b.khi[vb][:, 1024 * g:1024 * (g + 1)],
                                    in_=bass.AP(b.ag_out, 128 * g * T + 128 * J,
                                                [[T, 128], [2048 * T, NC], [1, 128]])
                                    ).then_inc(s_ld.h, 16)
                    for g in range(NKV):
                        e.dma_start(out=b.klo[vb][:, 1024 * g:1024 * (g + 1)],
                                    in_=bass.AP(b.ag_out, (512 + 128 * g) * T + 128 * J,
                                                [[T, 128], [2048 * T, NC], [1, 128]])
                                    ).then_inc(s_ld.h, 16)
                    e.dma_start(out=b.vhi[vb][:],
                                in_=bass.AP(b.ag_out, 1024 * T + 128 * J * 512, [[512, 128], [2048 * T, NC], [1, 512]])
                                ).then_inc(s_ld.h, 16)
                    e.dma_start(out=b.vlo[vb][:],
                                in_=bass.AP(b.ag_out, 1536 * T + 128 * J * 512, [[512, 128], [2048 * T, NC], [1, 512]])
                                ).then_inc(s_ld.h, 16)
                ph.em("sync", f_kv)
                for h in range(NH):
                    g = h // 4
                    bank = ps_sc[hseq % 2]
                    s_t.v += 1
                    sc_t[hseq] = s_t.v
                    def f_sc(e, h=h, g=g, bank=bank, kvld=kv_ld[vis], qb=qb, q_ld2=q_ld2,
                             pe=exp_a.get(hseq - 2, 0), pm=msk_v.get(hseq - 2, 0), vb=vb):
                        e.wait_ge(s_ld.h, kvld - 32)
                        e.wait_ge(s_ld2.h, q_ld2)
                        e.wait_ge(s_a.h, pe)
                        e.wait_ge(s_v.h, pm)
                        lh = b.qtlhi[qb][:, 128 * h:128 * (h + 1)]
                        ll = b.qtllo[qb][:, 128 * h:128 * (h + 1)]
                        outs, rhss, rlos = [], [], []
                        for kh in range(2):
                            outs.append(bank[:, 512 * kh:512 * (kh + 1)])
                            rhss.append(b.khi[vb][:, 1024 * g + 512 * kh:1024 * g + 512 * (kh + 1)])
                            rlos.append(b.klo[vb][:, 1024 * g + 512 * kh:1024 * g + 512 * (kh + 1)])
                        for kh in range(2):
                            e.matmul(outs[kh], lh, rhss[kh], start=True, stop=False)
                        for kh in range(2):
                            e.matmul(outs[kh], lh, rlos[kh], start=False, stop=False)
                        for kh in range(2):
                            li = e.matmul(outs[kh], ll, rhss[kh], start=False, stop=True)
                        li.then_inc(s_t.h, 1)
                    ph.em("tensor", f_sc)
                    if pend:
                        emit_head_pe_tail(hseq - 1)
                    nv = 5 if J == I else 4
                    s_v.v += nv
                    def f_mx(e, h=h, bank=bank, sct=sc_t[hseq], diag=(J == I), nv=nv):
                        e.wait_ge(s_t.h, sct)
                        if diag:
                            e.tensor_add(bank[:], bank[:], b.masksb[:])
                        e.reduce_max(b.tm1[:], bank[:], axis=AXX)
                        Build.sp(e)
                        e.tensor_scalar(b.f_st[:, h:h + 1], b.m_st[:, h:h + 1], b.tm1[:, 0:1],
                                        0.0, op0=ALU.subtract, op1=ALU.min)
                        e.tensor_scalar(b.nmn_st[:, h:h + 1], b.m_st[:, h:h + 1], b.tm1[:, 0:1],
                                        -1.0, op0=ALU.max, op1=ALU.mult)
                        e.tensor_scalar(b.m_st[:, h:h + 1], b.m_st[:, h:h + 1], b.tm1[:, 0:1],
                                        None, op0=ALU.max).then_inc(s_v.h, nv)
                    ph.em("vector", f_mx)
                    msk_v[hseq] = s_v.v
                    s_a.v += 2
                    def f_exp(e, h=h, hp=hseq, bank=bank, mv=msk_v[hseq],
                              ptr_v=trp_t.get(hseq - 2, (0, 0))):
                        e.wait_ge(s_v.h, mv)
                        e.wait_ge(s_t.h, ptr_v[1])
                        p32v = b.ta[:, 1024 * (hp % 2):1024 * (hp % 2 + 1)]
                        e.activation(out=b.f_st[:, h:h + 1], in_=b.f_st[:, h:h + 1],
                                     func=ACT.Exp, bias=0.0, scale=1.0).then_inc(s_a.h, 1)
                        e.activation(out=p32v, in_=bank[:], func=ACT.Exp,
                                     bias=b.nmn_st[:, h:h + 1], scale=1.0,
                                     accum_out=b.cs_st[:, h:h + 1]).then_inc(s_a.h, 1)
                    ph.em("scalar", f_exp)
                    exp_a[hseq] = s_a.v
                    pend.append((hseq, g, vb, vis, h))
                    hseq += 1
                s_v.v += 2
                def f_l(e, ea=s_a.v):
                    e.wait_ge(s_a.h, ea)
                    e.tensor_mul(b.l_st[:], b.l_st[:], b.f_st[:])
                    Build.sp(e)
                    e.tensor_add(b.l_st[:], b.l_st[:], b.cs_st[:]).then_inc(s_v.h, 2)
                ph.em("vector", f_l)
                vis += 1
            if pend:
                emit_head_pe_tail(hseq - 1)
            if I + 1 < n_qt:
                s_ld2.v += 32
                def f_qn(e, I=I, pe=s_t.v):
                    e.wait_ge(s_t.h, pe)
                    for src, dst in ((b.qhi_d, b.qtlhi[0]), (b.qlo_d, b.qtllo[0])):
                        in_ = bass.AP(src, 128 * (I + 1), [[T, 128], [128 * T, NDC], [1, 128]])
                        e.dma_start(out=dst[:], in_=in_).then_inc(s_ld2.h, 16)
                ph.em("scalar", f_qn)
            flush_gt()
            s_v.v += NH + 1
            v_att = s_v.v
            def f_att(e, prev=s_v.v):
                Build.sp(e)
                Build.sp(e)
                e.reciprocal(b.cs_st[:], b.l_st[:])
                Build.sp(e)
                Build.sp(e)
                for h in range(NH):
                    li = e.tensor_scalar(b.tb[:, 128 * h:128 * (h + 1)],
                                         b.O_sb[:, 128 * h:128 * (h + 1)],
                                         b.cs_st[:, h:h + 1], None, op0=ALU.mult)
                li.then_inc(s_v.h, NH + 1)
            ph.em("vector", f_att)
            s_v.v += 3
            v_ss = s_v.v
            def f_ss(e):
                e.tensor_mul(b.tc_[:], b.tb[:], b.tb[:])
                e.tensor_reduce(b.c1[:], b.tc_[:], axis=AXX, op=ALU.add)
                Build.sp(e)
                Build.sp(e)
                e.tensor_scalar_mul(b.c1[:], b.c1[:], 1.0 / D).then_inc(s_v.h, 3)
            ph.em("vector", f_ss)
            s_a.v += 1
            a_srt = s_a.v
            def f_srt(e, v_ss=v_ss):
                e.wait_ge(s_v.h, v_ss)
                e.activation(out=b.c2[:], in_=b.c1[:], func=ACT.Sqrt,
                             bias=b.epsc[:, 0:1], scale=1.0).then_inc(s_a.h, 1)
            ph.em("scalar", f_srt)
            s_v.v += 1
            v_quant = None
            def f_q(e, a_srt=a_srt, I=I):
                e.wait_ge(s_a.h, a_srt)
                e.reciprocal(b.c2[:], b.c2[:])
                Build.sp(e)
                e.tensor_scalar(b.tc_[:], b.tb[:], b.c2[:, 0:1], None, op0=ALU.mult)
                e.tensor_mul(b.tc_[:], b.tc_[:], b.nwsb[:])
                e.reduce_max(b.c1[:], b.tc_[:], axis=AXX, apply_absolute_value=True)
                Build.sp(e)
                e.tensor_scalar_max(b.c1[:], b.c1[:], 1e-5)
                Build.sp(e)
                e.reciprocal(b.c3[:], b.c1[:])
                Build.sp(e)
                e.tensor_scalar_mul(b.c3[:], b.c3[:], 127.0)
                Build.sp(e)
                e.reciprocal(b.c4[:], b.c3[:])
                Build.sp(e)
                e.tensor_copy(b.ds2_cols[:, I:I + 1], b.c4[:])
                e.tensor_scalar(b.te[:], b.tc_[:], b.c3[:, 0:1], None, op0=ALU.mult)
                e.tensor_copy(b.ti[:], b.te[:])
                e.tensor_copy(b.tc_[:], b.ti[:])
                e.tensor_sub(b.td[:], b.te[:], b.tc_[:])
                e.tensor_single_scalar(b.te[:], b.td[:], 0.5, op=ALU.is_ge)
                e.tensor_add(b.tc_[:], b.tc_[:], b.te[:])
                e.tensor_single_scalar(b.te[:], b.td[:], -0.5, op=ALU.is_le)
                e.tensor_sub(b.td[:], b.tc_[:], b.te[:])
                if self.cfg.get("p4_no_reset"):
                    e.memset(b.j0[:], 0.0)
                    e.memset(b.j0[:], 0.0)
                    e.memset(b.j1[:], 0.0).then_inc(s_v.h, 1)
                else:
                    e.memset(b.O_sb[:], 0.0)
                    e.memset(b.l_st[:], 0.0)
                    e.memset(b.m_st[:], NEG_BIG).then_inc(s_v.h, 1)
            ph.em("vector", f_q)
            v_quant = s_v.v
            for k in range(4):
                tbank = trp[k % 2]
                s_t.v += 1
                t_tr = s_t.v
                def f_tr2(e, k=k, tbank=tbank, vq=v_quant, pa=s_a.v):
                    e.wait_ge(s_v.h, vq)
                    e.wait_ge(s_a.h, pa)
                    for j in range(4):
                        dc = 4 * k + j
                        li = e.transpose(tbank[:, 128 * j:128 * (j + 1)],
                                         b.td[:, 128 * dc:128 * (dc + 1)], b.idf[:])
                    li.then_inc(s_t.h, 1)
                ph.em("tensor", f_tr2)
                s_a.v += 1
                def f_ev2(e, k=k, tbank=tbank, t_tr=t_tr, I=I):
                    e.wait_ge(s_t.h, t_tr)
                    out = bass.AP(b.r2T, (4 * k) * T + 128 * I,
                                  [[NDC * T, 128], [T, 4], [1, 128]])
                    e.activation(out=out, in_=tbank[:], func=ACT.Copy,
                                 bias=0.0, scale=1.0).then_inc(s_a.h, 1)
                ph.em("scalar", f_ev2)
            tr_a[hseq - 1] = s_a.v
            tr_v[hseq - 1] = s_v.v
            gq_oadd[(hseq - 1) // 4] = s_v.v
            gq_pv_t[(hseq - 1) // 4] = s_t.v

    def p5(self, ph, pst):
        b = self
        s_ld, s_ld2, s_v, s_a, s_t, s_st = b.s_ld, b.s_ld2, b.s_v, b.s_a, b.s_t, b.s_st
        ps_wo = [b.psum(f"p5wo{i}", [128, 512], F32, pst) for i in range(NT)]
        wot = [b.sbuf(f"p5wot{i}", [128, 512], F32, stack=pst, side="right") for i in range(2)]
        woev = b.sbuf("p5woev", [128, 512], F32, stack=pst, side="right")
        conv_v = {}
        mm_t = {}
        ld_v = {}
        for dm in range(4):
            evprev_a = s_a.v
            evprev_st = s_st.v
            for dc in range(NDC):
                i = dm * NDC + dc
                buf = i % 2
                s_ld.v += 16
                ld_v[i] = s_ld.v
                def f_ld(e, dm=dm, dc=dc, buf=buf, wc=conv_v.get(i - 2, 0)):
                    e.wait_ge(s_v.h, wc)
                    e.dma_start(out=wot[buf][:],
                                in_=b.woT[128 * dc:128 * (dc + 1), 512 * dm:512 * (dm + 1)]
                                ).then_inc(s_ld.h, 16)
                ph.em("sync", f_ld)
                s_v.v += 2
                conv_v[i] = s_v.v
                def f_cv(e, buf=buf, ldv=ld_v[i], wm=mm_t.get(i - 2, 0)):
                    e.wait_ge(s_ld.h, ldv)
                    e.wait_ge(s_t.h, wm)
                    e.tensor_copy(b.stage[:, 512 * buf:512 * (buf + 1)], wot[buf][:]).then_inc(s_v.h, 1)
                    e.tensor_sub(b.stage2[:, 512 * buf:512 * (buf + 1)], wot[buf][:],
                                 b.stage[:, 512 * buf:512 * (buf + 1)]).then_inc(s_v.h, 1)
                ph.em("vector", f_cv)
                s_t.v += 1
                mm_t[i] = s_t.v
                def f_mm(e, dm=dm, dc=dc, buf=buf, cv=conv_v[i], ea=evprev_a):
                    e.wait_ge(s_v.h, cv)
                    e.wait_ge(s_a.h, ea)
                    for tt in range(NT):
                        lhsT = b.r2T[:, dc * T + 128 * tt:dc * T + 128 * (tt + 1)]
                        for w_i, w in enumerate((b.stage[:, 512 * buf:512 * (buf + 1)],
                                                 b.stage2[:, 512 * buf:512 * (buf + 1)])):
                            li = e.matmul(ps_wo[tt][:], lhsT, w,
                                          start=(dc == 0 and w_i == 0), stop=(dc == NDC - 1 and w_i == 1))
                    li.then_inc(s_t.h, 1)
                ph.em("tensor", f_mm)
            for tt in range(NT):
                s_a.v += 1
                a_ev = s_a.v
                def f_ev(e, tt=tt, dm=dm, mmv=s_t.v, pst_=b.s_ld3.v):
                    e.wait_ge(s_t.h, mmv)
                    e.wait_ge(b.s_ld3.h, pst_)
                    e.activation(out=woev[:], in_=ps_wo[tt][:], func=ACT.Copy, bias=0.0,
                                 scale=b.ds2_cols[:, tt:tt + 1]).then_inc(s_a.h, 1)
                ph.em("scalar", f_ev)
                b.s_ld3.v += 16
                def f_acc(e, tt=tt, dm=dm, a_ev=a_ev):
                    e.wait_ge(s_a.h, a_ev)
                    e.dma_start(out=b.y_out[128 * tt:128 * (tt + 1), 512 * dm:512 * (dm + 1)],
                                in_=woev[:], accum_op=ALU.add).then_inc(b.s_ld3.h, 16)
                ph.em("gpsimd", f_acc)

    def declare_pC(self, stack):
        sb = self.sbuf
        self.xnthi = sb("xnthi", [128, NDC * 512], BF16, stack=stack, side="right")
        self.xntlo = sb("xntlo", [128, NDC * 512], BF16, stack=stack, side="right")
        self.gubuf_hi = sb("gubuf_hi", [128, 11 * 512], BF16, stack=stack, side="right")
        self.gubuf_lo = sb("gubuf_lo", [128, 11 * 512], BF16, stack=stack, side="right")
        self.wgi = [sb(f"wgi{i}", [128, D], I32, stack=stack, side="right") for i in range(2)]
        self.wui = [sb(f"wui{i}", [128, D], I32, stack=stack, side="right") for i in range(2)]
        self.wgbf = [sb(f"wgbf{i}", [128, D], BF16, stack=stack, side="right") for i in range(2)]
        self.wubf = [sb(f"wubf{i}", [128, D], BF16, stack=stack, side="right") for i in range(2)]
        self.widm = [sb(f"widm{i}", [128, 11 * 128], I32, stack=stack, side="right") for i in range(2)]
        self.wdbf = [sb(f"wdbf{i}", [128, 11 * 128], BF16, stack=stack, side="right") for i in range(2)]
        self.gsil = sb("gsil", [128, 512], F32, stack=stack, side="right")
        self.gu32 = sb("gu32", [128, 512], F32, stack=stack, side="right")
        self.ffev = sb("ffev", [128, 512], F32, stack=stack, side="right")
        self.fftr = sb("fftr", [128, 512], F32, stack=stack, side="right")

    def p6(self, ph, pst):
        b = self
        s_ld, s_ld2, s_v, s_a, s_t, s_st = b.s_ld, b.s_ld2, b.s_v, b.s_a, b.s_t, b.s_st
        ps_g_ = [b.psum(f"p6g{i}", [128, 512], F32, pst) for i in range(2)]
        ps_u_ = [b.psum(f"p6u{i}", [128, 512], F32, pst) for i in range(2)]
        ps_d = [b.psum(f"p6d{i}", [128, 512], F32, pst) for i in range(2)]
        trp6 = [b.psum(f"p6tr{i}", [128, 512], F32, pst) for i in range(2)]

        NQ = 4
        HCQ = NHC // NQ

        for th in range(2):
            s_ld2.v += 32
            xn_ld = s_ld2.v
            prev_all = (s_t.v, s_v.v, s_a.v)
            def f_xn(e, th=th, prev_all=prev_all):
                e.wait_ge(s_t.h, prev_all[0])
                e.wait_ge(s_v.h, prev_all[1])
                e.wait_ge(s_a.h, prev_all[2])
                for src, dst in ((b.xnhi_d, b.xnthi), (b.xnlo_d, b.xntlo)):
                    in_ = bass.AP(src, 512 * th, [[T, 128], [128 * T, NDC], [1, 512]])
                    e.dma_start(out=dst[:], in_=in_).then_inc(s_ld2.h, 16)
            ph.em("scalar", f_xn)
            guv = {}
            gua = {}
            ldg = {}
            cvv = {}
            mmgu_t = {}
            for q in range(NQ):
                for hc in range(HCQ):
                    hqc = q * HCQ + hc
                    buf = hc % 2
                    s_ld.v += 32
                    ldg[hqc] = s_ld.v
                    def f_ld(e, hqc=hqc, buf=buf, wc=cvv.get(hqc - 2, 0)):
                        e.wait_ge(s_v.h, wc)
                        for w_in, dst in ((b.wgT, b.wgi[buf]), (b.wuT, b.wui[buf])):
                            in_ = bass.AP(w_in, 128 * hqc, [[HID, 128], [128 * HID, NDC], [1, 128]])
                            e.dma_start(out=dst[:], in_=in_).then_inc(s_ld.h, 16)
                    ph.em("sync", f_ld)
                    s_v.v += 2
                    cvv[hqc] = s_v.v
                    def f_cv(e, buf=buf, ldv=ldg[hqc], wm=mmgu_t.get(hqc - 2, 0)):
                        e.wait_ge(s_ld.h, ldv)
                        e.wait_ge(s_t.h, wm)
                        e.tensor_scalar(b.wgbf[buf][:], b.wgi[buf][:], 128.0, None,
                                        op0=ALU.subtract).then_inc(s_v.h, 1)
                        e.tensor_scalar(b.wubf[buf][:], b.wui[buf][:], 128.0, None,
                                        op0=ALU.subtract).then_inc(s_v.h, 1)
                    ph.em("vector", f_cv)
                    s_t.v += 1
                    mmgu_t[hqc] = s_t.v
                    def f_mm(e, th=th, buf=buf, cv=cvv[hqc], xn_ld=xn_ld,
                             pg=gua.get(hqc - 2, 0), pv=guv.get(hqc - 2, 0)):
                        e.wait_ge(s_v.h, cv)
                        e.wait_ge(s_ld2.h, xn_ld)
                        e.wait_ge(s_a.h, pg)
                        e.wait_ge(s_v.h, pv)
                        for W, psx in ((b.wgbf[buf], ps_g_[buf]), (b.wubf[buf], ps_u_[buf])):
                            for dc in range(NDC):
                                for xn in (b.xnthi, b.xntlo):
                                    li = e.matmul(psx[:], W[:, 128 * dc:128 * (dc + 1)],
                                                  xn[:, 512 * dc:512 * (dc + 1)],
                                                  start=(dc == 0 and xn is b.xnthi),
                                                  stop=(dc == NDC - 1 and xn is b.xntlo))
                        li.then_inc(s_t.h, 1)
                    ph.em("tensor", f_mm)
                    s_a.v += 1
                    a_sil = s_a.v
                    def f_sil(e, hqc=hqc, buf=buf, mm=mmgu_t[hqc], pv=guv.get(hqc - 1, 0)):
                        e.wait_ge(s_t.h, mm)
                        e.wait_ge(s_v.h, pv)
                        e.activation(out=b.gsil[:], in_=ps_g_[buf][:], func=ACT.Silu, bias=0.0,
                                     scale=b.sgsb[:, hqc:hqc + 1]).then_inc(s_a.h, 1)
                    ph.em("scalar", f_sil)
                    s_v.v += 1
                    v_gu = s_v.v
                    def f_gu(e, hqc=hqc, buf=buf, a_sil=a_sil):
                        e.wait_ge(s_a.h, a_sil)
                        e.scalar_tensor_tensor(b.gu32[:], ps_u_[buf][:], b.susb[:, hqc:hqc + 1],
                                               b.gsil[:], op0=ALU.mult, op1=ALU.mult).then_inc(s_v.h, 1)
                    ph.em("vector", f_gu)
                    s_a.v += 1
                    a_hi = s_a.v
                    def f_hi(e, hc=hc, v_gu=v_gu):
                        e.wait_ge(s_v.h, v_gu)
                        e.activation(out=b.gubuf_hi[:, 512 * hc:512 * (hc + 1)], in_=b.gu32[:],
                                     func=ACT.Copy, bias=0.0, scale=1.0).then_inc(s_a.h, 1)
                    ph.em("scalar", f_hi)
                    gua[hqc] = s_a.v
                    s_v.v += 1
                    def f_lo(e, hc=hc, a_hi=a_hi):
                        e.wait_ge(s_a.h, a_hi)
                        e.tensor_sub(b.gubuf_lo[:, 512 * hc:512 * (hc + 1)], b.gu32[:],
                                     b.gubuf_hi[:, 512 * hc:512 * (hc + 1)]).then_inc(s_v.h, 1)
                    ph.em("vector", f_lo)
                    guv[hqc] = s_v.v
                gu_all_v = s_v.v
                gu_all_a = s_a.v
                dconv = {}
                dmm = {}
                for dm in range(NDC):
                    buf = dm % 2
                    s_ld.v += 16
                    dl = s_ld.v
                    def f_ld2(e, q=q, dm=dm, buf=buf, wc=dconv.get(dm - 2, 0)):
                        e.wait_ge(s_v.h, wc)
                        in_ = bass.AP(b.wdT, (q * HCQ * 128) * D + 128 * dm,
                                      [[D, 128], [128 * D, HCQ], [1, 128]])
                        e.dma_start(out=b.widm[buf][:], in_=in_).then_inc(s_ld.h, 16)
                    ph.em("sync", f_ld2)
                    s_v.v += 1
                    dconv[dm] = s_v.v
                    def f_cv2(e, buf=buf, dl=dl, wm=dmm.get(dm - 2, 0)):
                        e.wait_ge(s_ld.h, dl)
                        e.wait_ge(s_t.h, wm)
                        e.tensor_scalar(b.wdbf[buf][:], b.widm[buf][:], 128.0, None,
                                        op0=ALU.subtract).then_inc(s_v.h, 1)
                    ph.em("vector", f_cv2)
                    s_t.v += 1
                    dmm[dm] = s_t.v
                    def f_dmm(e, dm=dm, buf=buf, cv=dconv[dm], guv_=gu_all_v, gua_=gu_all_a,
                              pe=s_a.v):
                        e.wait_ge(s_v.h, cv)
                        e.wait_ge(s_v.h, guv_)
                        e.wait_ge(s_a.h, gua_)
                        e.wait_ge(s_a.h, pe)
                        for hc in range(HCQ):
                            for gu in (b.gubuf_hi, b.gubuf_lo):
                                li = e.matmul(ps_d[buf][:], b.wdbf[buf][:, 128 * hc:128 * (hc + 1)],
                                              gu[:, 512 * hc:512 * (hc + 1)],
                                              start=(hc == 0 and gu is b.gubuf_hi),
                                              stop=(hc == HCQ - 1 and gu is b.gubuf_lo))
                        li.then_inc(s_t.h, 1)
                    ph.em("tensor", f_dmm)
                    s_a.v += 1
                    a_ev = s_a.v
                    def f_ev(e, dm=dm, buf=buf, mm=dmm[dm], ptr=s_t.v):
                        e.wait_ge(s_t.h, mm)
                        e.wait_ge(s_t.h, ptr)
                        e.activation(out=b.ffev[:], in_=ps_d[buf][:], func=ACT.Copy, bias=0.0,
                                     scale=b.sdsb[:, dm:dm + 1]).then_inc(s_a.h, 1)
                    ph.em("scalar", f_ev)
                    s_t.v += 1
                    t_tr0 = s_t.v
                    def f_tr(e, a_ev=a_ev, pa=s_a.v):
                        e.wait_ge(s_a.h, a_ev)
                        e.wait_ge(s_a.h, pa)
                        for j in range(4):
                            li = e.transpose(trp6[0][:, 128 * j:128 * (j + 1)],
                                             b.ffev[:, 128 * j:128 * (j + 1)], b.idf[:])
                        li.then_inc(s_t.h, 1)
                    ph.em("tensor", f_tr)
                    s_a.v += 1
                    a_ev2 = s_a.v
                    def f_ev2(e, t_tr=t_tr0, pst_=b.s_ld3.v):
                        e.wait_ge(s_t.h, t_tr)
                        e.wait_ge(b.s_ld3.h, pst_)
                        e.activation(out=b.fftr[:], in_=trp6[0][:], func=ACT.Copy, bias=0.0,
                                     scale=1.0).then_inc(s_a.h, 1)
                    ph.em("scalar", f_ev2)
                    b.s_ld3.v += 16
                    def f_acc(e, th=th, dm=dm, a_ev2=a_ev2):
                        e.wait_ge(s_a.h, a_ev2)
                        out = bass.AP(b.y_out, (512 * th) * D + 128 * dm,
                                      [[D, 128], [128 * D, 4], [1, 128]])
                        e.dma_start(out=out, in_=b.fftr[:], accum_op=ALU.add).then_inc(b.s_ld3.h, 16)
                    ph.em("gpsimd", f_acc)



def make_core_inputs(inputs, core):
    x = np.asarray(inputs["L0_attn_x"])[0]
    wqkv = np.asarray(inputs["L0_attn_wqkv"])
    wo = np.asarray(inputs["L0_attn_wo"])
    nw = np.asarray(inputs["L0_attn_norm_w"])
    wg = np.asarray(inputs["L0_ffn_wq_gate"])
    sgv = np.asarray(inputs["L0_ffn_s_gate"])
    wu = np.asarray(inputs["L0_ffn_wq_up"])
    suv = np.asarray(inputs["L0_ffn_s_up"])
    wd = np.asarray(inputs["L0_ffn_wq_down"])
    sdv = np.asarray(inputs["L0_ffn_s_down"])

    shared = dict(
        wqkvT=np.ascontiguousarray(wqkv.T),
        woT=np.ascontiguousarray(wo.T),
        nwrep=np.ascontiguousarray(
            np.broadcast_to(nw.reshape(NH, 128).T[:, :, None], (128, NH, 128)).reshape(128, D)
        ).astype(np.float32),
        wgT=np.ascontiguousarray(wg.T),
        wuT=np.ascontiguousarray(wu.T),
        wdT=np.ascontiguousarray(wd.T),
        sg=np.ascontiguousarray(sgv.reshape(NHC, 128).T).astype(np.float32),
        su=np.ascontiguousarray(suv.reshape(NHC, 128).T).astype(np.float32),
        sd=np.ascontiguousarray(sdv.reshape(NDC, 128).T).astype(np.float32),
        xfr=np.ascontiguousarray((-(np.arange(128) // 2) / 64.0).reshape(128, 1)).astype(np.float32),
        sgn=np.ascontiguousarray(np.where(np.arange(128) % 2 == 0, -1.0, 1.0).reshape(128, 1)).astype(np.float32),
        idf=np.eye(128, dtype=np.float32),
        onesf=np.ones((128, 128), dtype=np.float32),
        onec=np.ones((128, 1), dtype=np.float32),
    )
    xc = np.ascontiguousarray(x[core::NC])
    pos = (core + NC * np.arange(T)).astype(np.float32).reshape(1, T)
    i_idx = np.arange(128)[:, None]
    r_idx = (np.arange(1024)[None, :] // 128)
    m_idx = (np.arange(1024)[None, :] % 128)
    allowed = (m_idx < i_idx) | ((m_idx == i_idx) & (r_idx <= core))
    mask = np.where(allowed, 0.0, NEG_BIG).astype(np.float32)
    return dict(shared, x=xc, pos=pos, mask=mask)


def build_full(cfg=None):
    b = Build(cfg or dict(debug_outputs={}))
    ph = Phase(b.nc)
    b.p0(ph)
    ph.run()
    with ExitStack() as sA:
        b.declare_pA(sA)
        with ExitStack() as sP1:
            ph = Phase(b.nc); b.p1(ph, sP1); ph.run()
            ph = Phase(b.nc); b.p15(ph, sP1); ph.run()
        with ExitStack() as sP2:
            ph = Phase(b.nc); b.guard_all(ph); b.p2(ph, sP2); ph.run()
            ph = Phase(b.nc); b.p3(ph); ph.run()
    with ExitStack() as sB:
        b.declare_pB(sB)
        with ExitStack() as sP4:
            ph = Phase(b.nc); b.guard_all(ph); b.p4(ph, sP4); ph.run()
        with ExitStack() as sP5:
            ph = Phase(b.nc); b.guard_all(ph); b.p5(ph, sP5); ph.run()
    with ExitStack() as sC:
        b.declare_pC(sC)
        with ExitStack() as sP6:
            ph = Phase(b.nc); b.guard_all(ph); b.p6(ph, sP6); ph.run()
    ph = Phase(b.nc)
    b.guard_all(ph)
    ph.run()
    b.es.close()
    return b


def kernel(**inputs):
    b = build_full()
    in_maps = [make_core_inputs(inputs, c) for c in range(NC)]
    res = run_bass_kernel_spmd(b.nc, in_maps, core_ids=list(range(NC)))
    y = np.empty((S, D), dtype=np.float32)
    for c in range(NC):
        y[c::NC] = res.results[c]["y"]
    return y.reshape(1, S, D)


# version 14 (final)
# speedup vs baseline: 1.0990x; 1.0122x over previous
import numpy as np
import ml_dtypes
from contextlib import ExitStack

import concourse.bass as bass
import concourse.mybir as mybir
from concourse.bass_utils import run_bass_kernel_spmd

F32 = mybir.dt.float32
BF16 = mybir.dt.bfloat16
I32 = mybir.dt.int32
ALU = mybir.AluOpType
ACT = mybir.ActivationFunctionType
AXX = mybir.AxisListType.X

S, D, NH, NKV, HD = 8192, 2048, 16, 4, 128
HQKV, HID, NC = 3072, 5632, 8
T = S // NC
NT = T // 128
NDC = D // 128
NHC = HID // 128
EPS = 1e-5
NEG_BIG = -30000.0
SCORE_SCALE = float(1.0 / np.float32(np.sqrt(np.float64(128.0))))
THETA = 500000.0

ENGS = ("sync", "vector", "scalar", "tensor", "gpsimd")


def _cw2pi():
    two_pi = 2.0 * np.pi
    c1 = float(np.float32(np.round(two_pi * 512) / 512.0))
    r = two_pi - c1
    c2 = float(np.float32(np.round(r * 2.0**32) / 2.0**32))
    c3 = float(np.float32(two_pi - c1 - c2))
    return c1, c2, c3


def _halfpi2():
    hp = 0.5 * np.pi
    a = float(np.float32(hp))
    b = float(np.float32(hp - a))
    return a, b


class Sem:
    def __init__(self, es, nc, name):
        self.h = es.enter_context(nc.semaphore(name))
        self.v = 0


class Phase:

    def __init__(self, nc):
        self.nc = nc
        self.ops = {k: [] for k in ENGS}

    def em(self, eng, fn):
        self.ops[eng].append(fn)

    def run(self):
        with self.nc.Block() as block:
            for eng in ENGS:
                fns = self.ops[eng]
                if not fns:
                    continue
                getattr(block, eng)(lambda e, fns=fns: [f(e) for f in fns])


class Build:

    def __init__(self, cfg):
        self.cfg = cfg
        self.nc = bass.Bass("TRN2", target_bir_lowering=False, detect_race_conditions=False)
        self.es = ExitStack()
        self._declare_io()
        self._declare_sems()
        self._declare_persistent()

    def _declare_io(self):
        nc = self.nc
        self.x_in = nc.dram_tensor("x", [T, D], F32, kind="ExternalInput")
        self.wqkvT = nc.dram_tensor("wqkvT", [D, HQKV], F32, kind="ExternalInput")
        self.woT = nc.dram_tensor("woT", [D, D], F32, kind="ExternalInput")
        self.nwrep = nc.dram_tensor("nwrep", [128, D], F32, kind="ExternalInput")
        self.wgT = nc.dram_tensor("wgT", [D, HID], I32, kind="ExternalInput")
        self.wuT = nc.dram_tensor("wuT", [D, HID], I32, kind="ExternalInput")
        self.wdT = nc.dram_tensor("wdT", [HID, D], I32, kind="ExternalInput")
        self.sg_in = nc.dram_tensor("sg", [128, NHC], F32, kind="ExternalInput")
        self.su_in = nc.dram_tensor("su", [128, NHC], F32, kind="ExternalInput")
        self.sd_in = nc.dram_tensor("sd", [128, NDC], F32, kind="ExternalInput")
        self.pos_in = nc.dram_tensor("pos", [1, T], F32, kind="ExternalInput")
        self.mask_in = nc.dram_tensor("mask", [128, 1024], F32, kind="ExternalInput")
        self.xfr_in = nc.dram_tensor("xfr", [128, 1], F32, kind="ExternalInput")
        self.sgn_in = nc.dram_tensor("sgn", [128, 1], F32, kind="ExternalInput")
        self.idf_in = nc.dram_tensor("idf", [128, 128], F32, kind="ExternalInput")
        self.onesf_in = nc.dram_tensor("onesf", [128, 128], F32, kind="ExternalInput")
        self.onec_in = nc.dram_tensor("onec", [128, 1], F32, kind="ExternalInput")
        self.y_out = nc.dram_tensor("y", [T, D], F32, kind="ExternalOutput")
        self.ag_in = nc.dram_tensor("ag_in", [2048, T], BF16)
        self.ag_out = nc.dram_tensor("ag_out", [NC * 2048, T], BF16)
        self.xnhi_d = nc.dram_tensor("xnhi_d", [D, T], BF16)
        self.xnlo_d = nc.dram_tensor("xnlo_d", [D, T], BF16)
        self.qhi_d = nc.dram_tensor("qhi_d", [D, T], BF16)
        self.qlo_d = nc.dram_tensor("qlo_d", [D, T], BF16)
        self.row_d = nc.dram_tensor("row_d", [1, 2048], F32)
        self.dbg = {}
        for name, shape in self.cfg.get("debug_outputs", {}).items():
            self.dbg[name] = self.nc.dram_tensor(name, list(shape), F32, kind="ExternalOutput")

    def _declare_sems(self):
        es, nc = self.es, self.nc
        self.s_ld = Sem(es, nc, "s_ld")
        self.s_ld2 = Sem(es, nc, "s_ld2")
        self.s_ld3 = Sem(es, nc, "s_ld3")
        self.s_v = Sem(es, nc, "s_v")
        self.s_a = Sem(es, nc, "s_a")
        self.s_t = Sem(es, nc, "s_t")
        self.s_g = Sem(es, nc, "s_g")
        self.s_st = Sem(es, nc, "s_st")
        self.all_sems = [self.s_ld, self.s_ld2, self.s_ld3, self.s_v, self.s_a,
                         self.s_t, self.s_g, self.s_st]

    def sbuf(self, name, shape, dtype, stack=None, side=None):
        kw = {} if side is None else {"side": side}
        return (stack or self.es).enter_context(self.nc.sbuf_tensor(name, shape, dtype, **kw))

    def psum(self, name, shape, dtype, stack):
        return stack.enter_context(self.nc.psum_tensor(name, shape, dtype))

    def _declare_persistent(self):
        sb = self.sbuf
        self.dq_cols = sb("dq_cols", [128, NT], F32)
        self.ds2_cols = sb("ds2_cols", [128, NT], F32)
        self.masksb = sb("masksb", [128, 1024], F32)
        self.nwsb = sb("nwsb", [128, D], F32)
        self.sgsb = sb("sgsb", [128, NHC], F32)
        self.susb = sb("susb", [128, NHC], F32)
        self.sdsb = sb("sdsb", [128, NDC], F32)
        self.xfrsb = sb("xfrsb", [128, 1], F32)
        self.sgnsb = sb("sgnsb", [128, 1], F32)
        self.idf = sb("idf_s", [128, 128], F32)
        self.onescol = sb("onescol", [128, 1], F32)
        self.onesrow1 = sb("onesrow1", [1, 128], F32)
        self.ta = sb("ta", [128, D], F32)
        self.tb = sb("tb", [128, D], F32)
        self.tc_ = sb("tc_", [128, D], F32)
        self.ti = sb("ti", [128, D], I32)
        self.td = sb("td", [128, D], F32)
        self.te = sb("te", [128, D], F32)
        self.st24 = sb("st24", [128, 24], F32)
        self.st8 = sb("st8", [128, 8], F32)
        self.c1 = sb("c1", [128, 1], F32)
        self.c2 = sb("c2", [128, 1], F32)
        self.c3 = sb("c3", [128, 1], F32)
        self.c4 = sb("c4", [128, 1], F32)
        self.stage = sb("stage", [128, D], BF16)
        self.stage2 = sb("stage2", [128, D], BF16)
        self.evrow = sb("evrow", [16, 128], F32)
        self.rowa = sb("rowa", [1, 128], F32)
        self.rowb = sb("rowb", [1, 128], F32)
        self.rowc = sb("rowc", [1, 128], F32)
        self.epsc = sb("epsc", [128, 1], F32)
        self.j0 = sb("j0", [128, 1], F32)
        self.j1 = sb("j1", [128, 1], F32)
        Build._sp_j0 = self.j0[:]
        Build._sp_j1 = self.j1[:]

    def declare_pA(self, stack):
        sb = self.sbuf
        self.rT = sb("rT", [128, NDC * T], BF16, stack=stack, side="right")
        self.dqb = sb("dqb", [128, T], F32, stack=stack, side="right")
        self.possb = sb("possb", [128, T], F32, stack=stack, side="right")
        self.ctabQ = sb("ctabQ", [128, T], F32, stack=stack, side="right")
        self.stabQ = sb("stabQ", [128, T], F32, stack=stack, side="right")
        self.ctabK = sb("ctabK", [128, T], F32, stack=stack, side="right")
        self.stabK = sb("stabK", [128, T], F32, stack=stack, side="right")

    @staticmethod
    def sp(e):
        e.tensor_copy(Build._sp_j1, Build._sp_j0)

    def dump_now(self, ph, names):
        for nm in names:
            t = getattr(self, nm)
            shp = list(t.ap().shape)
            d = self.nc.dram_tensor("dbg_" + nm, shp, t.ap().dtype, kind="ExternalOutput")
            self.s_st.v += 16
            def f(e, t=t, d=d):
                e.dma_start(out=d.ap(), in_=t.ap()).then_inc(self.s_st.h, 16)
            ph.em("sync", f)

    def guard_all(self, ph):
        vals = [(s.h, s.v) for s in self.all_sems if s.v > 0]
        for eng in ENGS:
            def f(e, vals=vals):
                for h, v in vals:
                    e.wait_ge(h, v)
            ph.em(eng, f)

    def p0(self, ph):
        b = self
        s_ld = b.s_ld
        loads = [
            (b.masksb[:], b.mask_in[:, :]),
            (b.nwsb[:], b.nwrep[:, :]),
            (b.sgsb[:], b.sg_in[:, :]),
            (b.susb[:], b.su_in[:, :]),
            (b.sdsb[:], b.sd_in[:, :]),
            (b.xfrsb[:], b.xfr_in[:, :]),
            (b.sgnsb[:], b.sgn_in[:, :]),
            (b.idf[:], b.idf_in[:, :]),
            (b.onescol[:], b.onec_in[:, :]),
            (b.onesrow1[:], b.onesf_in[0:1, 0:128]),
        ]
        def f(e):
            for out, in_ in loads:
                e.dma_start(out=out, in_=in_).then_inc(s_ld.h, 16)
        ph.em("sync", f)
        s_ld.v += 16 * len(loads)
        b.s_v.v += 1
        ph.em("vector", lambda e: (e.memset(b.epsc[:], EPS), e.memset(b.j0[:], 0.0),
                                   e.memset(b.j1[:], 0.0).then_inc(b.s_v.h, 1)))

    def p1(self, ph, pst):
        b = self
        nc = self.nc
        rT = b.rT
        b.xs = b.sbuf("xs", [128, D], F32, stack=pst)
        trp = [b.psum(f"p1trp{i}", [128, 512], F32, pst) for i in range(2)]
        s_ld, s_ld2, s_v, s_a, s_t, s_st = b.s_ld, b.s_ld2, b.s_v, b.s_a, b.s_t, b.s_st
        skip = self.cfg.get("p1_skip", set())
        n_tiles = self.cfg.get("p1_tiles", NT)

        for it in range(n_tiles):
            prev_v, prev_a, prev_st, prev_t = s_v.v, s_a.v, s_st.v, s_t.v
            s_ld.v += 16
            ldv = s_ld.v
            ph.em("sync", lambda e, it=it, prev_v=prev_v, prev_a=prev_a, prev_st=prev_st, ldv=ldv: (
                e.wait_ge(s_v.h, prev_v),
                e.wait_ge(s_a.h, prev_a),
                e.wait_ge(s_st.h, prev_st),
                e.dma_start(out=b.xs[:], in_=b.x_in[128 * it:128 * (it + 1), :]).then_inc(s_ld.h, 16),
            ))
            s_st.v += 16
            ph.em("scalar", lambda e, it=it, ldv=ldv: (
                e.wait_ge(s_ld.h, ldv),
                e.dma_start(out=b.y_out[128 * it:128 * (it + 1), :], in_=b.xs[:]).then_inc(s_st.h, 16),
            ))
            s_v.v += 1
            v_stats = s_v.v
            def f_stats(e, ldv=ldv):
                e.wait_ge(s_ld.h, ldv)
                e.tensor_reduce(b.c1[:], b.xs[:], axis=AXX, op=ALU.add)
                e.tensor_mul(b.te[:], b.xs[:], b.xs[:])
                e.tensor_reduce(b.c3[:], b.te[:], axis=AXX, op=ALU.add)
                Build.sp(e)
                e.tensor_scalar_mul(b.c1[:], b.c1[:], 1.0 / D)
                Build.sp(e)
                e.tensor_scalar_mul(b.c3[:], b.c3[:], 1.0 / D)
                e.tensor_mul(b.c2[:], b.c1[:], b.c1[:])
                Build.sp(e)
                e.tensor_sub(b.c3[:], b.c3[:], b.c2[:]).then_inc(s_v.h, 1)
            ph.em("vector", f_stats)
            s_a.v += 1
            a_rsq = s_a.v
            def f_rsq(e, v_stats=v_stats):
                e.wait_ge(s_v.h, v_stats)
                e.activation(out=b.c4[:], in_=b.c3[:], func=ACT.Sqrt, bias=b.epsc[:, 0:1], scale=1.0).then_inc(s_a.h, 1)
            ph.em("scalar", f_rsq)
            s_v.v += 1
            v_nmr = s_v.v
            def f_nmr(e, a_rsq=a_rsq):
                e.wait_ge(s_a.h, a_rsq)
                e.reciprocal(b.c4[:], b.c4[:]).then_inc(s_v.h, 1)
            ph.em("vector", f_nmr)
            s_v.v += 1
            v_xn = s_v.v
            def f_xn(e, prev_t=prev_t):
                e.wait_ge(s_t.h, prev_t)
                e.tensor_scalar(b.te[:], b.xs[:], b.c1[:, 0:1], None, op0=ALU.subtract)
                e.tensor_scalar(b.ta[:], b.te[:], b.c4[:, 0:1], None, op0=ALU.mult).then_inc(s_v.h, 1)
            ph.em("vector", f_xn)
            a_xn = v_xn
            s_v.v += 1
            v_qs = s_v.v
            def f_qs(e, it=it):
                e.reduce_max(b.c1[:], b.ta[:], axis=AXX, apply_absolute_value=True)
                Build.sp(e)
                e.tensor_scalar_max(b.c1[:], b.c1[:], 1e-5)
                Build.sp(e)
                e.reciprocal(b.c2[:], b.c1[:])
                Build.sp(e)
                e.tensor_scalar_mul(b.c2[:], b.c2[:], 127.0)
                Build.sp(e)
                e.reciprocal(b.c3[:], b.c2[:])
                Build.sp(e)
                e.tensor_copy(b.dq_cols[:, it:it + 1], b.c3[:]).then_inc(s_v.h, 1)
            ph.em("vector", f_qs)
            s_v.v += 1
            def f_yq(e):
                e.tensor_scalar(b.tb[:], b.ta[:], b.c2[:, 0:1], None, op0=ALU.mult).then_inc(s_v.h, 1)
            ph.em("vector", f_yq)
            s_v.v += 1
            v_rr = s_v.v
            def f_rr(e, prev_t=prev_t):
                e.wait_ge(s_t.h, prev_t)
                e.tensor_copy(b.ti[:], b.tb[:])
                e.tensor_copy(b.tc_[:], b.ti[:])
                e.tensor_sub(b.td[:], b.tb[:], b.tc_[:])
                e.tensor_single_scalar(b.te[:], b.td[:], 0.5, op=ALU.is_ge)
                e.tensor_add(b.tc_[:], b.tc_[:], b.te[:])
                e.tensor_single_scalar(b.te[:], b.td[:], -0.5, op=ALU.is_le)
                e.tensor_sub(b.td[:], b.tc_[:], b.te[:]).then_inc(s_v.h, 1)
            ph.em("vector", f_rr)
            for src_i in ([] if "tr" in skip else [i_ for i_ in range(2) if f"tr{i_}" not in skip]):
                for k in range(4):
                    bank = trp[(it * 8 + src_i * 4 + k) % 2]
                    src_sem, src_val = (s_v, v_rr) if src_i == 0 else (s_v, a_xn)
                    s_t.v += 1
                    t_tr = s_t.v
                    need_a = s_a.v
                    need_v = s_v.v
                    def f_tr(e, k=k, bank=bank, src_i=src_i, src_sem=src_sem,
                             src_val=src_val, need_a=need_a, need_v=need_v):
                        e.wait_ge(src_sem.h, src_val)
                        e.wait_ge(s_a.h, need_a)
                        e.wait_ge(s_v.h, need_v)
                        src = b.td if src_i == 0 else b.ta
                        for j in range(4):
                            dc = 4 * k + j
                            li = e.transpose(bank[:, 128 * j:128 * (j + 1)],
                                             src[:, 128 * dc:128 * (dc + 1)], b.idf[:])
                        li.then_inc(s_t.h, 1)
                    ph.em("tensor", f_tr)
                    if src_i == 0:
                        s_a.v += 1
                        def f_ev(e, k=k, bank=bank, it=it, t_tr=t_tr):
                            e.wait_ge(s_t.h, t_tr)
                            out = bass.AP(rT, (4 * k) * T + 128 * it,
                                          [[NDC * T, 128], [T, 4], [1, 128]])
                            e.activation(out=out, in_=bank[:], func=ACT.Copy,
                                         bias=0.0, scale=1.0).then_inc(s_a.h, 1)
                        ph.em("scalar", f_ev)
                    else:
                        s_a.v += 1
                        a_hi = s_a.v
                        def f_hi(e, k=k, bank=bank, t_tr=t_tr, prev_st=prev_st):
                            e.wait_ge(s_t.h, t_tr)
                            e.wait_ge(s_st.h, prev_st)
                            e.activation(out=b.stage[:, 512 * k:512 * (k + 1)], in_=bank[:],
                                         func=ACT.Copy, bias=0.0, scale=1.0).then_inc(s_a.h, 1)
                        ph.em("scalar", f_hi)
                        s_v.v += 1
                        def f_lo(e, k=k, bank=bank, a_hi=a_hi):
                            e.wait_ge(s_a.h, a_hi)
                            e.tensor_sub(b.stage2[:, 512 * k:512 * (k + 1)], bank[:],
                                         b.stage[:, 512 * k:512 * (k + 1)]).then_inc(s_v.h, 1)
                        ph.em("vector", f_lo)
            if "spill" in skip or "tr" in skip or "tr1" in skip:
                continue
            s_st.v += 32
            need_v = s_v.v
            need_a = s_a.v
            def f_spill(e, it=it, need_v=need_v, need_a=need_a):
                e.wait_ge(s_v.h, need_v)
                e.wait_ge(s_a.h, need_a)
                for src, dst in ((b.stage, b.xnhi_d), (b.stage2, b.xnlo_d)):
                    out = bass.AP(dst, 128 * it, [[T, 128], [128 * T, 16], [1, 128]])
                    in_ = bass.AP(src, 0, [[D, 128], [128, 16], [1, 128]])
                    e.dma_start(out=out, in_=in_).then_inc(s_st.h, 16)
            ph.em("sync", f_spill)

    def p15(self, ph, pst):
        b = self
        s_v, s_a, s_t, s_ld2, s_ld = b.s_v, b.s_a, b.s_t, b.s_ld2, b.s_ld
        ptr = b.psum(f"p15tr", [128, 128], F32, pst)
        s_ld.v += 16
        ph.em("sync", lambda e: e.dma_start(
            out=b.possb[:], in_=bass.AP(b.pos_in, 0, [[0, 128], [1, T]])).then_inc(s_ld.h, 16))

        s_t.v += 1
        t_tr = s_t.v
        need_v = s_v.v
        ph.em("tensor", lambda e, t=t_tr, nv=need_v: (
            e.wait_ge(s_v.h, nv),
            e.transpose(ptr[0:NT, :], b.dq_cols[:], b.idf[:]).then_inc(s_t.h, 1),
        ))
        s_a.v += 1
        a_ev = s_a.v
        ph.em("scalar", lambda e, t=t_tr: (
            e.wait_ge(s_t.h, t),
            e.activation(out=b.evrow[0:NT, :], in_=ptr[0:NT, :], func=ACT.Copy,
                         bias=0.0, scale=1.0).then_inc(s_a.h, 1),
        ))
        s_ld2.v += 32
        ld2 = s_ld2.v
        ph.em("scalar", lambda e, a=a_ev, ld2=ld2: (
            e.wait_ge(s_a.h, a),
            e.dma_start(out=bass.AP(b.row_d, 0, [[128, NT], [1, 128]]),
                        in_=bass.AP(b.evrow, 0, [[128, NT], [1, 128]])).then_inc(s_ld2.h, 16),
            e.wait_ge(s_ld2.h, ld2 - 16),
            e.dma_start(out=b.dqb[:], in_=bass.AP(b.row_d, 0, [[0, 128], [1, T]])).then_inc(s_ld2.h, 16),
        ))

        c1, c2, c3 = _cw2pi()
        hp1, hp2 = _halfpi2()
        inv2pi = float(np.float32(1.0 / (2.0 * np.pi)))
        pi_clamp = 3.1415925
        prev_a0 = s_a.v
        s_v.v += 1
        v_ms = s_v.v
        def f_ms(e, prev_a0=prev_a0):
            e.wait_ge(s_a.h, prev_a0)
            e.memset(b.c1[:], THETA).then_inc(s_v.h, 1)
        ph.em("vector", f_ms)
        s_a.v += 1
        a_ln = s_a.v
        def f_ln(e, v_ms=v_ms):
            e.wait_ge(s_v.h, v_ms)
            e.activation(out=b.c2[:], in_=b.c1[:], func=ACT.Ln, bias=0.0, scale=1.0).then_inc(s_a.h, 1)
        ph.em("scalar", f_ln)
        s_v.v += 1
        v_fr = s_v.v
        def f_fr(e, a_ln=a_ln):
            e.wait_ge(s_a.h, a_ln)
            e.tensor_scalar(b.c3[:], b.xfrsb[:], b.c2[:, 0:1], None, op0=ALU.mult).then_inc(s_v.h, 1)
        ph.em("vector", f_fr)
        s_a.v += 1
        a_fr = s_a.v
        ph.em("scalar", lambda e, v=v_fr: (
            e.wait_ge(s_v.h, v),
            e.activation(out=b.c4[:], in_=b.c3[:], func=ACT.Exp, bias=0.0, scale=1.0).then_inc(s_a.h, 1),
        ))
        ld_now = b.s_ld.v
        for ipass in range(2):
            prev_a2 = s_a.v
            s_v.v += 1
            v_ang = s_v.v
            def f_ang(e, ipass=ipass, prev_a2=prev_a2, ld_now=ld_now):
                e.wait_ge(s_a.h, prev_a2)
                e.wait_ge(b.s_ld.h, ld_now)
                ap_t = b.possb[:, 0:T]
                e.tensor_scalar(b.ta[:, 0:T], ap_t, b.c4[:, 0:1], None, op0=ALU.mult)
                if ipass == 1:
                    e.tensor_scalar(b.ta[:, 0:T], b.ta[:, 0:T], hp1, hp2, op0=ALU.add, op1=ALU.add)
                e.tensor_scalar(b.tb[:, 0:T], b.ta[:, 0:T], inv2pi, None, op0=ALU.mult)
                e.tensor_copy(b.ti[:, 0:T], b.tb[:, 0:T])
                e.tensor_copy(b.tb[:, 0:T], b.ti[:, 0:T])
                e.scalar_tensor_tensor(b.tc_[:, 0:T], b.tb[:, 0:T], -c1, b.ta[:, 0:T],
                                       op0=ALU.mult, op1=ALU.add)
                e.scalar_tensor_tensor(b.tc_[:, 0:T], b.tb[:, 0:T], -c2, b.tc_[:, 0:T],
                                       op0=ALU.mult, op1=ALU.add)
                e.scalar_tensor_tensor(b.tc_[:, 0:T], b.tb[:, 0:T], -c3, b.tc_[:, 0:T],
                                       op0=ALU.mult, op1=ALU.add)
                e.tensor_single_scalar(b.td[:, 0:T], b.tc_[:, 0:T], float(np.float32(np.pi)), op=ALU.is_gt)
                e.tensor_single_scalar(b.te[:, 0:T], b.tc_[:, 0:T], float(np.float32(-np.pi)), op=ALU.is_lt)
                e.tensor_sub(b.td[:, 0:T], b.te[:, 0:T], b.td[:, 0:T])
                e.scalar_tensor_tensor(b.tc_[:, 0:T], b.td[:, 0:T], float(np.float32(2 * np.pi)),
                                       b.tc_[:, 0:T], op0=ALU.mult, op1=ALU.add)
                e.tensor_scalar_min(b.tc_[:, 0:T], b.tc_[:, 0:T], pi_clamp)
                e.tensor_scalar_max(b.tc_[:, 0:T], b.tc_[:, 0:T], -pi_clamp).then_inc(s_v.h, 1)
            ph.em("vector", f_ang)
            s_a.v += 1
            a_sin = s_a.v
            tgt = [b.stabK, b.ctabK][ipass]
            ph.em("scalar", lambda e, v=v_ang, tgt=tgt: (
                e.wait_ge(s_v.h, v),
                e.activation(out=tgt[:], in_=b.tc_[:, 0:T], func=ACT.Sin, bias=0.0, scale=1.0).then_inc(s_a.h, 1),
            ))
        s_v.v += 1
        def f_tab(e, a=s_a.v, ld2=s_ld2.v):
            e.wait_ge(s_a.h, a)
            e.wait_ge(s_ld2.h, ld2)
            e.tensor_scalar(b.stabK[:], b.stabK[:], b.sgnsb[:, 0:1], None, op0=ALU.mult)
            e.tensor_mul(b.stabK[:], b.stabK[:], b.dqb[:])
            e.tensor_mul(b.ctabK[:], b.ctabK[:], b.dqb[:])
            e.tensor_scalar_mul(b.stabQ[:], b.stabK[:], SCORE_SCALE)
            e.tensor_scalar_mul(b.ctabQ[:], b.ctabK[:], SCORE_SCALE).then_inc(s_v.h, 1)
        ph.em("vector", f_tab)

    def p2(self, ph, pst):
        b = self
        s_ld, s_ld2, s_v, s_a, s_t = b.s_ld, b.s_ld2, b.s_v, b.s_a, b.s_t
        ps_qkv = [b.psum(f"p2qkv{i}", [128, 1024], F32, pst) for i in range(2)]
        trpv = [b.psum(f"p2trv{i}", [128, 512], F32, pst) for i in range(2)]
        wtile = [b.sbuf(f"p2wt{i}", [128, D], F32, stack=pst, side="right") for i in range(2)]
        whi = [b.sbuf(f"p2whi{i}", [128, D], BF16, stack=pst, side="right") for i in range(2)]
        wlo = [b.sbuf(f"p2wlo{i}", [128, D], BF16, stack=pst, side="right") for i in range(2)]
        qf = b.sbuf("p2qf", [128, T], F32, stack=pst, side="right")
        qsw = b.sbuf("p2qsw", [128, T], F32, stack=pst, side="right")
        qr = b.sbuf("p2qr", [128, T], F32, stack=pst, side="right")
        kst_hi = b.sbuf("p2ksth", [128, T], BF16, stack=pst, side="right")
        kst_lo = b.sbuf("p2kstl", [128, T], BF16, stack=pst, side="right")
        vst_hi = b.sbuf("p2vsth", [128, T], BF16, stack=pst, side="right")
        vst_lo = b.sbuf("p2vstl", [128, T], BF16, stack=pst, side="right")
        swapmask = [i ^ 1 for i in range(32)]

        conv_v = {}
        mm_t = {}
        epi_a = {}
        epi_v = {}
        ld_v = {}

        order = list(range(16, 24)) + list(range(16))
        for pos, hc in enumerate(order):
            prev1 = order[pos - 1] if pos >= 1 else -99
            prev2 = order[pos - 2] if pos >= 2 else -99
            buf = hc % 2
            s_ld.v += 16
            ld_v[hc] = s_ld.v
            wait_conv = conv_v.get(prev2, 0)
            def f_ld(e, hc=hc, buf=buf, wait_conv=wait_conv):
                e.wait_ge(s_v.h, wait_conv)
                in_ = bass.AP(b.wqkvT, 128 * hc, [[HQKV, 128], [128 * HQKV, NDC], [1, 128]])
                e.dma_start(out=wtile[buf][:], in_=in_).then_inc(s_ld.h, 16)
            ph.em("sync", f_ld)
            wait_mm = mm_t.get(prev2, 0)
            s_v.v += 2
            conv_v[hc] = s_v.v
            def f_cv(e, hc=hc, buf=buf, wait_mm=wait_mm, ldv=ld_v[hc]):
                e.wait_ge(s_ld.h, ldv)
                e.wait_ge(s_t.h, wait_mm)
                e.tensor_copy(whi[buf][:], wtile[buf][:]).then_inc(s_v.h, 1)
                e.tensor_sub(wlo[buf][:], wtile[buf][:], whi[buf][:]).then_inc(s_v.h, 1)
            ph.em("vector", f_cv)
            wait_ea = epi_a.get(prev2, 0)
            wait_ev = epi_v.get(prev2, 0)
            s_t.v += 1
            mm_t[hc] = s_t.v
            def f_mm(e, hc=hc, buf=buf, cv=conv_v[hc], wait_ea=wait_ea, wait_ev=wait_ev):
                e.wait_ge(s_v.h, cv)
                e.wait_ge(s_a.h, wait_ea)
                e.wait_ge(s_v.h, wait_ev)
                for dc in range(NDC):
                    for w in (whi[buf], wlo[buf]):
                        for th in range(2):
                            li = e.matmul(ps_qkv[buf][:, 512 * th:512 * (th + 1)],
                                          w[:, 128 * dc:128 * (dc + 1)],
                                          b.rT[:, dc * T + 512 * th: dc * T + 512 * (th + 1)],
                                          start=(dc == 0 and w is whi[buf]),
                                          stop=(dc == NDC - 1 and w is wlo[buf]))
                li.then_inc(s_t.h, 1)
            ph.em("tensor", f_mm)
            if hc < 16:
                s_a.v += 1
                a_ev = s_a.v
                def f_ev(e, buf=buf, mmv=mm_t[hc], pv=epi_v.get(prev1, 0)):
                    e.wait_ge(s_t.h, mmv)
                    e.wait_ge(s_v.h, pv)
                    e.activation(out=qf[:], in_=ps_qkv[buf][:], func=ACT.Copy,
                                 bias=0.0, scale=1.0).then_inc(s_a.h, 1)
                ph.em("scalar", f_ev)
                s_v.v += 4
                v_rope = s_v.v
                def f_rope(e, a_ev=a_ev):
                    e.wait_ge(s_a.h, a_ev)
                    e.stream_shuffle(qsw[:], qf[:], swapmask)
                    e.tensor_mul(qr[:], qf[:], b.ctabQ[:])
                    e.tensor_mul(qsw[:], qsw[:], b.stabQ[:])
                    e.tensor_add(qr[:], qr[:], qsw[:]).then_inc(s_v.h, 4)
                ph.em("vector", f_rope)
                s_a.v += 1
                a_hi = s_a.v
                def f_hi(e, v_rope=v_rope, pstore=b.s_ld2.v):
                    e.wait_ge(s_v.h, v_rope)
                    e.wait_ge(s_ld2.h, pstore)
                    e.activation(out=b.stage[:, 0:T], in_=qr[:],
                                 func=ACT.Copy, bias=0.0, scale=1.0).then_inc(s_a.h, 1)
                ph.em("scalar", f_hi)
                s_v.v += 1
                v_lo = s_v.v
                def f_lo(e, a_hi=a_hi):
                    e.wait_ge(s_a.h, a_hi)
                    e.tensor_sub(b.stage2[:, 0:T], qr[:], b.stage[:, 0:T]).then_inc(s_v.h, 1)
                ph.em("vector", f_lo)
                s_ld2.v += 32
                def f_st(e, hc=hc, v_lo=v_lo, a_hi=a_hi):
                    e.wait_ge(s_v.h, v_lo)
                    e.wait_ge(s_a.h, a_hi)
                    e.dma_start(out=b.qhi_d[128 * hc:128 * (hc + 1), :],
                                in_=b.stage[:, 0:T]).then_inc(s_ld2.h, 16)
                    e.dma_start(out=b.qlo_d[128 * hc:128 * (hc + 1), :],
                                in_=b.stage2[:, 0:T]).then_inc(s_ld2.h, 16)
                ph.em("scalar", f_st)
                epi_a[hc] = a_hi
                epi_v[hc] = v_lo
            elif hc < 20:
                g = hc - 16
                s_a.v += 1
                a_ev = s_a.v
                def f_ev(e, buf=buf, mmv=mm_t[hc], pv=epi_v.get(prev1, 0), pst_=0):
                    e.wait_ge(s_t.h, mmv)
                    e.wait_ge(s_v.h, pv)
                    e.activation(out=qf[:], in_=ps_qkv[buf][:], func=ACT.Copy,
                                 bias=0.0, scale=1.0).then_inc(s_a.h, 1)
                ph.em("scalar", f_ev)
                s_v.v += 4
                v_rope = s_v.v
                def f_rope(e, a_ev=a_ev, pstore=b.s_ld2.v):
                    e.wait_ge(s_a.h, a_ev)
                    e.wait_ge(s_ld2.h, pstore)
                    e.stream_shuffle(qsw[:], qf[:], swapmask)
                    e.tensor_mul(qr[:], qf[:], b.ctabK[:])
                    e.tensor_mul(qsw[:], qsw[:], b.stabK[:])
                    e.tensor_add(qr[:], qr[:], qsw[:]).then_inc(s_v.h, 4)
                ph.em("vector", f_rope)
                s_a.v += 1
                a_hi = s_a.v
                def f_hi(e, v_rope=v_rope):
                    e.wait_ge(s_v.h, v_rope)
                    e.activation(out=kst_hi[:], in_=qr[:], func=ACT.Copy,
                                 bias=0.0, scale=1.0).then_inc(s_a.h, 1)
                ph.em("scalar", f_hi)
                s_v.v += 1
                v_lo = s_v.v
                def f_lo(e, a_hi=a_hi):
                    e.wait_ge(s_a.h, a_hi)
                    e.tensor_sub(kst_lo[:], qr[:], kst_hi[:]).then_inc(s_v.h, 1)
                ph.em("vector", f_lo)
                s_ld2.v += 32
                def f_st(e, g=g, v_lo=v_lo, a_hi=a_hi):
                    e.wait_ge(s_v.h, v_lo)
                    e.wait_ge(s_a.h, a_hi)
                    e.dma_start(out=b.ag_in[128 * g:128 * (g + 1), :], in_=kst_hi[:]).then_inc(s_ld2.h, 16)
                    e.dma_start(out=b.ag_in[512 + 128 * g:512 + 128 * (g + 1), :], in_=kst_lo[:]).then_inc(s_ld2.h, 16)
                ph.em("scalar", f_st)
                epi_a[hc] = a_hi
                epi_v[hc] = v_lo
            else:
                g = hc - 20
                s_v.v += 1
                v_dq = s_v.v
                def f_dq(e, buf=buf, mmv=mm_t[hc], pv=epi_v.get(prev1, 0)):
                    e.wait_ge(s_t.h, mmv)
                    e.wait_ge(s_v.h, pv)
                    e.tensor_mul(qf[:], ps_qkv[buf][:], b.dqb[:]).then_inc(s_v.h, 1)
                ph.em("vector", f_dq)
                tr_t = []
                for kb in range(2):
                    s_t.v += 1
                    tr_t.append(s_t.v)
                    def f_tr(e, kb=kb, v_dq=v_dq, pa=epi_a.get(prev1, 0), pvv=epi_v.get(prev1, 0)):
                        e.wait_ge(s_v.h, v_dq)
                        e.wait_ge(s_a.h, pa)
                        e.wait_ge(s_v.h, pvv)
                        for j in range(4):
                            ts = 4 * kb + j
                            li = e.transpose(trpv[kb][:, 128 * j:128 * (j + 1)],
                                             qf[:, 128 * ts:128 * (ts + 1)], b.idf[:])
                        li.then_inc(s_t.h, 1)
                    ph.em("tensor", f_tr)
                s_a.v += 2
                a_hi = s_a.v
                def f_hi(e, tr_t=tuple(tr_t), pst2=b.s_ld2.v):
                    e.wait_ge(s_ld2.h, pst2)
                    for kb in range(2):
                        e.wait_ge(s_t.h, tr_t[kb])
                        e.activation(out=vst_hi[:, 512 * kb:512 * (kb + 1)], in_=trpv[kb][:],
                                     func=ACT.Copy, bias=0.0, scale=1.0).then_inc(s_a.h, 1)
                ph.em("scalar", f_hi)
                s_v.v += 2
                v_lo = s_v.v
                def f_lo(e, a_hi=a_hi):
                    e.wait_ge(s_a.h, a_hi)
                    for kb in range(2):
                        e.tensor_sub(vst_lo[:, 512 * kb:512 * (kb + 1)], trpv[kb][:],
                                     vst_hi[:, 512 * kb:512 * (kb + 1)]).then_inc(s_v.h, 1)
                ph.em("vector", f_lo)
                s_ld2.v += 32
                def f_st(e, g=g, v_lo=v_lo, a_hi=a_hi):
                    e.wait_ge(s_v.h, v_lo)
                    e.wait_ge(s_a.h, a_hi)
                    for src, base in ((vst_hi, 1024 * T), (vst_lo, 1536 * T)):
                        out = bass.AP(b.ag_in, base + 128 * g,
                                      [[512, 128], [128 * 512, 8], [1, 128]])
                        in_ = bass.AP(src, 0, [[T, 128], [128, 8], [1, 128]])
                        e.dma_start(out=out, in_=in_).then_inc(s_ld2.h, 16)
                ph.em("scalar", f_st)
                epi_a[hc] = a_hi
                epi_v[hc] = v_lo
            if hc == 23:
                b.s_g.v += 1
                def f_ag(e, need=s_ld2.v):
                    e.wait_ge(s_ld2.h, need)
                    e.collective_compute(
                        "AllGather", ALU.bypass,
                        replica_groups=[list(range(NC))],
                        ins=[b.ag_in[:, :]],
                        outs=[b.ag_out[:, :]],
                    ).then_inc(b.s_g.h, 1)
                ph.em("gpsimd", f_ag)

    def p3(self, ph):
        b = self
        gv = b.s_g.v
        for eng in ("sync", "vector", "scalar", "tensor"):
            ph.em(eng, lambda e, gv=gv: e.wait_ge(b.s_g.h, gv))

    def declare_pB(self, stack):
        sb = self.sbuf
        self.r2T = sb("r2T", [128, NDC * T], BF16, stack=stack, side="right")

    def declare_p4locals(self, stack):
        sb = self.sbuf
        self.qtlhi = [sb("qtlhi0", [128, D], BF16, stack=stack, side="right")] * 2
        self.qtllo = [sb("qtllo0", [128, D], BF16, stack=stack, side="right")] * 2
        self.khi = [sb(f"khi{i}", [128, NKV * 1024], BF16, stack=stack, side="right") for i in range(2)]
        self.klo = [sb(f"klo{i}", [128, NKV * 1024], BF16, stack=stack, side="right") for i in range(2)]
        self.vhi = [sb(f"vhi{i}", [128, NC * 512], BF16, stack=stack, side="right") for i in range(2)]
        self.vlo = [sb(f"vlo{i}", [128, NC * 512], BF16, stack=stack, side="right") for i in range(2)]
        self.phiT = sb("phiT", [128, 8 * 512], BF16, stack=stack, side="right")
        self.ploT = sb("ploT", [128, 8 * 512], BF16, stack=stack, side="right")
        self.fTrow = sb("fTrow", [1, 512], F32, stack=stack, side="right")
        self.fb_sb = sb("fb_sb", [128, 512], F32, stack=stack, side="right")
        self.q2b = sb("q2b", [128, 128], F32, stack=stack, side="right")
        self.colsum = sb("colsum", [128, 128], F32, stack=stack, side="right")
        self.O_sb = sb("O_sb", [128, NH * 128], F32, stack=stack, side="right")
        self.m_st = sb("m_st", [128, NH], F32, stack=stack, side="right")
        self.l_st = sb("l_st", [128, NH], F32, stack=stack, side="right")
        self.f_st = sb("f_st", [128, NH], F32, stack=stack, side="right")
        self.cs_st = sb("cs_st", [128, NH], F32, stack=stack, side="right")
        self.nmn_st = sb("nmn_st", [128, NH], F32, stack=stack, side="right")
        self.tm1 = sb("tm1", [128, 1], F32, stack=stack, side="right")

    def p4(self, ph, pst):
        b = self
        self.declare_p4locals(pst)
        s_ld, s_ld2, s_v, s_a, s_t = b.s_ld, b.s_ld2, b.s_v, b.s_a, b.s_t
        ps_sc = [b.psum(f"p4sc{i}", [128, 1024], F32, pst) for i in range(2)]
        trp = [b.psum(f"p4tr{i}", [128, 512], F32, pst) for i in range(2)]
        ps_pv = [b.psum(f"p4pv{i}", [128, 512], F32, pst) for i in range(2)]
        p32 = b.ta

        n_qt = self.cfg.get("n_qt", NT)
        hseq = 0
        exp_a = {}
        msk_v = {}
        pv_done_v = {}
        tr_a = {}
        tr_v = {}

        s_v.v += 3
        def f_init(e):
            e.memset(b.O_sb[:], 0.0)
            e.memset(b.l_st[:], 0.0)
            e.memset(b.m_st[:], NEG_BIG).then_inc(s_v.h, 3)
        ph.em("vector", f_init)
        s_ld2.v += 32
        def f_q0(e):
            for src, dst in ((b.qhi_d, b.qtlhi[0]), (b.qlo_d, b.qtllo[0])):
                in_ = bass.AP(src, 0, [[T, 128], [128 * T, NDC], [1, 128]])
                e.dma_start(out=dst[:], in_=in_).then_inc(s_ld2.h, 16)
        ph.em("scalar", f_q0)

        vis = 0
        sc_t = {}
        trp_t = {}
        pv_t = {}
        kv_free = {}
        kv_ld = {}
        pend = []

        gq_pv_t = {}
        gq_oadd = {}
        pend_gt = [None]

        def flush_gt():
            if pend_gt[0] is None:
                return
            pg, pgrp, ppvb, pape = pend_gt[0]
            pend_gt[0] = None
            s_t.v += 1
            t_pvt = s_t.v
            def f_pvt(e, pvb=ppvb, a_pe=pape):
                e.wait_ge(s_a.h, a_pe)
                for p2 in range(4):
                    li = e.transpose(pvb[:, 128 * p2:128 * (p2 + 1)],
                                     b.fb_sb[:, 128 * p2:128 * (p2 + 1)], b.idf[:])
                li.then_inc(s_t.h, 1)
            ph.em("tensor", f_pvt)
            s_v.v += 4
            def f_oa(e, g=pgrp, pvb=ppvb, t_pvt=t_pvt):
                e.wait_ge(s_t.h, t_pvt)
                for p2 in range(4):
                    hh2 = 4 * g + p2
                    li = e.scalar_tensor_tensor(b.O_sb[:, 128 * hh2:128 * (hh2 + 1)],
                                                b.O_sb[:, 128 * hh2:128 * (hh2 + 1)],
                                                b.f_st[:, hh2:hh2 + 1],
                                                pvb[:, 128 * p2:128 * (p2 + 1)],
                                                op0=ALU.mult, op1=ALU.add)
                li.then_inc(s_v.h, 4)
            ph.em("vector", f_oa)
            gq_oadd[pg] = s_v.v

        def emit_head_pe_tail(hp):
            info = pend.pop(0)
            assert info[0] == hp
            g_, vb_, visidx, hh = info[1], info[2], info[3], info[4]
            p_ = hh % 4
            gseq = hp // 4
            t_trs = []
            for kb in range(2):
                s_t.v += 1
                t_trs.append(s_t.v)
                def f_tr(e, kb=kb, hp=hp, ea=exp_a[hp], pa=tr_a.get(hp - 1, 0),
                         pv_=tr_v.get(hp - 1, 0)):
                    e.wait_ge(s_a.h, ea)
                    e.wait_ge(s_a.h, pa)
                    e.wait_ge(s_v.h, pv_)
                    p32v = b.ta[:, 1024 * (hp % 2):1024 * (hp % 2 + 1)]
                    for j in range(4):
                        kc = 4 * kb + j
                        li = e.transpose(trp[kb][:, 128 * j:128 * (j + 1)],
                                         p32v[:, 128 * kc:128 * (kc + 1)], b.idf[:])
                    li.then_inc(s_t.h, 1)
                ph.em("tensor", f_tr)
            trp_t[hp] = tuple(t_trs)
            s_a.v += 2
            a_phi = s_a.v
            def f_phi(e, t_trs=tuple(t_trs), p_=p_, ppv=gq_pv_t.get(gseq - 1, 0)):
                e.wait_ge(s_t.h, ppv)
                for kb in range(2):
                    e.wait_ge(s_t.h, t_trs[kb])
                    out = bass.AP(b.phiT, (4 * kb) * 512 + 128 * p_,
                                  [[8 * 512, 128], [512, 4], [1, 128]])
                    e.activation(out=out, in_=trp[kb][:], func=ACT.Copy,
                                 bias=0.0, scale=1.0).then_inc(s_a.h, 1)
            ph.em("scalar", f_phi)
            tr_a[hp] = s_a.v
            s_v.v += 2
            def f_plo(e, a_phi=a_phi, p_=p_):
                e.wait_ge(s_a.h, a_phi)
                for kb in range(2):
                    out = bass.AP(b.ploT, (4 * kb) * 512 + 128 * p_,
                                  [[8 * 512, 128], [512, 4], [1, 128]])
                    in1 = bass.AP(b.phiT, (4 * kb) * 512 + 128 * p_,
                                  [[8 * 512, 128], [512, 4], [1, 128]])
                    e.tensor_sub(out, trp[kb][:], in1).then_inc(s_v.h, 1)
            ph.em("vector", f_plo)
            tr_v[hp] = s_v.v
            if p_ != 3:
                return
            pvb = ps_pv[gseq % 2]
            s_t.v += 1
            gq_pv_t[gseq] = s_t.v
            def f_pv(e, g=g_, pvb=pvb, vb=vb_, kvld=kv_ld[visidx],
                     ta_=tr_a[hp], tv_=tr_v[hp], po=gq_oadd.get(gseq - 2, 0)):
                e.wait_ge(s_a.h, ta_)
                e.wait_ge(s_v.h, tv_)
                e.wait_ge(s_v.h, po)
                e.wait_ge(s_ld.h, kvld)
                for kc in range(NC):
                    vh = b.vhi[vb][:, 512 * kc + 128 * g:512 * kc + 128 * (g + 1)]
                    vl = b.vlo[vb][:, 512 * kc + 128 * g:512 * kc + 128 * (g + 1)]
                    pT = b.phiT[:, 512 * kc:512 * (kc + 1)]
                    pTl = b.ploT[:, 512 * kc:512 * (kc + 1)]
                    e.matmul(pvb[:], vh, pT, start=(kc == 0), stop=False)
                    e.matmul(pvb[:], vh, pTl, start=False, stop=False)
                    li = e.matmul(pvb[:], vl, pT, start=False, stop=(kc == NC - 1))
                li.then_inc(s_t.h, 1)
            ph.em("tensor", f_pv)
            flush_gt()
            s_a.v += 1
            a_pe = s_a.v
            def f_pe(e, pvb=pvb, pvt=s_t.v):
                e.wait_ge(s_t.h, pvt)
                e.activation(out=b.fb_sb[:], in_=pvb[:], func=ACT.Copy,
                             bias=0.0, scale=1.0).then_inc(s_a.h, 1)
            ph.em("scalar", f_pe)
            pend_gt[0] = (gseq, g_, pvb, a_pe)
            if hh == NH - 1:
                kv_free[visidx] = gq_pv_t[gseq]

        for I in range(n_qt):
            qb = I % 2
            q_ld2 = s_ld2.v
            for J in range(I + 1):
                vb = vis % 2
                s_ld.v += 16 * 10
                kv_ld[vis] = s_ld.v
                def f_kv(e, J=J, vb=vb, free_t=kv_free.get(vis - 2, 0)):
                    e.wait_ge(s_t.h, free_t)
                    for g in range(NKV):
                        e.dma_start(out=b.khi[vb][:, 1024 * g:1024 * (g + 1)],
                                    in_=bass.AP(b.ag_out, 128 * g * T + 128 * J,
                                                [[T, 128], [2048 * T, NC], [1, 128]])
                                    ).then_inc(s_ld.h, 16)
                    for g in range(NKV):
                        e.dma_start(out=b.klo[vb][:, 1024 * g:1024 * (g + 1)],
                                    in_=bass.AP(b.ag_out, (512 + 128 * g) * T + 128 * J,
                                                [[T, 128], [2048 * T, NC], [1, 128]])
                                    ).then_inc(s_ld.h, 16)
                    e.dma_start(out=b.vhi[vb][:],
                                in_=bass.AP(b.ag_out, 1024 * T + 128 * J * 512, [[512, 128], [2048 * T, NC], [1, 512]])
                                ).then_inc(s_ld.h, 16)
                    e.dma_start(out=b.vlo[vb][:],
                                in_=bass.AP(b.ag_out, 1536 * T + 128 * J * 512, [[512, 128], [2048 * T, NC], [1, 512]])
                                ).then_inc(s_ld.h, 16)
                ph.em("sync", f_kv)
                for h in range(NH):
                    g = h // 4
                    bank = ps_sc[hseq % 2]
                    s_t.v += 1
                    sc_t[hseq] = s_t.v
                    def f_sc(e, h=h, g=g, bank=bank, kvld=kv_ld[vis], qb=qb, q_ld2=q_ld2,
                             pe=exp_a.get(hseq - 2, 0), pm=msk_v.get(hseq - 2, 0), vb=vb):
                        e.wait_ge(s_ld.h, kvld - 32)
                        e.wait_ge(s_ld2.h, q_ld2)
                        e.wait_ge(s_a.h, pe)
                        e.wait_ge(s_v.h, pm)
                        lh = b.qtlhi[qb][:, 128 * h:128 * (h + 1)]
                        ll = b.qtllo[qb][:, 128 * h:128 * (h + 1)]
                        outs, rhss, rlos = [], [], []
                        for kh in range(2):
                            outs.append(bank[:, 512 * kh:512 * (kh + 1)])
                            rhss.append(b.khi[vb][:, 1024 * g + 512 * kh:1024 * g + 512 * (kh + 1)])
                            rlos.append(b.klo[vb][:, 1024 * g + 512 * kh:1024 * g + 512 * (kh + 1)])
                        for kh in range(2):
                            e.matmul(outs[kh], lh, rhss[kh], start=True, stop=False)
                        for kh in range(2):
                            e.matmul(outs[kh], lh, rlos[kh], start=False, stop=False)
                        for kh in range(2):
                            li = e.matmul(outs[kh], ll, rhss[kh], start=False, stop=True)
                        li.then_inc(s_t.h, 1)
                    ph.em("tensor", f_sc)
                    if pend:
                        emit_head_pe_tail(hseq - 1)
                    nv = 5 if J == I else 4
                    s_v.v += nv
                    def f_mx(e, h=h, bank=bank, sct=sc_t[hseq], diag=(J == I), nv=nv):
                        e.wait_ge(s_t.h, sct)
                        if diag:
                            e.tensor_add(bank[:], bank[:], b.masksb[:])
                        e.reduce_max(b.tm1[:], bank[:], axis=AXX)
                        Build.sp(e)
                        e.tensor_scalar(b.f_st[:, h:h + 1], b.m_st[:, h:h + 1], b.tm1[:, 0:1],
                                        0.0, op0=ALU.subtract, op1=ALU.min)
                        e.tensor_scalar(b.nmn_st[:, h:h + 1], b.m_st[:, h:h + 1], b.tm1[:, 0:1],
                                        -1.0, op0=ALU.max, op1=ALU.mult)
                        e.tensor_scalar(b.m_st[:, h:h + 1], b.m_st[:, h:h + 1], b.tm1[:, 0:1],
                                        None, op0=ALU.max).then_inc(s_v.h, nv)
                    ph.em("vector", f_mx)
                    msk_v[hseq] = s_v.v
                    s_a.v += 2
                    def f_exp(e, h=h, hp=hseq, bank=bank, mv=msk_v[hseq],
                              ptr_v=trp_t.get(hseq - 2, (0, 0))):
                        e.wait_ge(s_v.h, mv)
                        e.wait_ge(s_t.h, ptr_v[1])
                        p32v = b.ta[:, 1024 * (hp % 2):1024 * (hp % 2 + 1)]
                        e.activation(out=b.f_st[:, h:h + 1], in_=b.f_st[:, h:h + 1],
                                     func=ACT.Exp, bias=0.0, scale=1.0).then_inc(s_a.h, 1)
                        e.activation(out=p32v, in_=bank[:], func=ACT.Exp,
                                     bias=b.nmn_st[:, h:h + 1], scale=1.0,
                                     accum_out=b.cs_st[:, h:h + 1]).then_inc(s_a.h, 1)
                    ph.em("scalar", f_exp)
                    exp_a[hseq] = s_a.v
                    pend.append((hseq, g, vb, vis, h))
                    hseq += 1
                s_v.v += 2
                def f_l(e, ea=s_a.v):
                    e.wait_ge(s_a.h, ea)
                    e.tensor_mul(b.l_st[:], b.l_st[:], b.f_st[:])
                    Build.sp(e)
                    e.tensor_add(b.l_st[:], b.l_st[:], b.cs_st[:]).then_inc(s_v.h, 2)
                ph.em("vector", f_l)
                vis += 1
            if pend:
                emit_head_pe_tail(hseq - 1)
            if I + 1 < n_qt:
                s_ld2.v += 32
                def f_qn(e, I=I, pe=s_t.v):
                    e.wait_ge(s_t.h, pe)
                    for src, dst in ((b.qhi_d, b.qtlhi[0]), (b.qlo_d, b.qtllo[0])):
                        in_ = bass.AP(src, 128 * (I + 1), [[T, 128], [128 * T, NDC], [1, 128]])
                        e.dma_start(out=dst[:], in_=in_).then_inc(s_ld2.h, 16)
                ph.em("scalar", f_qn)
            flush_gt()
            s_v.v += NH + 1
            v_att = s_v.v
            def f_att(e, prev=s_v.v):
                Build.sp(e)
                Build.sp(e)
                e.reciprocal(b.cs_st[:], b.l_st[:])
                Build.sp(e)
                Build.sp(e)
                for h in range(NH):
                    li = e.tensor_scalar(b.tb[:, 128 * h:128 * (h + 1)],
                                         b.O_sb[:, 128 * h:128 * (h + 1)],
                                         b.cs_st[:, h:h + 1], None, op0=ALU.mult)
                li.then_inc(s_v.h, NH + 1)
            ph.em("vector", f_att)
            s_v.v += 3
            v_ss = s_v.v
            def f_ss(e):
                e.tensor_mul(b.tc_[:], b.tb[:], b.tb[:])
                e.tensor_reduce(b.c1[:], b.tc_[:], axis=AXX, op=ALU.add)
                Build.sp(e)
                Build.sp(e)
                e.tensor_scalar_mul(b.c1[:], b.c1[:], 1.0 / D).then_inc(s_v.h, 3)
            ph.em("vector", f_ss)
            s_a.v += 1
            a_srt = s_a.v
            def f_srt(e, v_ss=v_ss):
                e.wait_ge(s_v.h, v_ss)
                e.activation(out=b.c2[:], in_=b.c1[:], func=ACT.Sqrt,
                             bias=b.epsc[:, 0:1], scale=1.0).then_inc(s_a.h, 1)
            ph.em("scalar", f_srt)
            s_v.v += 1
            v_quant = None
            def f_q(e, a_srt=a_srt, I=I):
                e.wait_ge(s_a.h, a_srt)
                e.reciprocal(b.c2[:], b.c2[:])
                Build.sp(e)
                e.tensor_scalar(b.tc_[:], b.tb[:], b.c2[:, 0:1], None, op0=ALU.mult)
                e.tensor_mul(b.tc_[:], b.tc_[:], b.nwsb[:])
                e.reduce_max(b.c1[:], b.tc_[:], axis=AXX, apply_absolute_value=True)
                Build.sp(e)
                e.tensor_scalar_max(b.c1[:], b.c1[:], 1e-5)
                Build.sp(e)
                e.reciprocal(b.c3[:], b.c1[:])
                Build.sp(e)
                e.tensor_scalar_mul(b.c3[:], b.c3[:], 127.0)
                Build.sp(e)
                e.reciprocal(b.c4[:], b.c3[:])
                Build.sp(e)
                e.tensor_copy(b.ds2_cols[:, I:I + 1], b.c4[:])
                e.tensor_scalar(b.te[:], b.tc_[:], b.c3[:, 0:1], None, op0=ALU.mult)
                e.tensor_copy(b.ti[:], b.te[:])
                e.tensor_copy(b.tc_[:], b.ti[:])
                e.tensor_sub(b.td[:], b.te[:], b.tc_[:])
                e.tensor_single_scalar(b.te[:], b.td[:], 0.5, op=ALU.is_ge)
                e.tensor_add(b.tc_[:], b.tc_[:], b.te[:])
                e.tensor_single_scalar(b.te[:], b.td[:], -0.5, op=ALU.is_le)
                e.tensor_sub(b.td[:], b.tc_[:], b.te[:])
                if self.cfg.get("p4_no_reset"):
                    e.memset(b.j0[:], 0.0)
                    e.memset(b.j0[:], 0.0)
                    e.memset(b.j1[:], 0.0).then_inc(s_v.h, 1)
                else:
                    e.memset(b.O_sb[:], 0.0)
                    e.memset(b.l_st[:], 0.0)
                    e.memset(b.m_st[:], NEG_BIG).then_inc(s_v.h, 1)
            ph.em("vector", f_q)
            v_quant = s_v.v
            for k in range(4):
                tbank = trp[k % 2]
                s_t.v += 1
                t_tr = s_t.v
                def f_tr2(e, k=k, tbank=tbank, vq=v_quant, pa=s_a.v):
                    e.wait_ge(s_v.h, vq)
                    e.wait_ge(s_a.h, pa)
                    for j in range(4):
                        dc = 4 * k + j
                        li = e.transpose(tbank[:, 128 * j:128 * (j + 1)],
                                         b.td[:, 128 * dc:128 * (dc + 1)], b.idf[:])
                    li.then_inc(s_t.h, 1)
                ph.em("tensor", f_tr2)
                s_a.v += 1
                def f_ev2(e, k=k, tbank=tbank, t_tr=t_tr, I=I):
                    e.wait_ge(s_t.h, t_tr)
                    out = bass.AP(b.r2T, (4 * k) * T + 128 * I,
                                  [[NDC * T, 128], [T, 4], [1, 128]])
                    e.activation(out=out, in_=tbank[:], func=ACT.Copy,
                                 bias=0.0, scale=1.0).then_inc(s_a.h, 1)
                ph.em("scalar", f_ev2)
            tr_a[hseq - 1] = s_a.v
            tr_v[hseq - 1] = s_v.v
            gq_oadd[(hseq - 1) // 4] = s_v.v
            gq_pv_t[(hseq - 1) // 4] = s_t.v

    def p5(self, ph, pst):
        b = self
        s_ld, s_ld2, s_v, s_a, s_t, s_st = b.s_ld, b.s_ld2, b.s_v, b.s_a, b.s_t, b.s_st
        ps_wo = [b.psum(f"p5wo{i}", [128, 512], F32, pst) for i in range(NT)]
        wot = [b.sbuf(f"p5wot{i}", [128, 512], F32, stack=pst, side="right") for i in range(2)]
        woev = b.sbuf("p5woev", [128, 512], F32, stack=pst, side="right")
        conv_v = {}
        mm_t = {}
        ld_v = {}
        for dm in range(4):
            evprev_a = s_a.v
            evprev_st = s_st.v
            for dc in range(NDC):
                i = dm * NDC + dc
                buf = i % 2
                s_ld.v += 16
                ld_v[i] = s_ld.v
                def f_ld(e, dm=dm, dc=dc, buf=buf, wc=conv_v.get(i - 2, 0)):
                    e.wait_ge(s_v.h, wc)
                    e.dma_start(out=wot[buf][:],
                                in_=b.woT[128 * dc:128 * (dc + 1), 512 * dm:512 * (dm + 1)]
                                ).then_inc(s_ld.h, 16)
                ph.em("sync", f_ld)
                s_v.v += 2
                conv_v[i] = s_v.v
                def f_cv(e, buf=buf, ldv=ld_v[i], wm=mm_t.get(i - 2, 0)):
                    e.wait_ge(s_ld.h, ldv)
                    e.wait_ge(s_t.h, wm)
                    e.tensor_copy(b.stage[:, 512 * buf:512 * (buf + 1)], wot[buf][:]).then_inc(s_v.h, 1)
                    e.tensor_sub(b.stage2[:, 512 * buf:512 * (buf + 1)], wot[buf][:],
                                 b.stage[:, 512 * buf:512 * (buf + 1)]).then_inc(s_v.h, 1)
                ph.em("vector", f_cv)
                s_t.v += 1
                mm_t[i] = s_t.v
                def f_mm(e, dm=dm, dc=dc, buf=buf, cv=conv_v[i], ea=evprev_a):
                    e.wait_ge(s_v.h, cv)
                    e.wait_ge(s_a.h, ea)
                    for tt in range(NT):
                        lhsT = b.r2T[:, dc * T + 128 * tt:dc * T + 128 * (tt + 1)]
                        for w_i, w in enumerate((b.stage[:, 512 * buf:512 * (buf + 1)],
                                                 b.stage2[:, 512 * buf:512 * (buf + 1)])):
                            li = e.matmul(ps_wo[tt][:], lhsT, w,
                                          start=(dc == 0 and w_i == 0), stop=(dc == NDC - 1 and w_i == 1))
                    li.then_inc(s_t.h, 1)
                ph.em("tensor", f_mm)
            for tt in range(NT):
                s_a.v += 1
                a_ev = s_a.v
                def f_ev(e, tt=tt, dm=dm, mmv=s_t.v, pst_=b.s_ld3.v):
                    e.wait_ge(s_t.h, mmv)
                    e.wait_ge(b.s_ld3.h, pst_)
                    e.activation(out=woev[:], in_=ps_wo[tt][:], func=ACT.Copy, bias=0.0,
                                 scale=b.ds2_cols[:, tt:tt + 1]).then_inc(s_a.h, 1)
                ph.em("scalar", f_ev)
                b.s_ld3.v += 16
                def f_acc(e, tt=tt, dm=dm, a_ev=a_ev):
                    e.wait_ge(s_a.h, a_ev)
                    e.dma_start(out=b.y_out[128 * tt:128 * (tt + 1), 512 * dm:512 * (dm + 1)],
                                in_=woev[:], accum_op=ALU.add).then_inc(b.s_ld3.h, 16)
                ph.em("gpsimd", f_acc)

    def declare_pC(self, stack):
        sb = self.sbuf
        self.xnthi = sb("xnthi", [128, NDC * 512], BF16, stack=stack, side="right")
        self.xntlo = sb("xntlo", [128, NDC * 512], BF16, stack=stack, side="right")
        self.gubuf_hi = sb("gubuf_hi", [128, 11 * 512], BF16, stack=stack, side="right")
        self.gubuf_lo = sb("gubuf_lo", [128, 11 * 512], BF16, stack=stack, side="right")
        self.wgi = [sb(f"wgi{i}", [128, D], I32, stack=stack, side="right") for i in range(2)]
        self.wui = [sb(f"wui{i}", [128, D], I32, stack=stack, side="right") for i in range(2)]
        self.wgbf = [sb(f"wgbf{i}", [128, D], BF16, stack=stack, side="right") for i in range(2)]
        self.wubf = [sb(f"wubf{i}", [128, D], BF16, stack=stack, side="right") for i in range(2)]
        self.widm = [sb(f"widm{i}", [128, 11 * 128], I32, stack=stack, side="right") for i in range(2)]
        self.wdbf = [sb(f"wdbf{i}", [128, 11 * 128], BF16, stack=stack, side="right") for i in range(2)]
        self.gsil = sb("gsil", [128, 512], F32, stack=stack, side="right")
        self.gu32 = sb("gu32", [128, 512], F32, stack=stack, side="right")
        self.ffev = sb("ffev", [128, 512], F32, stack=stack, side="right")
        self.fftr = sb("fftr", [128, 512], F32, stack=stack, side="right")

    def p6(self, ph, pst):
        b = self
        s_ld, s_ld2, s_v, s_a, s_t, s_st = b.s_ld, b.s_ld2, b.s_v, b.s_a, b.s_t, b.s_st
        ps_g_ = [b.psum(f"p6g{i}", [128, 512], F32, pst) for i in range(2)]
        ps_u_ = [b.psum(f"p6u{i}", [128, 512], F32, pst) for i in range(2)]
        ps_d = [b.psum(f"p6d{i}", [128, 512], F32, pst) for i in range(2)]
        trp6 = [b.psum(f"p6tr{i}", [128, 512], F32, pst) for i in range(2)]

        NQ = 4
        HCQ = NHC // NQ

        for th in range(2):
            s_ld2.v += 32
            xn_ld = s_ld2.v
            prev_all = (s_t.v, s_v.v, s_a.v)
            def f_xn(e, th=th, prev_all=prev_all):
                e.wait_ge(s_t.h, prev_all[0])
                e.wait_ge(s_v.h, prev_all[1])
                e.wait_ge(s_a.h, prev_all[2])
                for src, dst in ((b.xnhi_d, b.xnthi), (b.xnlo_d, b.xntlo)):
                    in_ = bass.AP(src, 512 * th, [[T, 128], [128 * T, NDC], [1, 512]])
                    e.dma_start(out=dst[:], in_=in_).then_inc(s_ld2.h, 16)
            ph.em("scalar", f_xn)
            guv = {}
            gua = {}
            ldg = {}
            cvv = {}
            mmgu_t = {}
            for q in range(NQ):
                for hc in range(HCQ):
                    hqc = q * HCQ + hc
                    buf = hc % 2
                    s_ld.v += 32
                    ldg[hqc] = s_ld.v
                    def f_ld(e, hqc=hqc, buf=buf, wc=cvv.get(hqc - 2, 0)):
                        e.wait_ge(s_v.h, wc)
                        for w_in, dst in ((b.wgT, b.wgi[buf]), (b.wuT, b.wui[buf])):
                            in_ = bass.AP(w_in, 128 * hqc, [[HID, 128], [128 * HID, NDC], [1, 128]])
                            e.dma_start(out=dst[:], in_=in_).then_inc(s_ld.h, 16)
                    ph.em("sync", f_ld)
                    s_v.v += 2
                    cvv[hqc] = s_v.v
                    def f_cv(e, buf=buf, ldv=ldg[hqc], wm=mmgu_t.get(hqc - 2, 0)):
                        e.wait_ge(s_ld.h, ldv)
                        e.wait_ge(s_t.h, wm)
                        e.tensor_scalar(b.wgbf[buf][:], b.wgi[buf][:], 128.0, None,
                                        op0=ALU.subtract).then_inc(s_v.h, 1)
                        e.tensor_scalar(b.wubf[buf][:], b.wui[buf][:], 128.0, None,
                                        op0=ALU.subtract).then_inc(s_v.h, 1)
                    ph.em("vector", f_cv)
                    s_t.v += 1
                    mmgu_t[hqc] = s_t.v
                    def f_mm(e, th=th, buf=buf, cv=cvv[hqc], xn_ld=xn_ld,
                             pg=gua.get(hqc - 2, 0), pv=guv.get(hqc - 2, 0)):
                        e.wait_ge(s_v.h, cv)
                        e.wait_ge(s_ld2.h, xn_ld)
                        e.wait_ge(s_a.h, pg)
                        e.wait_ge(s_v.h, pv)
                        for W, psx in ((b.wgbf[buf], ps_g_[buf]), (b.wubf[buf], ps_u_[buf])):
                            for dc in range(NDC):
                                for xn in (b.xnthi, b.xntlo):
                                    li = e.matmul(psx[:], W[:, 128 * dc:128 * (dc + 1)],
                                                  xn[:, 512 * dc:512 * (dc + 1)],
                                                  start=(dc == 0 and xn is b.xnthi),
                                                  stop=(dc == NDC - 1 and xn is b.xntlo))
                        li.then_inc(s_t.h, 1)
                    ph.em("tensor", f_mm)
                    s_a.v += 1
                    a_sil = s_a.v
                    def f_sil(e, hqc=hqc, buf=buf, mm=mmgu_t[hqc], pv=guv.get(hqc - 1, 0)):
                        e.wait_ge(s_t.h, mm)
                        e.wait_ge(s_v.h, pv)
                        e.activation(out=b.gsil[:], in_=ps_g_[buf][:], func=ACT.Silu, bias=0.0,
                                     scale=b.sgsb[:, hqc:hqc + 1]).then_inc(s_a.h, 1)
                    ph.em("scalar", f_sil)
                    s_v.v += 1
                    v_gu = s_v.v
                    def f_gu(e, hqc=hqc, buf=buf, a_sil=a_sil):
                        e.wait_ge(s_a.h, a_sil)
                        e.scalar_tensor_tensor(b.gu32[:], ps_u_[buf][:], b.susb[:, hqc:hqc + 1],
                                               b.gsil[:], op0=ALU.mult, op1=ALU.mult).then_inc(s_v.h, 1)
                    ph.em("vector", f_gu)
                    s_a.v += 1
                    a_hi = s_a.v
                    def f_hi(e, hc=hc, v_gu=v_gu):
                        e.wait_ge(s_v.h, v_gu)
                        e.activation(out=b.gubuf_hi[:, 512 * hc:512 * (hc + 1)], in_=b.gu32[:],
                                     func=ACT.Copy, bias=0.0, scale=1.0).then_inc(s_a.h, 1)
                    ph.em("scalar", f_hi)
                    gua[hqc] = s_a.v
                    s_v.v += 1
                    def f_lo(e, hc=hc, a_hi=a_hi):
                        e.wait_ge(s_a.h, a_hi)
                        e.tensor_sub(b.gubuf_lo[:, 512 * hc:512 * (hc + 1)], b.gu32[:],
                                     b.gubuf_hi[:, 512 * hc:512 * (hc + 1)]).then_inc(s_v.h, 1)
                    ph.em("vector", f_lo)
                    guv[hqc] = s_v.v
                gu_all_v = s_v.v
                gu_all_a = s_a.v
                dconv = {}
                dmm = {}
                for dm in range(NDC):
                    buf = dm % 2
                    s_ld.v += 16
                    dl = s_ld.v
                    def f_ld2(e, q=q, dm=dm, buf=buf, wc=dconv.get(dm - 2, 0)):
                        e.wait_ge(s_v.h, wc)
                        in_ = bass.AP(b.wdT, (q * HCQ * 128) * D + 128 * dm,
                                      [[D, 128], [128 * D, HCQ], [1, 128]])
                        e.dma_start(out=b.widm[buf][:], in_=in_).then_inc(s_ld.h, 16)
                    ph.em("sync", f_ld2)
                    s_v.v += 1
                    dconv[dm] = s_v.v
                    def f_cv2(e, buf=buf, dl=dl, wm=dmm.get(dm - 2, 0)):
                        e.wait_ge(s_ld.h, dl)
                        e.wait_ge(s_t.h, wm)
                        e.tensor_scalar(b.wdbf[buf][:], b.widm[buf][:], 128.0, None,
                                        op0=ALU.subtract).then_inc(s_v.h, 1)
                    ph.em("vector", f_cv2)
                    s_t.v += 1
                    dmm[dm] = s_t.v
                    def f_dmm(e, dm=dm, buf=buf, cv=dconv[dm], guv_=gu_all_v, gua_=gu_all_a,
                              pe=s_a.v):
                        e.wait_ge(s_v.h, cv)
                        e.wait_ge(s_v.h, guv_)
                        e.wait_ge(s_a.h, gua_)
                        e.wait_ge(s_a.h, pe)
                        for hc in range(HCQ):
                            for gu in (b.gubuf_hi, b.gubuf_lo):
                                li = e.matmul(ps_d[buf][:], b.wdbf[buf][:, 128 * hc:128 * (hc + 1)],
                                              gu[:, 512 * hc:512 * (hc + 1)],
                                              start=(hc == 0 and gu is b.gubuf_hi),
                                              stop=(hc == HCQ - 1 and gu is b.gubuf_lo))
                        li.then_inc(s_t.h, 1)
                    ph.em("tensor", f_dmm)
                    s_a.v += 1
                    a_ev = s_a.v
                    def f_ev(e, dm=dm, buf=buf, mm=dmm[dm], ptr=s_t.v):
                        e.wait_ge(s_t.h, mm)
                        e.wait_ge(s_t.h, ptr)
                        e.activation(out=b.ffev[:], in_=ps_d[buf][:], func=ACT.Copy, bias=0.0,
                                     scale=b.sdsb[:, dm:dm + 1]).then_inc(s_a.h, 1)
                    ph.em("scalar", f_ev)
                    s_t.v += 1
                    t_tr0 = s_t.v
                    def f_tr(e, a_ev=a_ev, pa=s_a.v):
                        e.wait_ge(s_a.h, a_ev)
                        e.wait_ge(s_a.h, pa)
                        for j in range(4):
                            li = e.transpose(trp6[0][:, 128 * j:128 * (j + 1)],
                                             b.ffev[:, 128 * j:128 * (j + 1)], b.idf[:])
                        li.then_inc(s_t.h, 1)
                    ph.em("tensor", f_tr)
                    s_a.v += 1
                    a_ev2 = s_a.v
                    def f_ev2(e, t_tr=t_tr0, pst_=b.s_ld3.v):
                        e.wait_ge(s_t.h, t_tr)
                        e.wait_ge(b.s_ld3.h, pst_)
                        e.activation(out=b.fftr[:], in_=trp6[0][:], func=ACT.Copy, bias=0.0,
                                     scale=1.0).then_inc(s_a.h, 1)
                    ph.em("scalar", f_ev2)
                    b.s_ld3.v += 16
                    def f_acc(e, th=th, dm=dm, a_ev2=a_ev2):
                        e.wait_ge(s_a.h, a_ev2)
                        out = bass.AP(b.y_out, (512 * th) * D + 128 * dm,
                                      [[D, 128], [128 * D, 4], [1, 128]])
                        e.dma_start(out=out, in_=b.fftr[:], accum_op=ALU.add).then_inc(b.s_ld3.h, 16)
                    ph.em("gpsimd", f_acc)



def make_core_inputs(inputs, core):
    x = np.asarray(inputs["L0_attn_x"])[0]
    wqkv = np.asarray(inputs["L0_attn_wqkv"])
    wo = np.asarray(inputs["L0_attn_wo"])
    nw = np.asarray(inputs["L0_attn_norm_w"])
    wg = np.asarray(inputs["L0_ffn_wq_gate"])
    sgv = np.asarray(inputs["L0_ffn_s_gate"])
    wu = np.asarray(inputs["L0_ffn_wq_up"])
    suv = np.asarray(inputs["L0_ffn_s_up"])
    wd = np.asarray(inputs["L0_ffn_wq_down"])
    sdv = np.asarray(inputs["L0_ffn_s_down"])

    shared = dict(
        wqkvT=np.ascontiguousarray(wqkv.T),
        woT=np.ascontiguousarray(wo.T),
        nwrep=np.ascontiguousarray(
            np.broadcast_to(nw.reshape(NH, 128).T[:, :, None], (128, NH, 128)).reshape(128, D)
        ).astype(np.float32),
        wgT=np.ascontiguousarray(wg.T),
        wuT=np.ascontiguousarray(wu.T),
        wdT=np.ascontiguousarray(wd.T),
        sg=np.ascontiguousarray(sgv.reshape(NHC, 128).T).astype(np.float32),
        su=np.ascontiguousarray(suv.reshape(NHC, 128).T).astype(np.float32),
        sd=np.ascontiguousarray(sdv.reshape(NDC, 128).T).astype(np.float32),
        xfr=np.ascontiguousarray((-(np.arange(128) // 2) / 64.0).reshape(128, 1)).astype(np.float32),
        sgn=np.ascontiguousarray(np.where(np.arange(128) % 2 == 0, -1.0, 1.0).reshape(128, 1)).astype(np.float32),
        idf=np.eye(128, dtype=np.float32),
        onesf=np.ones((128, 128), dtype=np.float32),
        onec=np.ones((128, 1), dtype=np.float32),
    )
    xc = np.ascontiguousarray(x[core::NC])
    pos = (core + NC * np.arange(T)).astype(np.float32).reshape(1, T)
    i_idx = np.arange(128)[:, None]
    r_idx = (np.arange(1024)[None, :] // 128)
    m_idx = (np.arange(1024)[None, :] % 128)
    allowed = (m_idx < i_idx) | ((m_idx == i_idx) & (r_idx <= core))
    mask = np.where(allowed, 0.0, NEG_BIG).astype(np.float32)
    return dict(shared, x=xc, pos=pos, mask=mask)


def build_full(cfg=None):
    b = Build(cfg or dict(debug_outputs={}))
    ph = Phase(b.nc)
    b.p0(ph)
    ph.run()
    with ExitStack() as sA:
        b.declare_pA(sA)
        with ExitStack() as sP1:
            ph = Phase(b.nc); b.p1(ph, sP1); ph.run()
            ph = Phase(b.nc); b.p15(ph, sP1); ph.run()
        with ExitStack() as sP2:
            ph = Phase(b.nc); b.guard_all(ph); b.p2(ph, sP2); ph.run()
            ph = Phase(b.nc); b.p3(ph); ph.run()
    with ExitStack() as sB:
        b.declare_pB(sB)
        with ExitStack() as sP4:
            ph = Phase(b.nc); b.guard_all(ph); b.p4(ph, sP4); ph.run()
        with ExitStack() as sP5:
            ph = Phase(b.nc); b.guard_all(ph); b.p5(ph, sP5); ph.run()
    with ExitStack() as sC:
        b.declare_pC(sC)
        with ExitStack() as sP6:
            ph = Phase(b.nc); b.guard_all(ph); b.p6(ph, sP6); ph.run()
    ph = Phase(b.nc)
    b.guard_all(ph)
    ph.run()
    b.es.close()
    return b


def kernel(**inputs):
    b = build_full()
    in_maps = [make_core_inputs(inputs, c) for c in range(NC)]
    res = run_bass_kernel_spmd(b.nc, in_maps, core_ids=list(range(NC)))
    y = np.empty((S, D), dtype=np.float32)
    for c in range(NC):
        y[c::NC] = res.results[c]["y"]
    return y.reshape(1, S, D)
